# version 08
# baseline (speedup 1.0000x reference)
import numpy as np
import ml_dtypes
import concourse.bass as bass
import concourse.mybir as mybir
from concourse.bass_isa import ReduceOp
from concourse.bass_utils import run_bass_kernel_spmd

F32 = mybir.dt.float32
BF16 = mybir.dt.bfloat16
AF = mybir.ActivationFunctionType
ALU = mybir.AluOpType
AXX = mybir.AxisListType.X

NCORES = 8
D = 1024
TTOT = 16384
TC = TTOT // NCORES
SEQ = 4096
NCH = D // 128
MAGIC = 12582912.0
QMAX = 32767.0
EPS_LN = 1e-5

ENGS = ("sync", "tensor", "vector", "scalar", "gpsimd")


class Prog:

    def __init__(self, nc):
        self.nc = nc
        self.streams = {e: [] for e in ENGS}
        self.sems = {}

    def _sem(self, name):
        if name not in self.sems:
            self.sems[name] = [self.nc.alloc_semaphore(name), 0]
        return self.sems[name]

    def now(self, name):
        return self._sem(name)[1]

    def op(self, eng, build, wait=(), inc=()):
        w = []
        for n, v in wait:
            s = self._sem(n)
            assert v <= s[1], f"wait {n}>={v} but only {s[1]} emitted (deadlock)"
            if v > 0:
                w.append((s[0], v))
        ii = []
        out_counts = []
        for n, k in inc:
            s = self._sem(n)
            s[1] += k
            ii.append((s[0], k))
            out_counts.append(s[1])

        def fn(e, _w=w, _ii=ii, _b=build):
            for h, v in _w:
                e.wait_ge(h, v)
            ins = _b(e)
            for h, k in _ii:
                assert ins is not None, "build returned None but incs requested"
                ins = ins.then_inc(h, k)

        self.streams[eng].append(fn)
        return out_counts

    def wait(self, eng, *pairs):
        self.op(eng, lambda e: None, wait=pairs)

    def finalize(self):
        nc = self.nc
        with nc.Block() as block:
            for eng in ENGS:
                fns = self.streams[eng]
                if not fns:
                    continue

                def runner(e, _fns=fns):
                    for f in _fns:
                        f(e)

                getattr(block, eng)(runner)



def _t(w):
    return np.ascontiguousarray(np.asarray(w).T)


def host_prepare(inputs):
    d = {}
    x = np.asarray(inputs["L0_attn_x"])
    d["x_t"] = np.ascontiguousarray(x.reshape(TTOT, D).T)
    d["w_upT"] = _t(inputs["L1_ffn_w_up"])
    d["w_gateT"] = _t(inputs["L1_ffn_w_gate"])
    d["w_downT"] = _t(inputs["L1_ffn_w_down"])
    d["Wqvk"] = np.asarray(inputs["L1_attn_Wqvk"])
    d["Wo"] = np.asarray(inputs["L1_attn_Wo"])
    d["W_uT"] = _t(inputs["L0_attn_W_u"])
    d["W_gT"] = _t(inputs["L0_attn_W_g"])
    d["BmT"] = _t(inputs["L0_attn_Bm"])
    d["CmT"] = _t(inputs["L0_attn_Cm"])
    d["A_nat"] = np.ascontiguousarray(np.asarray(inputs["L0_attn_A"], dtype=np.float32))
    d["A_T"] = _t(inputs["L0_attn_A"])
    d["b_u"] = np.ascontiguousarray(np.asarray(inputs["L0_attn_b_u"]).reshape(NCH, 128).T)
    d["b_g"] = np.ascontiguousarray(np.asarray(inputs["L0_attn_b_g"]).reshape(NCH, 128).T)
    d["Dv"] = np.ascontiguousarray(np.asarray(inputs["L0_attn_D"]).reshape(NCH, 128).T)
    d["gate_wT"] = _t(inputs["L0_ffn_gate_w"])
    d["dyn_gT"] = np.ascontiguousarray(np.asarray(inputs["L0_ffn_dyn_gate_w"]).transpose(0, 2, 1))
    d["dyn_uT"] = np.ascontiguousarray(np.asarray(inputs["L0_ffn_dyn_up_w"]).transpose(0, 2, 1))
    d["dyn_dT"] = np.ascontiguousarray(np.asarray(inputs["L0_ffn_dyn_down_w"]).transpose(0, 2, 1))
    d["sh_gT"] = np.ascontiguousarray(np.asarray(inputs["L0_ffn_sh_gate_w"])[0].T)
    d["sh_uT"] = np.ascontiguousarray(np.asarray(inputs["L0_ffn_sh_up_w"])[0].T)
    d["sh_dT"] = np.ascontiguousarray(np.asarray(inputs["L0_ffn_sh_down_w"])[0].T)
    return d


def x_slice(d, core):
    return np.ascontiguousarray(d["x_t"][:, core * TC:(core + 1) * TC])


class Ctx:
    def __init__(self, nc):
        self.nc = nc
        self.p = Prog(nc)
        self.t = {}
        self.const = {}
        self.arena = None
        self._ns_mark = None

    def arena_reset(self):
        self.arena = (self.arena[0], 0)

    def begin_ns(self):
        self._ns_mark = dict(self.t)

    def end_ns(self):
        old = self._ns_mark
        self._ns_mark = None
        return {k: v for k, v in self.t.items() if old.get(k) is not v}

    def sbuf(self, name, shape, dtype):
        if self.arena is not None:
            ap, off = self.arena
            nbytes = shape[1] * (2 if dtype == BF16 else 4)
            off_al = (off + 63) // 64 * 64
            assert off_al + nbytes <= ap.shape[1] * 4, \
                f"arena overflow at {name}: need {nbytes} at {off_al}"
            base = ap[:, off_al // 4:(off_al + nbytes + 3) // 4]
            view = base.bitcast(BF16) if dtype == BF16 else base
            view = view[0:shape[0], 0:shape[1]]
            self.arena = (ap, off_al + nbytes)
            self.t[name] = view
            return view
        h = self.nc.alloc_sbuf_tensor(name, shape, dtype)
        self.t[name] = h
        return h

    def psum(self, name, shape, dtype=F32):
        h = self.nc.alloc_psum_tensor(name, shape, dtype)
        self.t[name] = h
        return h

    def dram(self, name, shape, dtype, **kw):
        h = self.nc.dram_tensor(name, shape, dtype, kind="Internal", **kw)
        self.t[name] = h
        return h

    def inline(self, name, arr):
        h = self.nc.inline_tensor(np.ascontiguousarray(arr), name=name)
        self.const[name] = h
        return h

    def mk_consts(self):
        self.inline("ones_col", np.ones((128, 1), dtype=ml_dtypes.bfloat16))
        self.inline("ones_row", np.ones((1, 128), dtype=ml_dtypes.bfloat16))
        self.inline("ident", np.eye(128, dtype=np.float32))


def load_consts(c, ident=True):
    p = c.p
    oc = c.sbuf("ones_col_sb", [128, 1], BF16)
    orow = c.sbuf("ones_row_sb", [1, 128], BF16)
    p.op("sync", lambda e, t=oc: e.dma_start(out=t[:, :], in_=c.const["ones_col"].ap()[:, :]),
         inc=[("ld", 16)])
    p.op("sync", lambda e, t=orow: e.dma_start(out=t[:, :], in_=c.const["ones_row"].ap()[:, :]),
         inc=[("ld", 16)])
    if ident:
        idn = c.sbuf("ident_sb", [128, 128], F32)
        p.op("sync", lambda e, t=idn: e.dma_start(out=t[:, :], in_=c.const["ident"].ap()[:, :]),
             inc=[("ld", 16)])



def emit_ln(c, xf, out_bf, Tt, absmax_to=None):
    p = c.p
    oc = c.t["ones_col_sb"]
    orow = c.t["ones_row_sb"]
    stat = c.t["stat"]
    row_bf = c.t["row_bf"]
    sq = c.t["sq_bf"]
    psA, psB = c.t["psA"], c.t["psB"]

    xt = c.t["xbf_scr"]
    for ch in range(NCH):
        cast_w = [("pe", p.now("pe")), ("ve", p.now("ve"))] if True else []
        v = p.op("scalar", (lambda e, a=xf(ch), t=xt: e.activation(t[:, 0:Tt], a, AF.Copy)),
                 wait=cast_w, inc=[("se", 1)])[0]
        v2 = p.op("scalar", (lambda e, t=xt, s=sq: e.activation(s[:, 0:Tt], t[:, 0:Tt], AF.Square)),
                  wait=[("se", v)], inc=[("se", 1)])[0]
        p.op("tensor", (lambda e, t=xt, ch=ch: e.matmul(
            psA[0:1, 0:Tt], oc[:, :], t[:, 0:Tt], start=(ch == 0), stop=(ch == NCH - 1))),
            wait=[("se", v)], inc=[])
        p.op("tensor", (lambda e, s=sq, ch=ch: e.matmul(
            psB[0:1, 0:Tt], oc[:, :], s[:, 0:Tt], start=(ch == 0), stop=(ch == NCH - 1))),
            wait=[("se", v2)], inc=[("pe", 1)])
    pe_done = p.now("pe")

    p.op("vector", lambda e: e.tensor_scalar(stat[0:1, 0:Tt], psA[0:1, 0:Tt], 1.0 / D, None, ALU.mult),
         wait=[("pe", pe_done)], inc=[("ve", 1)])
    p.op("vector", lambda e: e.tensor_scalar(stat[0:1, 512:512 + Tt], psB[0:1, 0:Tt], 1.0 / D, None, ALU.mult),
         inc=[("ve", 1)])
    p.op("vector", lambda e: e.tensor_mul(stat[0:1, 1024:1024 + Tt], stat[0:1, 0:Tt], stat[0:1, 0:Tt]),
         wait=[("ve", p.now("ve"))], inc=[("ve", 1)])
    p.op("vector", lambda e: e.tensor_sub(stat[0:1, 512:512 + Tt], stat[0:1, 512:512 + Tt], stat[0:1, 1024:1024 + Tt]),
         wait=[("ve", p.now("ve"))], inc=[("ve", 1)])
    p.op("vector", lambda e: e.tensor_scalar_add(stat[0:1, 512:512 + Tt], stat[0:1, 512:512 + Tt], EPS_LN),
         wait=[("ve", p.now("ve"))], inc=[("ve", 1)])
    sev = p.op("scalar", lambda e: e.activation(stat[0:1, 1024:1024 + Tt], stat[0:1, 512:512 + Tt], AF.Sqrt),
               wait=[("ve", p.now("ve"))], inc=[("se", 1)])[0]
    p.op("vector", lambda e: e.reciprocal(stat[0:1, 1024:1024 + Tt], stat[0:1, 1024:1024 + Tt]),
         wait=[("se", sev)], inc=[("ve", 1)])
    p.op("vector", lambda e: e.tensor_mul(stat[0:1, 1536:1536 + Tt], stat[0:1, 0:Tt], stat[0:1, 1024:1024 + Tt]),
         wait=[("ve", p.now("ve"))], inc=[("ve", 1)])
    p.op("vector", lambda e: e.tensor_copy(row_bf[0:1, 0:Tt], stat[0:1, 1024:1024 + Tt]),
         wait=[("ve", p.now("ve"))], inc=[("ve", 1)])
    p.op("vector", lambda e: e.tensor_copy(row_bf[0:1, 512:512 + Tt], stat[0:1, 1536:1536 + Tt]),
         wait=[("ve", p.now("ve"))], inc=[("ve", 1)])
    ve_rows = p.now("ve")

    p.op("tensor", lambda e: e.matmul(psA[:, 0:Tt], orow[:, :], row_bf[0:1, 0:Tt], start=True, stop=True),
         wait=[("ve", ve_rows)], inc=[("pe", 1)])
    bc = p.op("tensor", lambda e: e.matmul(psB[:, 0:Tt], orow[:, :], row_bf[0:1, 512:512 + Tt], start=True, stop=True),
              inc=[("pe", 1)])[0]

    scr = c.t["ln_scr"]
    red = c.t["red_scr"]
    for ch in range(NCH):
        p.op("vector", lambda e, a=xf(ch): e.tensor_mul(scr[:, 0:Tt], a, psA[:, 0:Tt]),
             wait=([("pe", bc)] if ch == 0 else [("ve", p.now("ve"))]), inc=[("ve", 1)])
        p.op("vector", lambda e, o=out_bf(ch): e.tensor_sub(o, scr[:, 0:Tt], psB[:, 0:Tt]),
             wait=[("ve", p.now("ve"))], inc=[("ve", 1)])
        if absmax_to is not None:
            p.op("vector", lambda e, o=out_bf(ch): e.reduce_max(
                red[:, 0:1], o, axis=AXX, apply_absolute_value=True),
                wait=[("ve", p.now("ve"))], inc=[("ve", 1)])
            p.op("vector", lambda e: e.tensor_max(absmax_to[:, 0:1], absmax_to[:, 0:1], red[:, 0:1]),
                 wait=[("ve", p.now("ve"))], inc=[("ve", 1)])


def emit_absmax_to_scalar(c, am128, slot):
    p = c.p
    psB = c.t["psB"]
    v = p.op("tensor", lambda e: e.transpose(psB[0:1, 0:128], am128[:, 0:1], c.t["ident_sb"][:, :]),
             wait=[("ve", p.now("ve")), ("se", p.now("se"))], inc=[("pe", 1)])[0]
    p.op("vector", lambda e: e.reduce_max(
        c.t["srow"][0:1, slot:slot + 1], psB[0:1, 0:128], axis=AXX),
        wait=[("pe", v)], inc=[("ve", 1)])


def gather_scales(c, nvals):
    p = c.p
    cin = c.t["cc_sc_in"]
    cout = c.t["cc_sc_out"]
    grow = c.t["grow"]
    p.op("gpsimd", lambda e: e.dma_start(
        out=cin.ap()[0:nvals, :].rearrange("n one -> one n"), in_=c.t["srow"][0:1, 0:nvals]),
        wait=[("po", p.now("po")), ("ve", p.now("ve"))], inc=[("cd", 16)])
    p.op("gpsimd", lambda e: e.collective_compute(
        "AllGather", ALU.bypass, replica_groups=[list(range(NCORES))],
        ins=[cin.ap()[0:nvals, :].opt()], outs=[cout.ap()[0:NCORES * nvals, :].opt()]),
        wait=[("cc", p.now("cc")), ("cd", p.now("cd"))], inc=[("cc", 1)])
    p.op("gpsimd", lambda e: e.dma_start(
        out=grow[0:1, 0:NCORES * nvals],
        in_=cout.ap()[0:NCORES * nvals, :].rearrange("n one -> one n")),
        wait=[("cc", p.now("cc")), ("cd", p.now("cd"))], inc=[("cd", 16)])
    ccv = p.now("cc")
    cdv = p.now("cd")
    for s in range(nvals):
        p.op("vector", lambda e, s=s: e.reduce_max(
            c.t["srow"][0:1, s:s + 1],
            grow[0:1, 0:NCORES * nvals].rearrange("one (r n) -> one n r", n=nvals)[:, s],
            axis=AXX),
            wait=([("cc", ccv), ("cd", cdv), ("ve", p.now("ve"))] if s == 0 else [("ve", p.now("ve"))]),
            inc=[("ve", 1)])


def finish_scale(c, slot, dst_col, extra_mult=None):
    p = c.p
    srow = c.t["srow"]
    s128 = c.t["s128"]
    p.op("vector", lambda e: e.tensor_scalar(
        srow[0:1, 32:33], srow[0:1, slot:slot + 1], 1.0 / QMAX, 1e-12, ALU.mult, ALU.max),
        wait=[("ve", p.now("ve")), ("po", p.now("po"))], inc=[("ve", 1)])
    p.op("vector", lambda e: e.reciprocal(srow[0:1, 33:34], srow[0:1, 32:33]),
         wait=[("ve", p.now("ve"))], inc=[("ve", 1)])
    if extra_mult is not None:
        p.op("vector", lambda e: e.tensor_scalar(
            srow[0:1, 32:33], srow[0:1, 32:33], extra_mult, None, ALU.mult),
            wait=[("ve", p.now("ve"))], inc=[("ve", 1)])
    p.op("gpsimd", lambda e: e.dma_start(out=c.t["bc_d"].ap()[:, :], in_=srow[0:1, 32:34]),
         wait=[("ve", p.now("ve"))], inc=[("po", 16)])
    p.op("gpsimd", lambda e: e.dma_start(
        out=s128[:, dst_col:dst_col + 2],
        in_=c.t["bc_d"].ap().partition_broadcast(128)[:, 0, :]),
        wait=[("po", p.now("po"))], inc=[("po", 16)])



IFF = 3072
IGRP = 512
NIG = IFF // IGRP
TT1 = 512
NTT = TC // TT1


def qdq_pair(c, out_ap, in_ap, inv_col, s_col, wait0=()):
    p = c.p
    s128 = c.t["s128"]
    q = c.t["q_scr"]
    Tt = 512
    p.op("vector", lambda e: e.tensor_scalar(
        q[:, 0:Tt], in_ap, s128[:, inv_col:inv_col + 1], MAGIC, ALU.mult, ALU.add),
        wait=list(wait0) + [("ve", p.now("ve"))], inc=[("ve", 1)])
    p.op("vector", lambda e: e.tensor_scalar(
        out_ap, q[:, 0:Tt], MAGIC, s128[:, s_col:s_col + 1], ALU.subtract, ALU.mult),
        wait=[("ve", p.now("ve"))], inc=[("ve", 1)])


def emit_f1_preconv(c):
    p = c.p
    for src, dst in (("w_upT_d", "up_bf"), ("w_gateT_d", "gt_bf"),
                     ("w_downT_d", "wd_bf")):
        sap = c.t[src].ap()
        dap = c.t[dst].ap()
        n = sap.shape[0]
        for h in range(2):
            p.op("gpsimd", (lambda e, s=sap, d=dap, h=h, n=n: e.dma_start(
                out=d[h * (n // 2):(h + 1) * (n // 2), :],
                in_=s[h * (n // 2):(h + 1) * (n // 2), :])), inc=[("cv", 16)])


def emit_f1(c, inv):
    p = c.p
    x_sb = c.t["x_sb"]
    xq = c.t["xq"]
    s128 = c.t["s128"]
    am, am2 = c.t["am128"], c.t["am2"]
    psD = [c.t[f"psO{i + 2}"] for i in range(4)]
    redall = c.t["redall"]
    mgc = c.t["mgc"]

    def xf(ch, j):
        return x_sb[:, ch * TC + j * TT1: ch * TC + j * TT1 + TT1]

    def xqa(ch, j):
        return xq[:, ch * TC + j * TT1: ch * TC + j * TT1 + TT1]

    p.op("vector", lambda e: e.memset(am[:, 0:1], 0.0), wait=[("po", p.now("po"))], inc=[("ve", 1)])
    p.op("vector", lambda e: e.memset(mgc[:, 0:1], MAGIC), wait=[("ve", p.now("ve"))], inc=[("ve", 1)])
    p.op("vector", lambda e: e.memset(mgc[:, 1:2], -MAGIC), wait=[("ve", p.now("ve"))], inc=[("ve", 1)])
    for j in range(NTT):
        emit_ln(c, (lambda ch, j=j: xf(ch, j)),
                (lambda ch, j=j: xqa(ch, j)), TT1, absmax_to=am)
    emit_absmax_to_scalar(c, am, 0)
    gather_scales(c, 1)
    finish_scale(c, 0, 0)
    for j in range(NTT):
        for ch in range(NCH):
            qdq_pair(c, xqa(ch, j), xqa(ch, j), 1, 0,
                     wait0=[("po", p.now("po"))])
    ve_x = p.now("ve")

    wab = [(c.t["wa0"], c.t["wb0"]), (c.t["wa1"], c.t["wb1"])]
    ev4p = [c.t["ev4a"], c.t["ev4b"]]
    upv = c.t["up_bf"].ap().rearrange("(kc p) m -> p kc m", p=128)
    gtv = c.t["gt_bf"].ap().rearrange("(kc p) m -> p kc m", p=128)
    cv_tot = p.now("cv")
    fw = ("fwa", "fwb")
    sp_sem = ("spa", "spb")

    def ld_grp(g, extra=()):
        par = g % 2
        wa_, wb_ = wab[par]
        for half, (wsb, wv) in enumerate(((wa_, upv), (wb_, gtv))):
            p.op("sync", (lambda e, wsb=wsb, wv=wv, g=g: e.dma_start(
                out=wsb.rearrange("p (kc m) -> p kc m", m=IGRP),
                in_=wv[:, :, g * IGRP:(g + 1) * IGRP])),
                wait=(list(extra) + [("cv", cv_tot)]) if half == 0 else [],
                inc=[(fw[par], 16)])

    ld_grp(0)
    ld_grp(1)
    bank_se = [0] * 4
    wave_ve = {}
    widx = 0
    for g in range(NIG):
        par = g % 2
        wa_, wb_ = wab[par]
        fw_tot = p.now(fw[par])
        first_mm_of_grp = True
        for j in range(NTT):
            for half, (w, amcol0) in enumerate(((wa_, 0), (wb_, 96))):
                ev4 = ev4p[widx % 2]
                pe_bank = []
                for mc in range(4):
                    for kc in range(NCH):
                        wl = []
                        if kc == 0:
                            wl.append(("se", bank_se[mc]))
                            if first_mm_of_grp:
                                wl += [(fw[par], fw_tot), ("ve", ve_x)]
                                first_mm_of_grp = False
                        r = p.op("tensor", (lambda e, w=w, kc=kc, mc=mc, j=j: e.matmul(
                            psD[mc][:, 0:TT1],
                            w[:, kc * IGRP + mc * 128: kc * IGRP + mc * 128 + 128],
                            xqa(kc, j), start=(kc == 0), stop=(kc == NCH - 1))),
                            wait=wl, inc=([("pe", 1)] if kc == NCH - 1 else []))
                        if kc == NCH - 1:
                            pe_bank.append(r[0])
                for mc in range(4):
                    wl = [("pe", pe_bank[mc]), ("se", p.now("se"))]
                    if mc == 0 and widx >= 2:
                        ssm = sp_sem[widx % 2]
                        wl += [(ssm, p.now(ssm)), ("ve", wave_ve[widx - 2])]
                    p.op("scalar", (lambda e, mc=mc, ev4=ev4: e.activation(
                        ev4[:, mc * TT1:(mc + 1) * TT1], psD[mc][:, 0:TT1], AF.Copy)),
                        wait=wl, inc=[("se", 1)])
                    bank_se[mc] = p.now("se")
                for mc in range(4):
                    cc_ = amcol0 + (g * NTT + j) * 4 + mc
                    p.op("vector", (lambda e, mc=mc, cc_=cc_, ev4=ev4: e.reduce_max(
                        redall[:, cc_:cc_ + 1], ev4[:, mc * TT1:(mc + 1) * TT1],
                        axis=AXX, apply_absolute_value=True)),
                        wait=[("se", bank_se[mc]), ("ve", p.now("ve"))], inc=[("ve", 1)])
                wave_ve[widx] = p.now("ve")
                spname = "sp_up" if half == 0 else "sp_gate"
                sp_view = c.t[spname].ap().rearrange("(g mc p) t -> g p mc t", g=NIG, mc=4)[g]
                p.op("scalar", (lambda e, sv=sp_view, j=j, ev4=ev4: e.dma_start(
                    out=sv[:, :, j * TT1:(j + 1) * TT1],
                    in_=ev4.rearrange("p (mc t) -> p mc t", mc=4))),
                    wait=[("se", p.now("se"))], inc=[(sp_sem[widx % 2], 16)])
                widx += 1
        if g + 2 < NIG:
            ld_grp(g + 2, extra=[("pe", p.now("pe"))])
    p.op("vector", lambda e: e.reduce_max(am[:, 0:1], redall[:, 0:96], axis=AXX),
         wait=[("ve", p.now("ve"))], inc=[("ve", 1)])
    p.op("vector", lambda e: e.reduce_max(am2[:, 0:1], redall[:, 96:192], axis=AXX),
         wait=[("ve", p.now("ve"))], inc=[("ve", 1)])
    emit_absmax_to_scalar(c, am, 0)
    emit_absmax_to_scalar(c, am2, 1)
    gather_scales(c, 2)
    finish_scale(c, 0, 0)
    finish_scale(c, 1, 2)

    fx = ("fx0", "fx1", "fx2", "fx3")
    sl_ld = [c.t[f"ld{s}"] for s in range(4)]
    sl_ul = [c.t[f"ul{s}"] for s in range(4)]
    sl_qb = [c.t[f"qb{s}"] for s in range(4)]
    sl_sg = [c.t[f"sg{s}"] for s in range(4)]
    sl_tb = [c.t[f"tb{s}"] for s in range(4)]
    NTI = NIG * 4 * NTT
    LA = 3
    gate_rows = c.t["sp_gate"].ap().rearrange("(r p) t -> r p t", p=128)
    up_rows = c.t["sp_up"].ap().rearrange("(r p) t -> r p t", p=128)
    pe_s1 = p.now("pe")
    spa_t, spb_t = p.now("spa"), p.now("spb")
    po_t = p.now("po")
    ld_free = {}

    def s2_load(i, first):
        r, j = divmod(i, NTT)
        s = i % 4
        wl = ([("pe", pe_s1), ("spa", spa_t), ("spb", spb_t)] if first
              else [("ve", ld_free[i - 4])])
        p.op("sync", (lambda e, r=r, j=j, s=s: e.dma_start(
            out=sl_ld[s][:, :], in_=gate_rows[r][:, j * TT1:(j + 1) * TT1])),
            wait=wl, inc=[(fx[s], 16)])

    for i in range(min(LA, NTI)):
        s2_load(i, True)
    for i in range(NTI):
        r, j = divmod(i, NTT)
        s = i % 4
        qb, sg, tb, ldb = sl_qb[s], sl_sg[s], sl_tb[s], sl_ld[s]
        w1 = [("ve", p.now("ve")), (fx[s], p.now(fx[s]))]
        if i == 0:
            w1.append(("po", po_t))
        p.op("vector", (lambda e, qb=qb, ldb=ldb: e.tensor_scalar(
            qb[:, :], ldb[:, :], s128[:, 3:4], MAGIC, ALU.mult, ALU.add)),
            wait=w1, inc=[("ve", 1)])
        ld_free[i] = p.now("ve")
        p.op("vector", (lambda e, qb=qb: e.tensor_scalar(
            qb[:, :], qb[:, :], MAGIC, s128[:, 2:3], ALU.subtract, ALU.mult)),
            wait=[("ve", p.now("ve"))], inc=[("ve", 1)])
        p.op("scalar", (lambda e, sg=sg, qb=qb: e.activation(sg[:, :], qb[:, :], AF.Sigmoid)),
             wait=[("se", p.now("se")), ("ve", p.now("ve"))], inc=[("se", 1)])
        p.op("scalar", (lambda e, sg=sg: e.activation(
            sg[:, :], sg[:, :], AF.Relu, bias=mgc[:, 0:1], scale=65536.0)),
            wait=[("se", p.now("se"))], inc=[("se", 1)])
        p.op("vector", (lambda e, sg=sg: e.tensor_scalar(
            sg[:, :], sg[:, :], MAGIC, 65535.0, ALU.subtract, ALU.min)),
            wait=[("ve", p.now("ve")), ("se", p.now("se"))], inc=[("ve", 1)])
        p.op("vector", (lambda e, qb=qb, sg=sg: e.tensor_mul(qb[:, :], qb[:, :], sg[:, :])),
             wait=[("ve", p.now("ve"))], inc=[("ve", 1)])
        p.op("vector", (lambda e, tb=tb, qb=qb: e.tensor_scalar(
            tb[:, :], qb[:, :], 1.0 / 65536.0, None, ALU.mult)),
            wait=[("ve", p.now("ve"))], inc=[("ve", 1)])
        p.op("vector", (lambda e, tb=tb, i=i: e.reduce_max(
            redall[:, i:i + 1], tb[:, :], axis=AXX, apply_absolute_value=True)),
            wait=[("ve", p.now("ve"))], inc=[("ve", 1)])
        p.op("sync", (lambda e, r=r, j=j, tb=tb: e.dma_start(
            out=gate_rows[r][:, j * TT1:(j + 1) * TT1], in_=tb[:, :])),
            wait=[("ve", p.now("ve"))], inc=[(fx[s], 16)])
        if i + LA < NTI:
            s2_load(i + LA, i + LA < 4)
    ve_s2 = p.now("ve")
    p.op("vector", lambda e: e.reduce_max(am[:, 0:1], redall[:, 0:96], axis=AXX),
         wait=[("ve", p.now("ve"))], inc=[("ve", 1)])
    emit_absmax_to_scalar(c, am, 0)
    gather_scales(c, 1)
    finish_scale(c, 0, 4)

    po_t = p.now("po")
    ldT_free = {}
    sg_free = {}

    def s3_load(i, first):
        r, j = divmod(i, NTT)
        s = i % 4
        wl = ([("ve", ve_s2)] if first
              else [("ve", ldT_free[i - 4]), ("se", sg_free[i - 4][1])])
        p.op("sync", (lambda e, r=r, j=j, s=s: e.dma_start(
            out=sl_ld[s][:, :], in_=gate_rows[r][:, j * TT1:(j + 1) * TT1])),
            wait=wl, inc=[(fx[s], 16)])
        p.op("sync", (lambda e, r=r, j=j, s=s: e.dma_start(
            out=sl_ul[s][:, :], in_=up_rows[r][:, j * TT1:(j + 1) * TT1])),
            inc=[(fx[s], 16)])

    for i in range(min(LA, NTI)):
        s3_load(i, True)
    for i in range(NTI):
        r, j = divmod(i, NTT)
        s = i % 4
        qb, sg, tb, ldb, ulb = sl_qb[s], sl_sg[s], sl_tb[s], sl_ld[s], sl_ul[s]
        fx_tot = p.now(fx[s])
        w1 = [("ve", p.now("ve")), (fx[s], fx_tot)]
        if i == 0:
            w1.append(("po", po_t))
        p.op("vector", (lambda e, qb=qb, ldb=ldb: e.tensor_scalar(
            qb[:, :], ldb[:, :], s128[:, 5:6], MAGIC, ALU.mult, ALU.add)),
            wait=w1, inc=[("ve", 1)])
        ldT_free[i] = p.now("ve")
        p.op("vector", (lambda e, qb=qb: e.tensor_scalar(
            qb[:, :], qb[:, :], MAGIC, s128[:, 4:5], ALU.subtract, ALU.mult)),
            wait=[("ve", p.now("ve"))], inc=[("ve", 1)])
        wsg = [("se", p.now("se")), (fx[s], fx_tot)]
        if i >= 4:
            wsg.append(("ve", sg_free[i - 4][0]))
        else:
            wsg.append(("ve", ve_s2))
        p.op("scalar", (lambda e, sg=sg, ulb=ulb: e.activation(
            sg[:, :], ulb[:, :], AF.Relu, bias=mgc[:, 0:1], scale=s128[:, 1:2])),
            wait=wsg, inc=[("se", 1)])
        se_u = p.now("se")
        p.op("vector", (lambda e, sg=sg: e.tensor_scalar(
            sg[:, :], sg[:, :], MAGIC, s128[:, 0:1], ALU.subtract, ALU.mult)),
            wait=[("ve", p.now("ve")), ("se", se_u)], inc=[("ve", 1)])
        p.op("vector", (lambda e, tb=tb, sg=sg, qb=qb: e.tensor_mul(
            tb[:, :], sg[:, :], qb[:, :])),
            wait=[("ve", p.now("ve"))], inc=[("ve", 1)])
        sg_free[i] = (p.now("ve"), se_u)
        p.op("vector", (lambda e, tb=tb, i=i: e.reduce_max(
            redall[:, i:i + 1], tb[:, :], axis=AXX, apply_absolute_value=True)),
            wait=[("ve", p.now("ve"))], inc=[("ve", 1)])
        p.op("sync", (lambda e, r=r, j=j, tb=tb: e.dma_start(
            out=up_rows[r][:, j * TT1:(j + 1) * TT1], in_=tb[:, :])),
            wait=[("ve", p.now("ve"))], inc=[(fx[s], 16)])
        if i + LA < NTI:
            s3_load(i + LA, i + LA < 4)
    p.op("vector", lambda e: e.reduce_max(am[:, 0:1], redall[:, 0:96], axis=AXX),
         wait=[("ve", p.now("ve"))], inc=[("ve", 1)])
    emit_absmax_to_scalar(c, am, 0)
    gather_scales(c, 1)
    finish_scale(c, 0, 6, extra_mult=0.5)
    fx_s3 = [p.now(fx[s]) for s in range(4)]
    po_t4 = p.now("po")

    wdp = [c.t["wd0"], c.t["wd1"]]
    oqp = [c.t["oq0"], c.t["oq1"]]
    wdv = c.t["wd_bf"].ap().rearrange("(g kc p) m -> g p kc m", p=128, g=NIG)
    ov = c.t["sp_up"].ap().rearrange("(g kc p) t -> g p kc t", g=NIG, kc=4)
    NU = NTT * NIG
    pe_unit = {}

    def wd_load(u, extra=()):
        g = u % NIG
        par = u % 2
        p.op("sync", (lambda e, g=g, par=par: e.dma_start(
            out=wdp[par].rearrange("p (kc m) -> p kc m", kc=4), in_=wdv[g])),
            wait=list(extra) + ([("cv", cv_tot)] if u < 2 else []),
            inc=[(fw[par], 16)])

    fo = ("foa", "fob")

    def o_load(u, extra=()):
        j, g = divmod(u, NIG)
        par = u % 2
        p.op("sync", (lambda e, g=g, j=j, par=par: e.dma_start(
            out=oqp[par].rearrange("p (kc t) -> p kc t", kc=4),
            in_=ov[g][:, :, j * TT1:(j + 1) * TT1])),
            wait=list(extra) + [(fx[j], fx_s3[j])], inc=[(fo[par], 16)])

    wd_load(0)
    wd_load(1)
    o_load(0)
    o_load(1)
    for u in range(NU):
        j, g = divmod(u, NIG)
        par = u % 2
        oq = oqp[par]
        wd = wdp[par]
        for kc in range(4):
            w1 = [(fo[par], p.now(fo[par]))] if kc == 0 else []
            if u == 0 and kc == 0:
                w1.append(("po", po_t4))
            qdq_pair(c, oq[:, kc * TT1:(kc + 1) * TT1], oq[:, kc * TT1:(kc + 1) * TT1],
                     7, 6, wait0=w1)
        ve_q = p.now("ve")
        first = True
        for mc in range(NCH):
            for kc in range(4):
                wl = []
                if first:
                    wl = [("ve", ve_q), (fw[par], p.now(fw[par]))]
                    first = False
                p.op("tensor", (lambda e, mc=mc, kc=kc, g=g, oq=oq, wd=wd: e.matmul(
                    c.t[f"psO{mc}"][:, 0:TT1],
                    wd[:, kc * D + mc * 128: kc * D + mc * 128 + 128],
                    oq[:, kc * TT1:(kc + 1) * TT1],
                    start=(g == 0 and kc == 0), stop=(g == NIG - 1 and kc == 3))),
                    wait=wl, inc=([("pe", 1)] if (mc == NCH - 1 and kc == 3) else []))
        pe_unit[u] = p.now("pe")
        if u + 2 < NU:
            wd_load(u + 2, extra=[("pe", pe_unit[u])])
            o_load(u + 2, extra=[("pe", pe_unit[u])])
        if g == NIG - 1:
            for mc in range(NCH):
                p.op("vector", (lambda e, mc=mc, j=j: e.tensor_add(
                    xf(mc, j), xf(mc, j), c.t[f"psO{mc}"][:, 0:TT1])),
                    wait=[("pe", pe_unit[u]), ("ve", p.now("ve"))], inc=[("ve", 1)])


def f1_alloc(c):
    u = c.sbuf("f1u", [128, 10240], F32)

    def carve(name, off, cols, dtype):
        v = u[:, off:off + cols]
        c.t[name] = v.bitcast(BF16)[0:128, 0:cols * 2] if dtype == BF16 else v

    carve("wa0", 0, 2048, BF16)
    carve("wa1", 2048, 2048, BF16)
    carve("wb0", 4096, 2048, BF16)
    carve("wb1", 6144, 2048, BF16)
    carve("ev4a", 8192, 1024, BF16)
    carve("ev4b", 9216, 1024, BF16)
    for s in range(4):
        b = s * 1792
        carve(f"ld{s}", b, 256, BF16)
        carve(f"ul{s}", b + 256, 256, BF16)
        carve(f"tb{s}", b + 512, 256, BF16)
        carve(f"qb{s}", b + 768, 512, F32)
        carve(f"sg{s}", b + 1280, 512, F32)
    c.sbuf("wd0", [128, 4 * D], BF16)
    c.sbuf("wd1", [128, 4 * D], BF16)
    c.sbuf("oq0", [128, 4 * 512], BF16)
    c.sbuf("oq1", [128, 4 * 512], BF16)
    c.sbuf("redall", [128, 192], F32)
    c.sbuf("mgc", [128, 2], F32)
    c.dram("sp_up", [IFF, TC], BF16)
    c.dram("sp_gate", [IFF, TC], BF16)
    c.dram("up_bf", [D, IFF], BF16)
    c.dram("gt_bf", [D, IFF], BF16)
    c.dram("wd_bf", [IFF, D], BF16)
    c.dram("cc_sc_in", [8, 1], F32)
    c.dram("cc_sc_out", [64, 1], F32)
    c.dram("bc_d", [1, 2], F32)


def build_f1_standalone(debug=False):
    nc = bass.Bass("TRN2", target_bir_lowering=False)
    c = Ctx(nc)
    x_in = nc.dram_tensor("x_in", [D, TC], F32, kind="ExternalInput")
    c.t["w_upT_d"] = nc.dram_tensor("w_upT", [D, IFF], F32, kind="ExternalInput")
    c.t["w_gateT_d"] = nc.dram_tensor("w_gateT", [D, IFF], F32, kind="ExternalInput")
    c.t["w_downT_d"] = nc.dram_tensor("w_downT", [IFF, D], F32, kind="ExternalInput")
    x_out = nc.dram_tensor("x_out", [D, TC], F32, kind="ExternalOutput")
    if debug:
        dbg_xq = nc.dram_tensor("dbg_xq", [D, TC], BF16, kind="ExternalOutput")
        dbg_s = nc.dram_tensor("dbg_s", [128, 8], F32, kind="ExternalOutput")
        dbg_o = nc.dram_tensor("dbg_o", [IFF, TC], BF16, kind="ExternalOutput")
        dbg_g2 = nc.dram_tensor("dbg_g2", [IFF, TC], BF16, kind="ExternalOutput")
    c.mk_consts()
    c.sbuf("x_sb", [128, NCH * TC], F32)
    c.sbuf("xq", [128, NCH * TC], BF16)
    c.sbuf("stat", [1, 2048], F32)
    c.sbuf("row_bf", [1, 1024], BF16)
    c.sbuf("sq_bf", [128, 512], BF16)
    c.sbuf("xbf_scr", [128, 512], BF16)
    c.sbuf("ln_scr", [128, 512], F32)
    c.sbuf("red_scr", [128, 1], F32)
    c.sbuf("am128", [128, 1], F32)
    c.sbuf("am2", [128, 1], F32)
    c.sbuf("srow", [1, 64], F32)
    c.sbuf("grow", [1, 64], F32)
    c.sbuf("s128", [128, 8], F32)
    c.sbuf("q_scr", [128, 512], F32)
    c.sbuf("ev_scr", [128, 512], F32)
    for i in range(NCH):
        c.psum(f"psO{i}", [128, 512], F32)
    c.t["psA"] = c.t["psO0"]
    c.t["psB"] = c.t["psO1"]
    f1_alloc(c)
    p = c.p
    load_consts(c)
    emit_f1_preconv(c)
    xv = x_in.ap().rearrange("(c p) t -> p c t", p=128)
    p.op("sync", lambda e: e.dma_start(
        out=c.t["x_sb"].rearrange("p (c t) -> p c t", c=NCH), in_=xv), inc=[("ld", 16)])
    for eng in ("vector", "tensor", "scalar"):
        p.wait(eng, ("ld", p.now("ld")))
    emit_f1(c, 0)
    p.op("sync", lambda e: e.dma_start(
        out=x_out.ap().rearrange("(c p) t -> p c t", p=128),
        in_=c.t["x_sb"].rearrange("p (c t) -> p c t", c=NCH)),
        wait=[("ve", p.now("ve")), ("se", p.now("se"))], inc=[("st", 16)])
    if debug:
        p.op("sync", lambda e: e.dma_start(
            out=dbg_xq.ap().rearrange("(c p) t -> p c t", p=128),
            in_=c.t["xq"].rearrange("p (c t) -> p c t", c=NCH)),
            wait=[("ve", p.now("ve"))], inc=[("st", 16)])
        p.op("sync", lambda e: e.dma_start(out=dbg_s.ap()[:, :], in_=c.t["s128"][:, :]),
             wait=[("po", p.now("po"))], inc=[("st", 16)])
        p.op("sync", lambda e: e.dma_start(out=dbg_o.ap()[:, :], in_=c.t["sp_up"].ap()[:, :]),
             wait=[("sp", p.now("sp"))], inc=[("st", 16)])
        p.op("sync", lambda e: e.dma_start(out=dbg_g2.ap()[:, :], in_=c.t["sp_gate"].ap()[:, :]),
             inc=[("st", 16)])
    p.wait("sync", ("st", p.now("st")))
    p.finalize()
    return nc


def run_f1(d, debug=False):
    nc = build_f1_standalone(debug)
    in_maps = []
    for core in range(NCORES):
        in_maps.append({
            "x_in": x_slice(d, core),
            "w_upT": d["w_upT"], "w_gateT": d["w_gateT"], "w_downT": d["w_downT"],
        })
    res = run_bass_kernel_spmd(nc, in_maps, core_ids=list(range(NCORES)))
    if debug:
        return res
    return np.concatenate([res.results[cc]["x_out"] for cc in range(NCORES)], axis=1)





NEXP = 8
IDYN = 512
ISH = 2048
CAP = 2560.0
NG = TC // 128
TT0 = 512
NTT0 = TC // TT0


def emit_f0(c, inv):
    p = c.p
    x_sb = c.t["x_sb"]
    xq = c.t["xq"]
    oc = c.t["ones_col_sb"]
    orow = c.t["ones_row_sb"]
    idn = c.t["ident_sb"]
    psA, psB = c.t["psA"], c.t["psB"]
    lgt = c.t["lgt"]
    gsc = c.t["gsc"]
    rw = c.t["rw"]
    glt = c.t["glt"]
    mcap = c.t["mcap_fm"]
    capm = c.t["capm_fm"]
    ones8 = c.t["ones8"]

    def xf(ch, j, Tt=TT0):
        return x_sb[:, ch * TC + j * Tt: ch * TC + j * Tt + Tt]

    def xqa(ch, j, Tt=TT0):
        return xq[:, ch * TC + j * Tt: ch * TC + j * Tt + Tt]

    for j in range(4):
        emit_ln(c, (lambda ch, j=j: x_sb[:, ch * TC + j * 512: ch * TC + j * 512 + 512]),
                (lambda ch, j=j: xq[:, ch * TC + j * 512: ch * TC + j * 512 + 512]), 512)

    w9 = c.t["w9"]
    p.op("gpsimd", lambda e: e.dma_start(
        out=w9.rearrange("p (kc m) -> p kc m", m=9),
        in_=c.t["gate_wT_d"].ap().rearrange("(kc p) m -> p kc m", p=128)),
        wait=[("pe", p.now("pe"))], inc=[("wl", 16)])
    wl = p.now("wl")
    lg_fm = c.t["lg_fm"]
    for j in range(4):
        for kc in range(NCH):
            p.op("tensor", (lambda e, kc=kc, j=j: e.matmul(
                psA[0:9, 0:512], w9[:, kc * 9:(kc + 1) * 9], xqa(kc, j, 512),
                start=(kc == 0), stop=(kc == NCH - 1))),
                wait=([("wl", wl), ("ve", p.now("ve")), ("se", p.now("se"))] if kc == 0 else []),
                inc=([("pe", 1)] if kc == NCH - 1 else []))
        v = p.now("pe")
        p.op("vector", (lambda e, j=j: e.tensor_copy(lg_fm[0:9, j * 512:(j + 1) * 512], psA[0:9, 0:512])),
             wait=[("pe", v)], inc=[("ve", 1)])
    if "dbg_lg_d" in c.t:
        p.op("sync", lambda e: e.dma_start(out=c.t["dbg_lg_d"].ap()[:, :], in_=lg_fm[0:9, :]),
             wait=[("ve", p.now("ve"))], inc=[("st", 16)])
    for g in range(NG):
        v = p.op("tensor", (lambda e, g=g: e.transpose(
            psB[:, 0:9], lg_fm[0:9, g * 128:(g + 1) * 128], idn[0:9, 0:9])),
            wait=[("ve", p.now("ve")), ("se", p.now("se"))], inc=[("pe", 1)])[0]
        p.op("vector", (lambda e, g=g: e.tensor_copy(lgt[:, g * 9:(g + 1) * 9], psB[:, 0:9])),
             wait=[("pe", v)], inc=[("ve", 1)])

    rw_all = c.t["rw_all"]
    for g in range(NG):
        dl = lgt[:, g * 9: g * 9 + 8]
        fl = lgt[:, g * 9: g * 9 + 9]
        srt = gsc[:, 0:8]
        exps = gsc[:, 8:16]
        msk = gsc[:, 16:24]
        msk2 = gsc[:, 24:32]
        onehot = gsc[:, 32:40]
        mr8 = gsc[:, 48:56]
        esum = gsc[:, 56:57]
        csum = gsc[:, 57:58]
        thr075 = gsc[:, 58:59]
        valid = gsc[:, 60:64]
        num = gsc[:, 64:65]
        gsum = gsc[:, 65:66]
        wk = gsc[:, 66:67]
        eff = gsc[:, 72:80]
        V = lambda build, w=(): p.op("vector", build, wait=list(w) + [("ve", p.now("ve"))], inc=[("ve", 1)])
        S = lambda build: p.op("scalar", build, wait=[("ve", p.now("ve")), ("se", p.now("se"))], inc=[("se", 1)])

        V(lambda e, dl=dl: e.max(srt, dl))
        sv = S(lambda e: e.activation(exps, srt, AF.Exp, accum_out=esum))[0]
        V(lambda e: e.memset(valid[:, 0:1], 1.0), w=[("se", sv)])
        V(lambda e: e.tensor_scalar(thr075, esum, 0.75, None, ALU.mult))
        V(lambda e: e.tensor_copy(csum, exps[:, 0:1]))
        for k in range(1, 4):
            V(lambda e, k=k: e.tensor_tensor(valid[:, k:k + 1], csum, thr075, ALU.is_lt))
            if k < 3:
                V(lambda e, k=k: e.tensor_add(csum, csum, exps[:, k:k + 1]))
        if g == 0 and "dbg_gate_d" in c.t:
            V(lambda e: e.tensor_copy(c.t["dbg_gate_sb"][:, 0:8], srt))
            V(lambda e: e.tensor_copy(c.t["dbg_gate_sb"][:, 8:12], valid))
            V(lambda e: e.tensor_copy(c.t["dbg_gate_sb"][:, 12:13], esum))
            V(lambda e: e.tensor_copy(c.t["dbg_gate_sb"][:, 13:14], thr075))
            V(lambda e, dl=dl: e.tensor_copy(c.t["dbg_gate_sb"][:, 14:22], dl))
            V(lambda e: e.tensor_copy(c.t["dbg_gate_sb"][:, 22:30], exps))
        V(lambda e, dl=dl: e.tensor_copy(msk, dl))
        V(lambda e: e.memset(mr8, -1e38))
        V(lambda e: e.memset(rw[:, :], 0.0))
        adl = gsc[:, 80:88]
        asrt = gsc[:, 88:96]
        S(lambda e, dl=dl: e.activation(adl, dl, AF.Abs))
        sv = S(lambda e: e.activation(asrt, srt, AF.Abs))[0]
        p.wait("vector", ("se", sv))
        for k in range(4):
            V(lambda e, k=k: e.tensor_copy(mr8[:, 0:1], srt[:, k:k + 1]))
            V(lambda e: e.match_replace(msk2, mr8, msk, -1e30))
            V(lambda e: e.tensor_tensor(onehot, msk, msk2, ALU.not_equal))
            V(lambda e, k=k, dl=dl: e.tensor_scalar(gsc[:, 40:48], dl, srt[:, k:k + 1], None, ALU.subtract))
            V(lambda e, k=k: e.tensor_scalar(eff, adl, asrt[:, k:k + 1], None, ALU.max))
            V(lambda e: e.scalar_tensor_tensor(eff, eff, 0.2, gsc[:, 40:48], ALU.mult, ALU.add))
            V(lambda e: e.tensor_single_scalar(eff, eff, 0.0, ALU.is_ge))
            V(lambda e: e.tensor_scalar(gsc[:, 40:48], eff, 1e30, -1e30, ALU.mult, ALU.add))
            V(lambda e: e.tensor_mul(eff, eff, msk))
            V(lambda e: e.tensor_add(eff, eff, gsc[:, 40:48]))
            sv = S(lambda e: e.activation(eff, eff, AF.Exp, accum_out=gsum))[0]
            V(lambda e: e.tensor_mul(eff, eff, onehot), w=[("se", sv)])
            V(lambda e: e.reduce_sum(num, eff, axis=AXX))
            V(lambda e: e.reciprocal(wk, gsum))
            V(lambda e: e.tensor_mul(wk, wk, num))
            V(lambda e, k=k: e.tensor_mul(wk, wk, valid[:, k:k + 1]))
            V(lambda e: e.scalar_tensor_tensor(rw[:, :], onehot, wk, rw[:, :], ALU.mult, ALU.add))
            if k < 3:
                V(lambda e: e.tensor_copy(msk, msk2))
        V(lambda e: e.reduce_sum(num, rw[:, :], axis=AXX))
        V(lambda e: e.reciprocal(num, num))
        V(lambda e, g=g: e.tensor_scalar(rw_all[:, g * 8:(g + 1) * 8], rw[:, :], num, None, ALU.mult))
        V(lambda e, g=g: e.tensor_single_scalar(gsc[:, 40:48], rw[:, :], 0.0, ALU.is_gt))
        tv = p.op("tensor", (lambda e: e.transpose(psB[0:8, 128:256], gsc[:, 40:48], idn[:, :])),
                  wait=[("ve", p.now("ve")), ("se", p.now("se"))], inc=[("pe", 1)])[0]
        p.op("vector", (lambda e, g=g: e.tensor_copy(mcap[0:8, g * 128:(g + 1) * 128], psB[0:8, 128:256])),
             wait=[("pe", tv)], inc=[("ve", 1)])

    p.op("gpsimd", lambda e: e.dma_start(out=c.t["cc_cap_in"].ap()[:, :], in_=mcap[0:8, :]),
         wait=[("ve", p.now("ve"))], inc=[("cd", 16)])
    p.op("gpsimd", lambda e: e.collective_compute(
        "AllGather", ALU.bypass, replica_groups=[list(range(NCORES))],
        ins=[c.t["cc_cap_in"].ap().opt()], outs=[c.t["cc_cap_out"].ap().opt()]),
        wait=[("cc", p.now("cc")), ("cd", p.now("cd"))], inc=[("cc", 1)])
    gcap = c.t["gcap"]
    p.op("gpsimd", lambda e: e.dma_start(out=gcap[:, :], in_=c.t["cc_cap_out"].ap()[:, :]),
         wait=[("cc", p.now("cc")), ("cd", p.now("cd"))], inc=[("cd", 16)])
    ccv = p.now("cc")
    cdv = p.now("cd")
    tot = c.t["tot64"]
    p.op("vector", lambda e: e.reduce_sum(tot[:, 0:1], gcap[:, :], axis=AXX),
         wait=[("cc", ccv), ("cd", cdv), ("ve", p.now("ve"))], inc=[("ve", 1)])
    tv = p.op("tensor", lambda e: e.transpose(psB[0:1, 256:320], tot[:, 0:1], idn[0:64, 0:64]),
              wait=[("ve", p.now("ve")), ("se", p.now("se"))], inc=[("pe", 1)])[0]
    p.op("vector", lambda e: e.tensor_copy(c.t["srow"][0:1, 48:112][0:1, 0:64], psB[0:1, 256:320]),
         wait=[("pe", tv)], inc=[("ve", 1)])
    for e_ in range(8):
        p.op("vector", (lambda e, e_=e_: e.tensor_mul(
            c.t["srow"][0:1, 112:120],
            c.t["srow"][0:1, 48:112].rearrange("one (r n) -> one n r", n=8)[:, e_],
            c.t["prevrow_sb"][0:1, 0:8])),
            wait=[("ve", p.now("ve"))], inc=[("ve", 1)])
        p.op("vector", (lambda e, e_=e_: e.reduce_sum(
            c.t["srow"][0:1, 40 + e_:41 + e_], c.t["srow"][0:1, 112:120], axis=AXX)),
            wait=[("ve", p.now("ve"))], inc=[("ve", 1)])
    tv = p.op("tensor", lambda e: e.transpose(psB[0:8, 320:321], c.t["srow"][0:1, 40:48], idn[0:1, 0:1]),
              wait=[("ve", p.now("ve")), ("se", p.now("se"))], inc=[("pe", 1)])[0]
    offs = c.t["offs8"]
    p.op("vector", lambda e: e.tensor_copy(offs[0:8, 0:1], psB[0:8, 320:321]),
         wait=[("pe", tv)], inc=[("ve", 1)])
    for jj in range(4):
        init = 0.0 if jj == 0 else capm[0:8, jj * 512 - 1:jj * 512]
        p.op("vector", (lambda e, jj=jj, init=init: e.tensor_tensor_scan(
            capm[0:8, jj * 512:(jj + 1) * 512], ones8[0:8, 0:512],
            mcap[0:8, jj * 512:(jj + 1) * 512], init, ALU.mult, ALU.add)),
            wait=[("ve", p.now("ve"))], inc=[("ve", 1)])
    p.op("vector", lambda e: e.tensor_scalar(capm[0:8, :], capm[0:8, :], offs[0:8, 0:1], None, ALU.add),
         wait=[("ve", p.now("ve"))], inc=[("ve", 1)])
    p.op("vector", lambda e: e.tensor_single_scalar(capm[0:8, :], capm[0:8, :], CAP, ALU.is_le),
         wait=[("ve", p.now("ve"))], inc=[("ve", 1)])
    p.op("vector", lambda e: e.tensor_mul(capm[0:8, :], capm[0:8, :], mcap[0:8, :]),
         wait=[("ve", p.now("ve"))], inc=[("ve", 1)])

    for g in range(NG):
        fl = lgt[:, g * 9: g * 9 + 9]
        cm_t = gsc[:, 0:8]
        eff9 = gsc[:, 8:17]
        gsum = gsc[:, 56:57]
        sd = gsc[:, 57:58]
        V = lambda build, w=(): p.op("vector", build, wait=list(w) + [("ve", p.now("ve"))], inc=[("ve", 1)])
        tv = p.op("tensor", (lambda e, g=g: e.transpose(
            psB[:, 0:8], capm[0:8, g * 128:(g + 1) * 128], idn[0:8, 0:8])),
            wait=[("ve", p.now("ve")), ("se", p.now("se"))], inc=[("pe", 1)])[0]
        V(lambda e: e.tensor_copy(cm_t, psB[:, 0:8]), w=[("pe", tv)])
        V(lambda e: e.tensor_scalar(gsc[:, 40:48], cm_t, 1e30, -1e30, ALU.mult, ALU.add))
        V(lambda e, fl=fl: e.tensor_mul(eff9[:, 0:8], fl[:, 0:8], cm_t))
        V(lambda e: e.tensor_add(eff9[:, 0:8], eff9[:, 0:8], gsc[:, 40:48]))
        V(lambda e, fl=fl: e.tensor_copy(eff9[:, 8:9], fl[:, 8:9]))
        sv = p.op("scalar", lambda e: e.activation(eff9, eff9, AF.Exp, accum_out=gsum),
                  wait=[("ve", p.now("ve")), ("se", p.now("se"))], inc=[("se", 1)])[0]
        V(lambda e: e.reciprocal(gsum, gsum), w=[("se", sv)])
        V(lambda e: e.tensor_scalar(eff9, eff9, gsum, None, ALU.mult))
        V(lambda e: e.reduce_sum(sd, eff9[:, 0:8], axis=AXX))
        V(lambda e, g=g: e.tensor_scalar(gsc[:, 40:48], rw_all[:, g * 8:(g + 1) * 8], sd, None, ALU.mult))
        V(lambda e: e.tensor_mul(gsc[:, 40:48], gsc[:, 40:48], cm_t))
        V(lambda e, g=g: e.tensor_scalar(glt[:, g * 9:g * 9 + 8], gsc[:, 40:48], 0.5, None, ALU.mult))
        V(lambda e, g=g: e.tensor_scalar(glt[:, g * 9 + 8:g * 9 + 9], eff9[:, 8:9], 0.5, None, ALU.mult))
    gl_fm = c.t["gl_fm"]
    for g in range(NG):
        tv = p.op("tensor", (lambda e, g=g: e.transpose(
            psB[0:9, 0:128], glt[:, g * 9:(g + 1) * 9], idn[:, :])),
            wait=[("ve", p.now("ve")), ("se", p.now("se"))], inc=[("pe", 1)])[0]
        p.op("vector", (lambda e, g=g: e.tensor_copy(gl_fm[0:9, g * 128:(g + 1) * 128], psB[0:9, 0:128])),
             wait=[("pe", tv)], inc=[("ve", 1)])
    p.op("gpsimd", lambda e: e.dma_start(out=c.t["gl_d"].ap()[:, :], in_=gl_fm[0:9, :]),
         wait=[("ve", p.now("ve"))], inc=[("cd", 16)])

    wegp = [c.t["weg0"], c.t["weg1"]]
    weup = [c.t["weu0"], c.t["weu1"]]
    wedp = [c.t["wed0"], c.t["wed1"]]
    gu = c.t["gu_sb"]
    wbc = c.t["wbc"]
    silu_scr = c.t["ev_scr"]
    bank_free = {mc: ("ve", 0) for mc in range(NCH)}
    fw = ("fwa", "fwb")
    cv_tot = p.now("cv")
    NE = NEXP + 4

    def wsrc(e_):
        if e_ < NEXP:
            return (c.t["dyn_g_bf"].ap()[e_].rearrange("(kc p) m -> p kc m", p=128),
                    c.t["dyn_u_bf"].ap()[e_].rearrange("(kc p) m -> p kc m", p=128),
                    c.t["dyn_d_bf"].ap()[e_].rearrange("(kc p) m -> p kc m", p=128))
        q = e_ - NEXP
        return (c.t["sh_g_bf"].ap().rearrange("(kc p) m -> p kc m", p=128)[:, :, q * 512:(q + 1) * 512],
                c.t["sh_u_bf"].ap().rearrange("(kc p) m -> p kc m", p=128)[:, :, q * 512:(q + 1) * 512],
                c.t["sh_d_bf"].ap().rearrange("(q kc p) m -> q p kc m", p=128, q=4)[q])

    fw_mark = {}

    def ld_exp(e_, extra=()):
        par = e_ % 2
        g_, u_, d_ = wsrc(e_)
        p.op("sync", (lambda e2, s=g_, par=par: e2.dma_start(
            out=wegp[par].rearrange("p (kc m) -> p kc m", kc=NCH), in_=s)),
            wait=list(extra) + ([("cv", cv_tot)] if e_ < 2 else []),
            inc=[(fw[par], 16)])
        p.op("sync", (lambda e2, s=u_, par=par: e2.dma_start(
            out=weup[par].rearrange("p (kc m) -> p kc m", kc=NCH), in_=s)),
            inc=[(fw[par], 16)])
        p.op("sync", (lambda e2, s=d_, par=par: e2.dma_start(
            out=wedp[par].rearrange("p (kc m) -> p kc m", kc=4), in_=s)),
            inc=[(fw[par], 16)])
        fw_mark[e_] = p.now(fw[par])

    ld_exp(0)
    ld_exp(1)
    cd_gl = p.now("cd")
    bv_prev = 0
    for e_ in range(NE):
        par = e_ % 2
        weg, weu, wed = wegp[par], weup[par], wedp[par]
        er = min(e_, NEXP)
        for j in range(NTT0):
            glv = p.op("gpsimd", (lambda e, er=er, j=j: e.dma_start(
                out=c.t["glrow_s"][0:1, 0:TT0],
                in_=c.t["gl_d"].ap()[er:er + 1, j * TT0:(j + 1) * TT0])),
                wait=[("cd", cd_gl), ("pe", bv_prev)], inc=[("cd", 16)])[0]
            bv = p.op("tensor", (lambda e: e.matmul(
                psA[:, 0:TT0], orow[:, :], c.t["glrow_s"][0:1, 0:TT0],
                start=True, stop=True)),
                wait=[bank_free[0], ("se", p.now("se")), ("cd", glv)], inc=[("pe", 1)])[0]
            bv_prev = bv
            v = p.op("vector", lambda e: e.tensor_copy(wbc[:, 0:TT0], psA[:, 0:TT0]),
                     wait=[("pe", bv)], inc=[("ve", 1)])[0]
            bank_free[0] = ("ve", v)
            for half2, w in ((0, weg), (1, weu)):
                for mc in range(4):
                    bank_i = half2 * 4 + mc
                    for kc in range(NCH):
                        wl = []
                        if kc == 0:
                            wl = [bank_free[bank_i]]
                            if j == 0 and half2 == 0 and mc == 0:
                                wl.append((fw[par], fw_mark[e_]))
                        p.op("tensor", (lambda e, w=w, mc=mc, kc=kc, j=j, bank_i=bank_i: e.matmul(
                            c.t[f"psO{bank_i}"][:, 0:TT0],
                            w[:, kc * 512 + mc * 128: kc * 512 + mc * 128 + 128],
                            xqa(kc, j), start=(kc == 0), stop=(kc == NCH - 1))),
                            wait=wl, inc=([("pe", 1)] if kc == NCH - 1 else []))
            pe_mm = p.now("pe")
            for mc in range(4):
                sv = p.op("scalar", (lambda e, mc=mc: e.activation(
                    silu_scr[:, 0:TT0], c.t[f"psO{mc}"][:, 0:TT0], AF.Silu)),
                    wait=([("pe", pe_mm), ("ve", p.now("ve"))] if mc == 0
                          else [("se", p.now("se")), ("ve", p.now("ve"))]),
                    inc=[("se", 1)])[0]
                bank_free[mc] = ("se", sv)
                v = p.op("vector", (lambda e, mc=mc: e.tensor_tensor(
                    c.t["q_scr"][:, 0:TT0], silu_scr[:, 0:TT0], c.t[f"psO{4 + mc}"][:, 0:TT0], ALU.mult)),
                    wait=[("se", sv)], inc=[("ve", 1)])[0]
                bank_free[4 + mc] = ("ve", v)
                p.op("vector", (lambda e, mc=mc: e.tensor_tensor(
                    gu[:, mc * TT0:(mc + 1) * TT0],
                    c.t["q_scr"][:, 0:TT0], wbc[:, 0:TT0], ALU.mult)),
                    wait=[("ve", p.now("ve"))], inc=[("ve", 1)])
            ve_gu = p.now("ve")
            for mc in range(NCH):
                for kc in range(4):
                    wl = []
                    if kc == 0:
                        wl = [("ve", ve_gu), bank_free[mc]] if mc == 0 else [bank_free[mc]]
                    p.op("tensor", (lambda e, mc=mc, kc=kc, wed=wed: e.matmul(
                        c.t[f"psO{mc}"][:, 0:TT0],
                        wed[:, kc * D + mc * 128: kc * D + mc * 128 + 128],
                        gu[:, kc * TT0:(kc + 1) * TT0],
                        start=(kc == 0), stop=(kc == 3))),
                        wait=wl, inc=([("pe", 1)] if kc == 3 else []))
            pe_dn = p.now("pe")
            for mc in range(NCH):
                v = p.op("vector", (lambda e, mc=mc, j=j: e.tensor_add(
                    xf(mc, j), xf(mc, j), c.t[f"psO{mc}"][:, 0:TT0])),
                    wait=[("pe", pe_dn), ("ve", p.now("ve"))], inc=[("ve", 1)])[0]
                bank_free[mc] = ("ve", v)
        if e_ + 2 < NE:
            ld_exp(e_ + 2, extra=[("pe", p.now("pe"))])


def emit_f0_preconv(c):
    p = c.p
    for src, dst in (("dyn_gT_d", "dyn_g_bf"), ("dyn_uT_d", "dyn_u_bf"),
                     ("dyn_dT_d", "dyn_d_bf"), ("sh_gT_d", "sh_g_bf"),
                     ("sh_uT_d", "sh_u_bf"), ("sh_dT_d", "sh_d_bf")):
        sap = c.t[src].ap()
        dap = c.t[dst].ap()
        if len(sap.shape) == 3:
            for h in range(NEXP):
                p.op("gpsimd", (lambda e, s=sap, d=dap, h=h: e.dma_start(
                    out=d[h], in_=s[h])), inc=[("cv", 16)])
        else:
            n = sap.shape[0]
            for h in range(2):
                p.op("gpsimd", (lambda e, s=sap, d=dap, h=h, n=n: e.dma_start(
                    out=d[h * (n // 2):(h + 1) * (n // 2), :],
                    in_=s[h * (n // 2):(h + 1) * (n // 2), :])), inc=[("cv", 16)])


def f0_alloc(c):
    c.sbuf("w9", [128, NCH * 9], BF16)
    c.sbuf("fm9", [9, TC], F32)
    c.t["lg_fm"] = c.t["fm9"]
    c.t["gl_fm"] = c.t["fm9"]
    c.sbuf("lgt", [128, NG * 9], F32)
    c.sbuf("gsc", [128, 96], F32)
    c.sbuf("rw", [128, 8], F32)
    c.sbuf("rw_all", [128, NG * 8], F32)
    c.sbuf("glt", [128, NG * 9], F32)
    c.sbuf("mcap_fm", [8, TC], BF16)
    c.sbuf("ones8", [8, 512], F32)
    c.sbuf("gcap", [64, TC], F32)
    c.t["capm_fm"] = c.t["gcap"]
    c.sbuf("tot64", [64, 1], F32)
    c.sbuf("offs8", [8, 1], F32)
    c.sbuf("glrow_s", [1, 512], BF16)
    for par in range(2):
        c.sbuf(f"weg{par}", [128, NCH * 512], BF16)
        c.sbuf(f"weu{par}", [128, NCH * 512], BF16)
        c.sbuf(f"wed{par}", [128, 4 * D], BF16)
    c.sbuf("gu_sb", [128, 4 * 512], BF16)
    c.sbuf("wbc", [128, 512], BF16)
    c.dram("cc_cap_in", [8, TC], F32)
    c.dram("cc_cap_out", [64, TC], F32)
    c.dram("gl_d", [9, TC], F32)
    c.dram("dyn_g_bf", [NEXP, D, IDYN], BF16)
    c.dram("dyn_u_bf", [NEXP, D, IDYN], BF16)
    c.dram("dyn_d_bf", [NEXP, IDYN, D], BF16)
    c.dram("sh_g_bf", [D, ISH], BF16)
    c.dram("sh_u_bf", [D, ISH], BF16)
    c.dram("sh_d_bf", [ISH, D], BF16)


def build_f0_standalone(debug=False):
    nc = bass.Bass("TRN2", target_bir_lowering=False)
    c = Ctx(nc)
    x_in = nc.dram_tensor("x_in", [D, TC], F32, kind="ExternalInput")
    c.t["gate_wT_d"] = nc.dram_tensor("gate_wT", [D, 9], F32, kind="ExternalInput")
    c.t["dyn_gT_d"] = nc.dram_tensor("dyn_gT", [NEXP, D, IDYN], F32, kind="ExternalInput")
    c.t["dyn_uT_d"] = nc.dram_tensor("dyn_uT", [NEXP, D, IDYN], F32, kind="ExternalInput")
    c.t["dyn_dT_d"] = nc.dram_tensor("dyn_dT", [NEXP, IDYN, D], F32, kind="ExternalInput")
    c.t["sh_gT_d"] = nc.dram_tensor("sh_gT", [D, ISH], F32, kind="ExternalInput")
    c.t["sh_uT_d"] = nc.dram_tensor("sh_uT", [D, ISH], F32, kind="ExternalInput")
    c.t["sh_dT_d"] = nc.dram_tensor("sh_dT", [ISH, D], F32, kind="ExternalInput")
    prevrow = nc.dram_tensor("prevrow", [1, 8], F32, kind="ExternalInput")
    x_out = nc.dram_tensor("x_out", [D, TC], F32, kind="ExternalOutput")
    _ = None
    if debug:
        c.t["dbg_lg_d"] = nc.dram_tensor("dbg_lg", [9, TC], F32, kind="ExternalOutput")
        dbg_gl = nc.dram_tensor("dbg_gl", [9, TC], F32, kind="ExternalOutput")
        dbg_capm = nc.dram_tensor("dbg_capm", [8, TC], F32, kind="ExternalOutput")
        dbg_rw = nc.dram_tensor("dbg_rw", [128, NG * 8], F32, kind="ExternalOutput")
        dbg_mcap = nc.dram_tensor("dbg_mcap", [8, TC], F32, kind="ExternalOutput")
        dbg_offs = nc.dram_tensor("dbg_offs", [8, 1], F32, kind="ExternalOutput")
        c.t["dbg_gate_d"] = nc.dram_tensor("dbg_gate", [128, 32], F32, kind="ExternalOutput")
        c.sbuf("dbg_gate_sb", [128, 32], F32)

    c.mk_consts()
    c.sbuf("x_sb", [128, NCH * TC], F32)
    c.sbuf("xq", [128, NCH * TC], BF16)
    c.sbuf("stat", [1, 2048], F32)
    c.sbuf("row_bf", [1, 1024], BF16)
    c.sbuf("sq_bf", [128, 512], BF16)
    c.sbuf("xbf_scr", [128, 512], BF16)
    c.sbuf("ln_scr", [128, 512], F32)
    c.sbuf("red_scr", [128, 1], F32)
    c.sbuf("srow", [1, 128], F32)
    c.sbuf("q_scr", [128, 512], F32)
    c.sbuf("ev_scr", [128, 512], F32)
    c.sbuf("prevrow_sb", [1, 8], F32)
    for i in range(NCH):
        c.psum(f"psO{i}", [128, 512], F32)
    c.t["psA"] = c.t["psO0"]
    c.t["psB"] = c.t["psO1"]
    f0_alloc(c)
    p = c.p
    load_consts(c)
    emit_f0_preconv(c)
    xv = x_in.ap().rearrange("(c p) t -> p c t", p=128)
    p.op("sync", lambda e: e.dma_start(
        out=c.t["x_sb"].rearrange("p (c t) -> p c t", c=NCH), in_=xv), inc=[("ld", 16)])
    p.op("sync", lambda e: e.dma_start(out=c.t["prevrow_sb"][0:1, :], in_=prevrow.ap()[:, :]),
         inc=[("ld", 16)])
    p.op("vector", lambda e: e.memset(c.t["ones8"][0:8, :], 1.0), inc=[("ve", 1)])
    for eng in ("vector", "tensor", "scalar"):
        p.wait(eng, ("ld", p.now("ld")))
    emit_f0(c, 0)
    p.op("sync", lambda e: e.dma_start(
        out=x_out.ap().rearrange("(c p) t -> p c t", p=128),
        in_=c.t["x_sb"].rearrange("p (c t) -> p c t", c=NCH)),
        wait=[("ve", p.now("ve")), ("se", p.now("se"))], inc=[("st", 16)])
    if debug:
        p.op("sync", lambda e: e.dma_start(out=dbg_gl.ap()[:, :], in_=c.t["gl_fm"][0:9, :]),
             inc=[("st", 16)])
        p.op("sync", lambda e: e.dma_start(out=dbg_capm.ap()[:, :], in_=c.t["capm_fm"][0:8, :]),
             inc=[("st", 16)])
        p.op("sync", lambda e: e.dma_start(out=dbg_rw.ap()[:, :], in_=c.t["rw_all"][:, :]),
             inc=[("st", 16)])
        p.op("gpsimd", lambda e: e.dma_start(out=dbg_mcap.ap()[:, :], in_=c.t["mcap_fm"][0:8, :]),
             inc=[("st", 16)])
        p.op("sync", lambda e: e.dma_start(out=dbg_offs.ap()[:, :], in_=c.t["offs8"][0:8, :]),
             inc=[("st", 16)])
        p.op("sync", lambda e: e.dma_start(out=c.t["dbg_gate_d"].ap()[:, :], in_=c.t["dbg_gate_sb"][:, :]),
             inc=[("st", 16)])
    p.wait("sync", ("st", p.now("st")))
    p.finalize()
    return nc


def run_f0(d, debug=False):
    nc = build_f0_standalone(debug)
    in_maps = []
    for core in range(NCORES):
        prev = np.zeros((1, 8), np.float32)
        prev[0, :core] = 1.0
        in_maps.append({
            "x_in": x_slice(d, core), "gate_wT": d["gate_wT"],
            "dyn_gT": d["dyn_gT"], "dyn_uT": d["dyn_uT"], "dyn_dT": d["dyn_dT"],
            "sh_gT": d["sh_gT"], "sh_uT": d["sh_uT"], "sh_dT": d["sh_dT"],
            "prevrow": prev,
        })
    res = run_bass_kernel_spmd(nc, in_maps, core_ids=list(range(NCORES)))
    if debug:
        return res
    return np.concatenate([res.results[cc]["x_out"] for cc in range(NCORES)], axis=1)





NHD = 8
HD = 64


def emit_a1(c, inv):
    p = c.p
    x_sb = c.t["x_sb"]
    xq = c.t["xq"]
    orow = c.t["ones_row_sb"]
    psA, psB = c.t["psA"], c.t["psB"]
    psS = [(c.t["psO2"], c.t["psO3"]), (c.t["psO4"], c.t["psO5"])]
    psAV = (c.t["psO6"], c.t["psO7"])
    qkv_sb = c.t["qkv_sb"]
    th_sb = c.t["th_sb"]
    wq = c.t["wqkv_w"]
    parsel = c.t["parsel_sb"]
    pk = c.t["pk_sb"]
    blk = c.t["blk_sb"]
    vt = c.t["vt_sb"]
    pT = c.t["pT_sb"]
    oh = c.t["oh_sb"]
    lrec = c.t["lrec_sb"]
    lb = c.t["lbc_sb"]
    trimask = c.t["trimask_sb"]

    def xf(ch, j):
        return x_sb[:, ch * TC + j * 512: ch * TC + j * 512 + 512]

    def xqa(ch, j):
        return xq[:, ch * TC + j * 512: ch * TC + j * 512 + 512]

    p.op("sync", lambda e: e.dma_start(out=trimask[:, :], in_=c.const["trimask"].ap()[:, :]),
         inc=[("ld", 16)])
    p.op("sync", lambda e: e.dma_start(out=c.t["ident_bf_sb"][:, :], in_=c.const["ident_bf"].ap()[:, :]),
         inc=[("ld", 16)])
    ldc = p.now("ld")

    for j in range(4):
        emit_ln(c, (lambda ch, j=j: xf(ch, j)), (lambda ch, j=j: xqa(ch, j)), 512)

    for ocg in range(3):
        pe_b = p.now("pe")
        wv = c.t["wqvk_d"].ap().rearrange("(kc p) m -> p kc m", p=128)[:, :, ocg * 1024:(ocg + 1) * 1024]
        p.op("gpsimd", (lambda e, s=wv: e.dma_start(
            out=wq[:, 0:NCH * 1024].rearrange("p (kc m) -> p kc m", kc=NCH), in_=s)),
            wait=[("pe", pe_b)], inc=[("wl", 16)])
        wl = p.now("wl")
        for j in range(4):
            for oc2 in range(8):
                bank = c.t[f"psO{oc2}"]
                for kc in range(NCH):
                    p.op("tensor", (lambda e, oc2=oc2, kc=kc, j=j, bank=bank: e.matmul(
                        bank[:, 0:512],
                        wq[:, kc * 1024 + oc2 * 128: kc * 1024 + oc2 * 128 + 128],
                        xqa(kc, j), start=(kc == 0), stop=(kc == NCH - 1))),
                        wait=([("wl", wl), ("ve", p.now("ve")), ("se", p.now("se"))]
                              if (oc2 == 0 and kc == 0) else []),
                        inc=([("pe", 1)] if kc == NCH - 1 else []))
            pe_mm = p.now("pe")
            for oc2 in range(8):
                ochunk = ocg * 8 + oc2
                bank = c.t[f"psO{oc2}"]
                if ochunk < 12:
                    dst = qkv_sb[:, ochunk * TC + j * 512: ochunk * TC + j * 512 + 512]
                    p.op("vector", (lambda e, dst=dst, bank=bank: e.tensor_copy(dst, bank[:, 0:512])),
                         wait=([("pe", pe_mm)] if oc2 == 0 else [("ve", p.now("ve"))]),
                         inc=[("ve", 1)])
                else:
                    p.op("vector", (lambda e, bank=bank: e.tensor_copy(th_sb[:, 0:512], bank[:, 0:512])),
                         wait=([("pe", pe_mm), ("cd", p.now("cd"))] if oc2 == 0
                               else [("ve", p.now("ve")), ("cd", p.now("cd"))]),
                         inc=[("ve", 1)])
                    tv = c.t["cc_qkv_in"].ap().rearrange("(ch p) t -> ch p t", p=128)[ochunk - 12]
                    p.op("gpsimd", (lambda e, tv=tv, j=j: e.dma_start(
                        out=tv[:, j * 512:(j + 1) * 512], in_=th_sb[:, 0:512])),
                        wait=[("ve", p.now("ve"))], inc=[("cd", 16)])
    p.op("gpsimd", lambda e: e.collective_compute(
        "AllGather", ALU.bypass, replica_groups=[[0, 1], [2, 3], [4, 5], [6, 7]],
        ins=[c.t["cc_qkv_in"].ap().opt()], outs=[c.t["cc_qkv_out"].ap().opt()]),
        wait=[("cc", p.now("cc")), ("cd", p.now("cd"))], inc=[("cc", 1)])
    ccg = p.now("cc")

    psXbf = psB.bitcast(BF16)

    for pk_i in range(NHD // 2):
        for b_i in range(2):
            for s_i in range(3):
                rows0 = b_i * 1536 + s_i * 512 + pk_i * 128
                gv = c.t["cc_qkv_out"].ap().rearrange("(r p) t -> r p t", p=128)[rows0 // 128]
                p.op("gpsimd", (lambda e, gv=gv, b_i=b_i, s_i=s_i: e.dma_start(
                    out=blk[:, (b_i * 3 + s_i) * TC:(b_i * 3 + s_i) * TC + TC], in_=gv)),
                    wait=([("cc", ccg), ("ve", p.now("ve"))] if (b_i == 0 and s_i == 0) else []),
                    inc=[("cd", 16)])
        ccb = p.now("cd")
        for s_i in range(3):
            mine = qkv_sb[:, (s_i * 4 + pk_i) * TC:(s_i * 4 + pk_i) * TC + TC]
            b0 = blk[:, (0 + s_i) * TC:(0 + s_i) * TC + TC]
            b1 = blk[:, (3 + s_i) * TC:(3 + s_i) * TC + TC]
            for half in range(2):
                dst = pk[:, s_i * 4096 + half * 2048: s_i * 4096 + half * 2048 + 2048]
                cm = 2 + half
                cb0 = 4 + 2 * half
                cb1 = 5 + 2 * half
                p.op("vector", (lambda e, dst=dst, mine=mine, cm=cm: e.tensor_scalar(
                    dst, mine, parsel[:, cm:cm + 1], None, ALU.mult)),
                    wait=[("cd", ccb), ("ve", p.now("ve"))], inc=[("ve", 1)])
                p.op("vector", (lambda e, dst=dst, b0=b0, cb0=cb0: e.scalar_tensor_tensor(
                    dst, b0, parsel[:, cb0:cb0 + 1], dst, ALU.mult, ALU.add)),
                    wait=[("ve", p.now("ve"))], inc=[("ve", 1)])
                p.op("vector", (lambda e, dst=dst, b1=b1, cb1=cb1: e.scalar_tensor_tensor(
                    dst, b1, parsel[:, cb1:cb1 + 1], dst, ALU.mult, ALU.add)),
                    wait=[("ve", p.now("ve"))], inc=[("ve", 1)])
        p.op("vector", lambda e: e.memset(vt[:, :], 1.0), wait=[("ve", p.now("ve"))], inc=[("ve", 1)])
        for kt in range(32):
            tv = p.op("tensor", (lambda e, kt=kt: e.transpose(
                psXbf[:, 0:128],
                pk[:, 2 * 4096 + kt * 128: 2 * 4096 + kt * 128 + 128],
                c.t["ident_bf_sb"][:, :])),
                wait=[("ve", p.now("ve")), ("se", p.now("se")), ("ld", ldc)], inc=[("pe", 1)])[0]
            for hh in range(2):
                p.op("vector", (lambda e, kt=kt, hh=hh: e.tensor_copy(
                    vt[:, (hh * 32 + kt) * 65:(hh * 32 + kt) * 65 + 64],
                    psXbf[:, hh * 64:hh * 64 + 64])),
                    wait=([("pe", tv)] if hh == 0 else [("ve", p.now("ve"))]), inc=[("ve", 1)])
        for hh in range(2):
            for qg in range(4):
                q0 = qg * 1024
                for kt in range((qg + 1) * 8):
                    di = kt - qg * 8
                    n0 = 0 if di < 0 else di * 128
                    sbanks = psS[kt % 2]
                    first_w = [("se", p.now("se")), ("ve", p.now("ve"))]
                    first_half = 0 if n0 < 512 else 1
                    for half in range(2):
                        a0, a1 = half * 512, half * 512 + 512
                        if a1 <= n0:
                            continue
                        s0 = max(a0, n0)
                        p.op("tensor", (lambda e, kt=kt, s0=s0, a1=a1, q0=q0, sb=sbanks[half], a0=a0, hh=hh: e.matmul(
                            sb[:, s0 - a0:a1 - a0],
                            pk[hh * 64:hh * 64 + 64, 4096 + kt * 128: 4096 + kt * 128 + 128],
                            pk[hh * 64:hh * 64 + 64, q0 + s0: q0 + a1],
                            start=True, stop=True)),
                            wait=(first_w if half == first_half else []),
                            inc=[("pe", 1)])
                    pe_s = p.now("pe")
                    if di >= 0:
                        hf = n0 // 512
                        p.op("vector", (lambda e, sb=sbanks[n0 // 512], r0=n0 % 512: e.tensor_add(
                            sb[:, r0:r0 + 128], sb[:, r0:r0 + 128], trimask[:, :])),
                            wait=[("pe", pe_s), ("ve", p.now("ve"))], inc=[("ve", 1)])
                    sv = None
                    for half in range(2):
                        a0, a1 = half * 512, half * 512 + 512
                        if a1 <= n0:
                            continue
                        s0 = max(a0, n0)
                        sv = p.op("scalar", (lambda e, sb=sbanks[half], s0=s0, a0=a0, a1=a1: e.activation(
                            pT[:, s0:a1], sb[:, s0 - a0:a1 - a0], AF.Exp, scale=1.0 / 32.0)),
                            wait=([("pe", pe_s), ("ve", p.now("ve")), ("se", p.now("se"))]
                                  if half == first_half else []),
                            inc=[("se", 1)])[0]
                    for half in range(2):
                        a0, a1 = half * 512, half * 512 + 512
                        if a1 <= n0:
                            continue
                        s0 = max(a0, n0)
                        p.op("tensor", (lambda e, kt=kt, s0=s0, a0=a0, a1=a1, hh=hh, qg=qg, av=psAV[half]: e.matmul(
                            av[0:65, s0 - a0:a1 - a0],
                            vt[:, (hh * 32 + kt) * 65:(hh * 32 + kt) * 65 + 65],
                            pT[:, s0:a1],
                            start=(kt == 0), stop=(kt == (qg + 1) * 8 - 1),
                            skip_group_check=True)),
                            wait=([("se", sv)] + ([("ve", p.now("ve"))] if kt == 0 else [])
                                  if half == first_half else []),
                            inc=[("pe", 1)])
                pe_av = p.now("pe")
                for half in range(2):
                    h0 = half * 512
                    p.op("vector", (lambda e, av=psAV[half]: e.reciprocal(
                        lrec[0:1, 0:512], av[64:65, 0:512])),
                        wait=[("pe", pe_av), ("ve", p.now("ve")), ("cd", p.now("cd"))], inc=[("ve", 1)])
                    p.op("vector", (lambda e: e.tensor_copy(
                        c.t["lrow_bf"][0:1, 0:512], lrec[0:1, 0:512])),
                        wait=[("ve", p.now("ve"))], inc=[("ve", 1)])
                    bv = p.op("tensor", (lambda e: e.matmul(
                        psB[0:64, 0:512], orow[:, 0:64],
                        c.t["lrow_bf"][0:1, 0:512], start=True, stop=True)),
                        wait=[("ve", p.now("ve")), ("se", p.now("se"))], inc=[("pe", 1)])[0]
                    p.op("vector", (lambda e: e.tensor_copy(
                        lb[0:64, 0:512], psB[0:64, 0:512])),
                        wait=[("pe", bv)], inc=[("ve", 1)])
                    p.op("vector", (lambda e, h0=h0, av=psAV[half]: e.tensor_mul(
                        oh[0:64, h0: h0 + 512],
                        av[0:64, 0:512], lb[0:64, 0:512])),
                        wait=[("ve", p.now("ve"))], inc=[("ve", 1)])
                hrow = (pk_i * 2 + hh) * 64
                ovq = c.t["cc_o_in"].ap().rearrange("(h p) t -> h p t", p=64)[hrow // 64]
                p.op("gpsimd", (lambda e, ovq=ovq, qg=qg: e.dma_start(
                    out=ovq[:, qg * 1024:(qg + 1) * 1024], in_=oh[0:64, 0:1024])),
                    wait=[("ve", p.now("ve"))], inc=[("cd", 16)])

    p.op("gpsimd", lambda e: e.collective_compute(
        "AllGather", ALU.bypass, replica_groups=[[0, 1], [2, 3], [4, 5], [6, 7]],
        ins=[c.t["cc_o_in"].ap().opt()], outs=[c.t["cc_o_out"].ap().opt()]),
        wait=[("cc", p.now("cc")), ("cd", p.now("cd"))], inc=[("cc", 1)])
    cco = p.now("cc")
    o_my = qkv_sb
    for ch in range(8):
        gv0 = c.t["cc_o_out"].ap().rearrange("(ch p) t -> ch p t", p=128)[ch]
        p.op("gpsimd", (lambda e, gv0=gv0: e.dma_start(
            out=blk[:, 0:4096], in_=gv0)),
            wait=[("cc", cco), ("ve", p.now("ve"))], inc=[("cd", 16)])
        ccb = p.now("cd")
        b0 = blk[:, 0:2048]
        b1 = blk[:, 2048:4096]
        dst = o_my[:, ch * TC: ch * TC + TC]
        p.op("vector", (lambda e, dst=dst, b0=b0: e.tensor_scalar(
            dst, b0, parsel[:, 2:3], None, ALU.mult)),
            wait=[("cd", ccb), ("ve", p.now("ve"))], inc=[("ve", 1)])
        p.op("vector", (lambda e, dst=dst, b1=b1: e.scalar_tensor_tensor(
            dst, b1, parsel[:, 3:4], dst, ALU.mult, ALU.add)),
            wait=[("ve", p.now("ve"))], inc=[("ve", 1)])
    wo = blk
    pe_b = p.now("pe")
    p.op("gpsimd", lambda e: e.dma_start(
        out=wo[:, 0:8 * D].rearrange("p (kc m) -> p kc m", m=D),
        in_=c.t["wo_d"].ap().rearrange("(kc p) m -> p kc m", p=128)),
        wait=[("pe", pe_b), ("ve", p.now("ve"))], inc=[("wl", 16)])
    wl = p.now("wl")
    for j in range(4):
        for mc in range(NCH):
            for kc in range(NCH):
                p.op("tensor", (lambda e, mc=mc, kc=kc, j=j: e.matmul(
                    c.t[f"psO{mc}"][:, 0:512],
                    wo[:, kc * D + mc * 128: kc * D + mc * 128 + 128],
                    o_my[:, kc * TC + j * 512: kc * TC + j * 512 + 512],
                    start=(kc == 0), stop=(kc == NCH - 1))),
                    wait=([("wl", wl), ("ve", p.now("ve")), ("se", p.now("se"))]
                          if (mc == 0 and kc == 0) else []),
                    inc=([("pe", 1)] if kc == NCH - 1 else []))
        pe_dn = p.now("pe")
        for mc in range(NCH):
            p.op("vector", (lambda e, mc=mc, j=j: e.tensor_add(
                xf(mc, j), xf(mc, j), c.t[f"psO{mc}"][:, 0:512])),
                wait=([("pe", pe_dn)] if mc == 0 else [("ve", p.now("ve"))]), inc=[("ve", 1)])


def a1_alloc(c):
    c.sbuf("qkv_sb", [128, 12 * TC], BF16)
    c.sbuf("th_sb", [128, 512], BF16)
    c.sbuf("pk_sb", [128, 3 * 4096], BF16)
    c.t["wqkv_w"] = c.t["pk_sb"]
    c.t["blk_sb"] = c.t["xq"]
    c.sbuf("vt_sb", [128, 2 * 32 * 65], BF16)
    c.sbuf("pT_sb", [128, 1024], BF16)
    c.sbuf("oh_sb", [64, 1024], BF16)
    c.sbuf("lrec_sb", [1, 512], F32)
    c.sbuf("lrow_bf", [1, 512], BF16)
    c.sbuf("lbc_sb", [64, 512], F32)
    c.sbuf("trimask_sb", [128, 128], F32)
    c.sbuf("ident_bf_sb", [128, 128], BF16)
    if "trimask" not in c.const:
        c.inline("trimask", np.where(np.arange(128)[:, None] > np.arange(128)[None, :],
                                     np.float32(-1e30), np.float32(0.0)).astype(np.float32))
        c.inline("ident_bf", np.eye(128, dtype=ml_dtypes.bfloat16))
    c.dram("cc_qkv_in", [12 * 128, TC], BF16)
    c.dram("cc_qkv_out", [2 * 12 * 128, TC], BF16)
    c.dram("cc_o_in", [512, 4096], BF16)
    c.dram("cc_o_out", [1024, 4096], BF16)


def a1_psum_alloc(c):
    for i in range(NCH):
        c.psum(f"psO{i}", [128, 512], F32)
    c.t["psA"] = c.t["psO0"]
    c.t["psB"] = c.t["psO1"]


def host_parsel(core):
    par = core & 1
    c0 = 1.0 if par == 1 else 0.0
    c1 = 1.0 - c0
    c2 = 1.0 if par == 0 else 0.0
    c3 = 1.0 - c2
    row = np.array([c0, c1, c2, c3, c0 * c3, c1 * c3, c0 * c2, c1 * c2], np.float32)
    return np.broadcast_to(row, (128, 8)).copy()


def host_wqvk_perm(Wqvk, core):
    par = core & 1
    mine = list(range(8 * par, 8 * par + 8))
    oth = list(range(8 * (1 - par), 8 * (1 - par) + 8))
    cols = []
    for heads in (mine, oth):
        for slot in (0, 2, 1):
            for h in heads:
                base = slot * 1024 + h * 64
                cols.extend(range(base, base + 64))
    return np.ascontiguousarray(Wqvk[:, cols])


def build_a1_standalone(debug=False):
    nc = bass.Bass("TRN2", target_bir_lowering=False)
    c = Ctx(nc)
    x_in = nc.dram_tensor("x_in", [D, TC], F32, kind="ExternalInput")
    c.t["wqvk_d"] = nc.dram_tensor("wqvk", [D, 3072], F32, kind="ExternalInput")
    c.t["wo_d"] = nc.dram_tensor("wo", [D, D], F32, kind="ExternalInput")
    parsel_in = nc.dram_tensor("parsel", [128, 8], F32, kind="ExternalInput")
    x_out = nc.dram_tensor("x_out", [D, TC], F32, kind="ExternalOutput")
    c.mk_consts()
    c.sbuf("x_sb", [128, NCH * TC], F32)
    c.sbuf("xq", [128, NCH * TC], BF16)
    c.sbuf("stat", [1, 2048], F32)
    c.sbuf("row_bf", [1, 1024], BF16)
    c.sbuf("sq_bf", [128, 512], BF16)
    c.sbuf("xbf_scr", [128, 512], BF16)
    c.sbuf("ln_scr", [128, 512], F32)
    c.sbuf("red_scr", [128, 1], F32)
    c.sbuf("srow", [1, 128], F32)
    a1_psum_alloc(c)
    a1_alloc(c)
    p = c.p
    load_consts(c, ident=False)
    xv = x_in.ap().rearrange("(c p) t -> p c t", p=128)
    p.op("sync", lambda e: e.dma_start(
        out=c.t["x_sb"].rearrange("p (c t) -> p c t", c=NCH), in_=xv), inc=[("ld", 16)])
    p.op("sync", lambda e: e.dma_start(out=c.t["parsel_sb"][:, :], in_=parsel_in.ap()[:, :]),
         inc=[("ld", 16)])
    for eng in ("vector", "tensor", "scalar"):
        p.wait(eng, ("ld", p.now("ld")))
    emit_a1(c, 0)
    p.op("sync", lambda e: e.dma_start(
        out=x_out.ap().rearrange("(c p) t -> p c t", p=128),
        in_=c.t["x_sb"].rearrange("p (c t) -> p c t", c=NCH)),
        wait=[("ve", p.now("ve")), ("se", p.now("se"))], inc=[("st", 16)])
    p.wait("sync", ("st", p.now("st")))
    p.finalize()
    return nc


def run_a1(d, debug=False):
    nc = build_a1_standalone(debug)
    in_maps = []
    for core in range(NCORES):
        in_maps.append({
            "x_in": x_slice(d, core),
            "wqvk": host_wqvk_perm(d["Wqvk"], core),
            "wo": d["Wo"],
            "parsel": host_parsel(core),
        })
    res = run_bass_kernel_spmd(nc, in_maps, core_ids=list(range(NCORES)))
    return np.concatenate([res.results[cc]["x_out"] for cc in range(NCORES)], axis=1)





LCH = 16
NCHK = TC // LCH


def emit_a0_powers(c):
    p = c.p
    pwa = c.t["pw_a"]
    pwb = c.t["pw_b"]
    myp, myq = c.t["myp_sb"], c.t["myq_sb"]
    p.op("gpsimd", lambda e: e.dma_start(
        out=pwa.rearrange("p (kc m) -> p kc m", kc=NCH),
        in_=c.t["A_T_d"].ap().rearrange("(kc p) m -> p kc m", p=128)), inc=[("pw", 16)])
    p.op("gpsimd", lambda e: e.dma_start(
        out=pwb.rearrange("p (kc m) -> p kc m", kc=NCH),
        in_=c.t["A_nat_d"].ap().rearrange("(kc p) m -> p kc m", p=128)), inc=[("pw", 16)])
    d = 1
    for rnd in range(8):
        pwv = p.now("pw")
        for which in range(2):
            lw = pwb if which == 0 else pwa
            rhs = myp if which == 0 else myq
            dstmy = myp if which == 0 else myq
            for ic in range(NCH):
                bank = c.t[f"psO{ic}"]
                for kc in range(NCH):
                    p.op("tensor", (lambda e, lw=lw, ic=ic, kc=kc, rhs=rhs, bank=bank: e.matmul(
                        bank[:, 0:128],
                        lw[:, kc * 1024 + ic * 128: kc * 1024 + ic * 128 + 128],
                        rhs[:, kc * 128:(kc + 1) * 128],
                        start=(kc == 0), stop=(kc == NCH - 1))),
                        wait=([("pw", pwv), ("ve", p.now("ve")), ("se", p.now("se"))]
                              if (ic == 0 and kc == 0) else []),
                        inc=([("pe", 1)] if kc == NCH - 1 else []))
            pe_mm = p.now("pe")
            for ic in range(NCH):
                bank = c.t[f"psO{ic}"]
                p.op("vector", (lambda e, ic=ic, bank=bank, dstmy=dstmy: e.tensor_copy(
                    dstmy[:, ic * 128:(ic + 1) * 128], bank[:, 0:128])),
                    wait=([("pe", pe_mm)] if ic == 0 else [("ve", p.now("ve"))]), inc=[("ve", 1)])
        for which in range(2):
            dstmy = myp if which == 0 else myq
            dst = pwa if which == 0 else pwb
            cin = c.t["cc_pw_in"]
            cout = c.t["cc_pw_out"] if which == 0 else c.t["cc_pw_out2"]
            p.op("gpsimd", (lambda e, cin=cin, dstmy=dstmy: e.dma_start(
                out=cin.ap().rearrange("(kc p) m -> p kc m", p=128),
                in_=dstmy.rearrange("p (kc m) -> p kc m", kc=NCH))),
                wait=[("ve", p.now("ve")), ("pw", p.now("pw")), ("cc", p.now("cc")), ("cd", p.now("cd"))], inc=[("pw", 16)])
            p.op("gpsimd", (lambda e, cin=cin, cout=cout: e.collective_compute(
                "AllGather", ALU.bypass, replica_groups=[list(range(NCORES))],
                ins=[cin.ap().opt()], outs=[cout.ap().opt()])),
                wait=[("pw", p.now("pw")), ("cc", p.now("cc")), ("cd", p.now("cd"))], inc=[("cc", 1)])
            pwg = p.now("pw")
            for r_ in range(NCORES):
                cv = cout.ap().rearrange("(r kc p) m -> r p kc m", r=NCORES, p=128)[r_]
                dv_ = dst.rearrange("p (kc rm) -> p kc rm", kc=NCH)[:, :, r_ * 128:(r_ + 1) * 128]
                p.op("gpsimd", (lambda e, cv=cv, dv_=dv_: e.dma_start(out=dv_, in_=cv)),
                     wait=([("pw", pwg), ("cc", p.now("cc")), ("cd", p.now("cd"))] if r_ == 0 else []), inc=[("pw", 16)])
        d *= 2
        if d in (16, 256):
            tgt = c.t[f"pw_p{d}"]
            p.op("gpsimd", (lambda e, tgt=tgt: e.dma_start(
                out=tgt.ap().rearrange("(kc p) m -> p kc m", p=128),
                in_=pwa.rearrange("p (kc m) -> p kc m", kc=NCH))),
                wait=[("pw", p.now("pw")), ("cc", p.now("cc")), ("cd", p.now("cd"))], inc=[("pw", 16)])
    p.wait("vector", ("pw", p.now("pw")), ("cc", p.now("cc")), ("cd", p.now("cd")))


def emit_a0(c, inv):
    p = c.p
    x_sb = c.t["x_sb"]
    xq = c.t["xq"]
    psA, psB = c.t["psA"], c.t["psB"]
    wbuf = c.t["wu_sb"]
    wbuf2 = c.t["wg_sb"]
    uB = c.t["uB_all"]
    s_all = c.t["xq"]
    sp_ = [c.t["s_ping"], c.t["s_pong"]]
    parsel = c.t["parsel_sb"]

    def xf(ch, j):
        return x_sb[:, ch * TC + j * 512: ch * TC + j * 512 + 512]

    def xqa(ch, j):
        return xq[:, ch * TC + j * 512: ch * TC + j * 512 + 512]

    for j in range(4):
        emit_ln(c, (lambda ch, j=j: xf(ch, j)), (lambda ch, j=j: xqa(ch, j)), 512)

    for nm, w in (("W_uT_d", wbuf), ("W_gT_d", wbuf2)):
        p.op("gpsimd", (lambda e, nm=nm, w=w: e.dma_start(
            out=w.rearrange("p (kc m) -> p kc m", kc=NCH),
            in_=c.t[nm].ap().rearrange("(kc p) m -> p kc m", p=128))),
            wait=[("pe", p.now("pe"))] if nm == "W_uT_d" else [], inc=[("wl", 16)])
    wl = p.now("wl")
    gel = c.t["gel_scr"]
    u_t = c.t["u_tile"]
    for j in range(4):
        for half, w in ((0, wbuf), (1, wbuf2)):
            for mc in range(NCH):
                bank = c.t[f"psO{mc}"]
                for kc in range(NCH):
                    p.op("tensor", (lambda e, w=w, mc=mc, kc=kc, j=j, bank=bank: e.matmul(
                        bank[:, 0:512],
                        w[:, kc * 1024 + mc * 128: kc * 1024 + mc * 128 + 128],
                        xqa(kc, j), start=(kc == 0), stop=(kc == NCH - 1))),
                        wait=([("wl", wl), ("ve", p.now("ve")), ("se", p.now("se"))]
                              if (mc == 0 and kc == 0) else []),
                        inc=([("pe", 1)] if kc == NCH - 1 else []))
            pe_mm = p.now("pe")
            for mc in range(NCH):
                bank = c.t[f"psO{mc}"]
                if half == 0:
                    p.op("scalar", (lambda e, mc=mc, bank=bank: e.activation(
                        gel[:, mc * 512:(mc + 1) * 512], bank[:, 0:512], AF.Gelu,
                        bias=c.t["b_u_sb"][:, mc:mc + 1])),
                        wait=([("pe", pe_mm), ("ve", p.now("ve"))] if mc == 0 else [("se", p.now("se"))]),
                        inc=[("se", 1)])
                else:
                    p.op("scalar", (lambda e, mc=mc, bank=bank: e.activation(
                        c.t["ev_scr"][:, 0:512], bank[:, 0:512], AF.Sigmoid,
                        bias=c.t["b_g_sb"][:, mc:mc + 1])),
                        wait=([("pe", pe_mm), ("ve", p.now("ve"))] if mc == 0 else [("se", p.now("se")), ("ve", p.now("ve"))]),
                        inc=[("se", 1)])
                    p.op("vector", (lambda e, mc=mc: e.tensor_mul(
                        u_t[:, mc * 512:(mc + 1) * 512],
                        gel[:, mc * 512:(mc + 1) * 512], c.t["ev_scr"][:, 0:512])),
                        wait=[("se", p.now("se"))], inc=[("ve", 1)])
        ve_u = p.now("ve")
        for mh in range(2):
            pe_b = p.now("pe")
            bmv = c.t["BmT_d"].ap().rearrange("(kc p) m -> p kc m", p=128)[:, :, mh * 512:(mh + 1) * 512]
            p.op("gpsimd", (lambda e, s=bmv: e.dma_start(
                out=c.t["wu2_sb"].rearrange("p (kc m) -> p kc m", kc=NCH), in_=s)),
                wait=[("pe", pe_b)], inc=[("wl", 16)])
            wlb = p.now("wl")
            for mc2 in range(4):
                mc = mh * 4 + mc2
                bank = c.t[f"psO{mc}"]
                for kc in range(NCH):
                    p.op("tensor", (lambda e, mc2=mc2, kc=kc, bank=bank: e.matmul(
                        bank[:, 0:512],
                        c.t["wu2_sb"][:, kc * 512 + mc2 * 128: kc * 512 + mc2 * 128 + 128],
                        u_t[:, kc * 512:(kc + 1) * 512],
                        start=(kc == 0), stop=(kc == NCH - 1))),
                        wait=([("ve", ve_u), ("wl", wlb), ("se", p.now("se"))] if (mc2 == 0 and kc == 0) else []),
                        inc=([("pe", 1)] if kc == NCH - 1 else []))
            pe_mm = p.now("pe")
            for mc2 in range(4):
                mc = mh * 4 + mc2
                bank = c.t[f"psO{mc}"]
                p.op("vector", (lambda e, mc=mc, j=j, bank=bank: e.tensor_copy(
                    uB[:, mc * TC + j * 512: mc * TC + j * 512 + 512], bank[:, 0:512])),
                    wait=([("pe", pe_mm)] if mc2 == 0 else [("ve", p.now("ve"))]), inc=[("ve", 1)])

    p.op("gpsimd", lambda e: e.dma_start(
        out=wbuf.rearrange("p (kc m) -> p kc m", kc=NCH),
        in_=c.t["A_T_d"].ap().rearrange("(kc p) m -> p kc m", p=128)),
        wait=[("pe", p.now("pe"))], inc=[("wl", 16)])
    wl = p.now("wl")

    def scan_steps(nsteps, rhs_init, uB_at, out_compact, out_strided=None, lhs=None, NW=128):
        lhs = lhs if lhs is not None else wbuf
        cur = rhs_init
        for i in range(nsteps):
            for mc in range(NCH):
                bank = c.t[f"psO{mc}"]
                for kc in range(NCH):
                    p.op("tensor", (lambda e, mc=mc, kc=kc, cur=cur, lhs=lhs: e.matmul(
                        bank[:, 0:NW] if False else c.t[f"psO{mc}"][:, 0:NW],
                        lhs[:, kc * 1024 + mc * 128: kc * 1024 + mc * 128 + 128],
                        cur[:, kc * NW:(kc + 1) * NW],
                        start=(kc == 0), stop=(kc == NCH - 1))),
                        wait=([("wl", p.now("wl")), ("ve", p.now("ve"))]
                              if (mc == 0 and kc == 0) else []),
                        inc=([("pe", 1)] if kc == NCH - 1 else []))
            pe_mm = p.now("pe")
            nxt = sp_[i % 2]
            for mc in range(NCH):
                p.op("vector", (lambda e, mc=mc, i=i, nxt=nxt: e.tensor_add(
                    nxt[:, mc * NW:(mc + 1) * NW], c.t[f"psO{mc}"][:, 0:NW], uB_at(i, mc))),
                    wait=([("pe", pe_mm)] if mc == 0 else [("ve", p.now("ve"))]), inc=[("ve", 1)])
                if out_strided is not None:
                    p.op("vector", (lambda e, mc=mc, i=i, nxt=nxt: e.tensor_copy(
                        out_strided(i, mc), nxt[:, mc * NW:(mc + 1) * NW])),
                        wait=[("ve", p.now("ve"))], inc=[("ve", 1)])
            cur = nxt
        if out_compact is not None:
            for mc in range(NCH):
                p.op("vector", (lambda e, mc=mc, cur=cur: e.tensor_copy(
                    out_compact[:, mc * NW:(mc + 1) * NW], cur[:, mc * NW:(mc + 1) * NW])),
                    wait=[("ve", p.now("ve"))], inc=[("ve", 1)])
        return cur

    uBv = uB.rearrange("p (ch c i) -> p ch c i", ch=NCH, i=LCH)
    sallv = s_all.rearrange("p (ch c i) -> p ch c i", ch=NCH, i=LCH)

    p.op("vector", lambda e: e.memset(sp_[1][:, :], 0.0),
         wait=[("ve", p.now("ve")), ("pw", p.now("pw")), ("cc", p.now("cc")), ("cd", p.now("cd")), ("pe", p.now("pe"))], inc=[("ve", 1)])
    fin = scan_steps(LCH, sp_[1], (lambda i, ch: uBv[:, ch, :, i]), None)

    p.op("gpsimd", (lambda e, fin=fin: e.dma_start(
        out=c.t["cc_sum_in"].ap().rearrange("(ch p) r -> p ch r", p=128),
        in_=fin.rearrange("p (ch r) -> p ch r", ch=NCH))),
        wait=[("ve", p.now("ve"))], inc=[("cd", 16)])
    p.op("gpsimd", lambda e: e.collective_compute(
        "AllGather", ALU.bypass, replica_groups=[[0, 1], [2, 3], [4, 5], [6, 7]],
        ins=[c.t["cc_sum_in"].ap().opt()], outs=[c.t["cc_sum_out"].ap().opt()]),
        wait=[("cc", p.now("cc")), ("cd", p.now("cd"))], inc=[("cc", 1)])
    for b_i in range(2):
        p.op("gpsimd", (lambda e, b_i=b_i: e.dma_start(
            out=c.t["Rbuf"].rearrange("p (ch b r) -> p ch b r", ch=NCH, b=2)[:, :, b_i],
            in_=c.t["cc_sum_out"].ap().rearrange("(b ch p) r -> p ch r", b=2, p=128)[:, b_i * NCH:(b_i + 1) * NCH]
            if False else c.t["cc_sum_out"].ap().rearrange("(b chp) r -> b chp r", b=2)[b_i].rearrange("(ch p) r -> p ch r", p=128))),
            wait=[("cc", p.now("cc")), ("cd", p.now("cd"))], inc=[("cd", 16)])
    p.wait("vector", ("cc", p.now("cc")), ("cd", p.now("cd")))

    p.op("gpsimd", lambda e: e.dma_start(
        out=wbuf2.rearrange("p (kc m) -> p kc m", kc=NCH),
        in_=c.t["pw_p16"].ap().rearrange("(kc p) m -> p kc m", p=128)),
        wait=[("pe", p.now("pe"))], inc=[("wl", 16)])
    Rv = c.t["Rbuf"].rearrange("p (ch g i) -> p ch g i", ch=NCH, i=LCH)
    p.op("vector", lambda e: e.memset(sp_[1][:, 0:NCH * 16], 0.0),
         wait=[("ve", p.now("ve")), ("cc", p.now("cc")), ("cd", p.now("cd"))], inc=[("ve", 1)])
    scan_steps(16, sp_[1], (lambda i, ch: Rv[:, ch, :, i]), c.t["G_sb"], lhs=wbuf2, NW=16)
    p.op("gpsimd", lambda e: e.dma_start(
        out=wbuf.rearrange("p (kc m) -> p kc m", kc=NCH),
        in_=c.t["pw_p256"].ap().rearrange("(kc p) m -> p kc m", p=128)),
        wait=[("pe", p.now("pe"))], inc=[("wl", 16)])
    wl3 = p.now("wl")
    Sg = c.t["Sg_sb"]
    p.op("vector", lambda e: e.memset(Sg[:, :], 0.0), wait=[("ve", p.now("ve"))], inc=[("ve", 1)])
    for g in range(1, 16):
        for mc in range(NCH):
            for kc in range(NCH):
                p.op("tensor", (lambda e, mc=mc, kc=kc, g=g: e.matmul(
                    c.t[f"psO{mc}"][:, 0:1],
                    wbuf[:, kc * 1024 + mc * 128: kc * 1024 + mc * 128 + 128],
                    Sg[:, kc * 16 + g - 1: kc * 16 + g],
                    start=(kc == 0), stop=(kc == NCH - 1))),
                    wait=([("wl", wl3), ("ve", p.now("ve"))] if (mc == 0 and kc == 0) else []),
                    inc=([("pe", 1)] if kc == NCH - 1 else []))
        pe_mm = p.now("pe")
        for mc in range(NCH):
            p.op("vector", (lambda e, mc=mc, g=g: e.tensor_add(
                Sg[:, mc * 16 + g: mc * 16 + g + 1], c.t[f"psO{mc}"][:, 0:1],
                c.t["G_sb"][:, mc * 16 + g - 1: mc * 16 + g])),
                wait=([("pe", pe_mm)] if mc == 0 else [("ve", p.now("ve"))]), inc=[("ve", 1)])
    Sallv = c.t["Sall_sb"].rearrange("p (ch g i) -> p ch g i", ch=NCH, i=LCH)
    for mc in range(NCH):
        p.op("vector", (lambda e, mc=mc: e.tensor_copy(
            Sallv[:, mc, :, 0], Sg[:, mc * 16:(mc + 1) * 16])),
            wait=[("ve", p.now("ve"))], inc=[("ve", 1)])
    cur = Sg
    for i in range(1, 16):
        for mc in range(NCH):
            for kc in range(NCH):
                p.op("tensor", (lambda e, mc=mc, kc=kc, cur=cur: e.matmul(
                    c.t[f"psO{mc}"][:, 0:16],
                    wbuf2[:, kc * 1024 + mc * 128: kc * 1024 + mc * 128 + 128],
                    cur[:, kc * 16:(kc + 1) * 16],
                    start=(kc == 0), stop=(kc == NCH - 1))),
                    wait=([("ve", p.now("ve"))] if (mc == 0 and kc == 0) else []),
                    inc=([("pe", 1)] if kc == NCH - 1 else []))
        pe_mm = p.now("pe")
        nxt = sp_[i % 2]
        for mc in range(NCH):
            p.op("vector", (lambda e, mc=mc, i=i, nxt=nxt: e.tensor_add(
                nxt[:, mc * 16:(mc + 1) * 16], c.t[f"psO{mc}"][:, 0:16],
                Rv[:, mc, :, i - 1])),
                wait=([("pe", pe_mm)] if mc == 0 else [("ve", p.now("ve"))]), inc=[("ve", 1)])
            p.op("vector", (lambda e, mc=mc, i=i, nxt=nxt: e.tensor_copy(
                Sallv[:, mc, :, i], nxt[:, mc * 16:(mc + 1) * 16])),
                wait=[("ve", p.now("ve"))], inc=[("ve", 1)])
        cur = nxt

    p.op("gpsimd", lambda e: e.dma_start(
        out=wbuf.rearrange("p (kc m) -> p kc m", kc=NCH),
        in_=c.t["A_T_d"].ap().rearrange("(kc p) m -> p kc m", p=128)),
        wait=[("pe", p.now("pe"))], inc=[("wl", 16)])
    Sall_f = c.t["Sall_sb"]
    for mc in range(NCH):
        p.op("vector", (lambda e, mc=mc: e.tensor_scalar(
            sp_[1][:, mc * 128:(mc + 1) * 128],
            Sall_f[:, mc * 256: mc * 256 + 128], parsel[:, 2:3], None, ALU.mult)),
            wait=[("ve", p.now("ve"))], inc=[("ve", 1)])
        p.op("vector", (lambda e, mc=mc: e.scalar_tensor_tensor(
            sp_[1][:, mc * 128:(mc + 1) * 128],
            Sall_f[:, mc * 256 + 128: mc * 256 + 256], parsel[:, 3:4],
            sp_[1][:, mc * 128:(mc + 1) * 128], ALU.mult, ALU.add)),
            wait=[("ve", p.now("ve"))], inc=[("ve", 1)])
    scan_steps(LCH, sp_[1], (lambda i, ch: uBv[:, ch, :, i]), None,
               out_strided=(lambda i, ch: sallv[:, ch, :, i]))

    p.op("gpsimd", lambda e: e.dma_start(
        out=wbuf.rearrange("p (kc m) -> p kc m", kc=NCH),
        in_=c.t["CmT_d"].ap().rearrange("(kc p) m -> p kc m", p=128)),
        wait=[("pe", p.now("pe"))], inc=[("wl", 16)])
    wl = p.now("wl")
    for j in range(4):
        for mc in range(NCH):
            bank = c.t[f"psO{mc}"]
            for kc in range(NCH):
                p.op("tensor", (lambda e, mc=mc, kc=kc, j=j, bank=bank: e.matmul(
                    bank[:, 0:512],
                    wbuf[:, kc * 1024 + mc * 128: kc * 1024 + mc * 128 + 128],
                    s_all[:, kc * TC + j * 512: kc * TC + j * 512 + 512],
                    start=(kc == 0), stop=(kc == NCH - 1))),
                    wait=([("wl", wl), ("ve", p.now("ve")), ("se", p.now("se"))]
                          if (mc == 0 and kc == 0) else []),
                    inc=([("pe", 1)] if kc == NCH - 1 else []))
        pe_mm = p.now("pe")
        for mc in range(NCH):
            bank = c.t[f"psO{mc}"]
            p.op("vector", (lambda e, mc=mc, bank=bank: e.tensor_scalar(
                c.t["ev_scr"][:, 0:512], bank[:, 0:512], c.t["D_sb"][:, mc:mc + 1], None, ALU.add)),
                wait=([("pe", pe_mm), ("ve", p.now("ve"))] if mc == 0 else [("ve", p.now("ve"))]),
                inc=[("ve", 1)])
            p.op("vector", (lambda e, mc=mc, j=j: e.tensor_add(
                xf(mc, j), xf(mc, j), c.t["ev_scr"][:, 0:512])),
                wait=[("ve", p.now("ve"))], inc=[("ve", 1)])


def a0_alloc(c):
    c.sbuf("wu_sb", [128, NCH * 1024], BF16)
    c.sbuf("wg_sb", [128, NCH * 1024], BF16)
    c.sbuf("wu2_sb", [128, NCH * 512], BF16)
    c.sbuf("u_tile", [128, NCH * 512], BF16)
    c.sbuf("uB_all", [128, NCH * TC], BF16)
    c.sbuf("gel_scr", [128, NCH * 512], BF16)
    c.t["s_ping"] = None
    c.t["s_pong"] = None
    c.t["Rbuf"] = None
    c.sbuf("G_sb", [128, NCH * 16], BF16)
    c.sbuf("Sg_sb", [128, NCH * 16], BF16)
    c.t["pw_a"] = c.t["wu_sb"]
    c.t["pw_b"] = c.t["wg_sb"]
    c.t["Rbuf"] = c.t["gel_scr"][:, 0:NCH * 256]
    c.t["Sall_sb"] = c.t["gel_scr"][:, NCH * 256:NCH * 512]
    c.t["s_ping"] = c.t["u_tile"][:, 0:NCH * 128]
    c.t["s_pong"] = c.t["u_tile"][:, NCH * 128:NCH * 256]
    c.t["myp_sb"] = c.t["s_ping"]
    c.t["myq_sb"] = c.t["s_pong"]
    c.dram("cc_pw_in", [D, 128], BF16)
    c.dram("cc_pw_out", [NCORES * D, 128], BF16)
    c.dram("cc_pw_out2", [NCORES * D, 128], BF16)
    c.dram("pw_p16", [D, D], BF16)
    c.dram("pw_p256", [D, D], BF16)
    c.dram("cc_sum_in", [D, 128], BF16)
    c.dram("cc_sum_out", [2 * D, 128], BF16)


def build_a0_standalone(debug=False):
    nc = bass.Bass("TRN2", target_bir_lowering=False)
    c = Ctx(nc)
    x_in = nc.dram_tensor("x_in", [D, TC], F32, kind="ExternalInput")
    for nm, shp in (("W_uT", [D, D]), ("W_gT", [D, D]), ("BmT", [D, D]), ("CmT", [D, D]),
                    ("A_T", [D, D]), ("A_nat", [D, D])):
        c.t[nm + "_d"] = nc.dram_tensor(nm, shp, F32, kind="ExternalInput")
    bu_in = nc.dram_tensor("b_u", [128, NCH], F32, kind="ExternalInput")
    bg_in = nc.dram_tensor("b_g", [128, NCH], F32, kind="ExternalInput")
    dv_in = nc.dram_tensor("Dv", [128, NCH], F32, kind="ExternalInput")
    amyp = nc.dram_tensor("Amy_p", [D, 128], F32, kind="ExternalInput")
    amyq = nc.dram_tensor("Amy_q", [D, 128], F32, kind="ExternalInput")
    parsel_in = nc.dram_tensor("parsel", [128, 8], F32, kind="ExternalInput")
    x_out = nc.dram_tensor("x_out", [D, TC], F32, kind="ExternalOutput")
    if debug:
        dbg_uB = nc.dram_tensor("dbg_uB", [D, TC], BF16, kind="ExternalOutput")
        dbg_sall = nc.dram_tensor("dbg_sall", [D, TC], BF16, kind="ExternalOutput")
        dbg_p16 = nc.dram_tensor("dbg_p16", [D, D], BF16, kind="ExternalOutput")
        dbg_sinit = nc.dram_tensor("dbg_sinit", [D, 256], BF16, kind="ExternalOutput")
    c.mk_consts()
    c.sbuf("x_sb", [128, NCH * TC], F32)
    c.sbuf("xq", [128, NCH * TC], BF16)
    c.sbuf("stat", [1, 2048], F32)
    c.sbuf("row_bf", [1, 1024], BF16)
    c.sbuf("sq_bf", [128, 512], BF16)
    c.sbuf("xbf_scr", [128, 512], BF16)
    c.sbuf("ln_scr", [128, 512], F32)
    c.sbuf("red_scr", [128, 1], F32)
    c.sbuf("srow", [1, 128], F32)
    for i in range(NCH):
        c.psum(f"psO{i}", [128, 512], F32)
    c.t["psA"] = c.t["psO0"]
    c.t["psB"] = c.t["psO1"]
    a0_alloc(c)
    p = c.p
    load_consts(c, ident=False)
    xv = x_in.ap().rearrange("(c p) t -> p c t", p=128)
    p.op("sync", lambda e: e.dma_start(
        out=c.t["x_sb"].rearrange("p (c t) -> p c t", c=NCH), in_=xv), inc=[("ld", 16)])
    p.op("sync", lambda e: e.dma_start(out=c.t["parsel_sb"][:, :], in_=parsel_in.ap()[:, :]),
         inc=[("ld", 16)])
    for nm, t_ in (("b_u_sb", bu_in), ("b_g_sb", bg_in), ("D_sb", dv_in)):
        p.op("sync", (lambda e, nm=nm, t_=t_: e.dma_start(
            out=c.t[nm][:, :], in_=t_.ap()[:, :])),
            inc=[("ld", 16)])
    p.op("gpsimd", lambda e: e.dma_start(
        out=c.t["myp_sb"].rearrange("p (kc m) -> p kc m", kc=NCH),
        in_=amyp.ap().rearrange("(kc p) m -> p kc m", p=128)), inc=[("pw", 16)])
    p.op("gpsimd", lambda e: e.dma_start(
        out=c.t["myq_sb"].rearrange("p (kc m) -> p kc m", kc=NCH),
        in_=amyq.ap().rearrange("(kc p) m -> p kc m", p=128)), inc=[("pw", 16)])
    for eng in ("vector", "tensor", "scalar", "gpsimd"):
        p.wait(eng, ("ld", p.now("ld")), ("pw", p.now("pw")), ("cc", p.now("cc")), ("cd", p.now("cd")))
    emit_a0_powers(c)
    emit_a0(c, 0)
    p.op("sync", lambda e: e.dma_start(
        out=x_out.ap().rearrange("(c p) t -> p c t", p=128),
        in_=c.t["x_sb"].rearrange("p (c t) -> p c t", c=NCH)),
        wait=[("ve", p.now("ve")), ("se", p.now("se"))], inc=[("st", 16)])
    if debug:
        p.op("sync", lambda e: e.dma_start(
            out=dbg_uB.ap().rearrange("(c p) t -> p c t", p=128),
            in_=c.t["uB_all"].rearrange("p (c t) -> p c t", c=NCH)), inc=[("st", 16)])
        p.op("sync", lambda e: e.dma_start(
            out=dbg_sall.ap().rearrange("(c p) t -> p c t", p=128),
            in_=c.t["xq"].rearrange("p (c t) -> p c t", c=NCH)), inc=[("st", 16)])
        p.op("gpsimd", lambda e: e.dma_start(
            out=dbg_p16.ap()[:, :], in_=c.t["pw_p16"].ap()[:, :]), inc=[("cd", 16)])
        p.op("sync", lambda e: e.dma_start(
            out=dbg_sinit.ap().rearrange("(c p) t -> p c t", p=128),
            in_=c.t["Sall_sb"].rearrange("p (c t) -> p c t", c=NCH)), inc=[("st", 16)])
        p.wait("sync", ("cd", p.now("cd")))
    p.wait("sync", ("st", p.now("st")))
    p.finalize()
    return nc


def run_a0(d, debug=False):
    nc = build_a0_standalone(debug)
    in_maps = []
    for core in range(NCORES):
        in_maps.append({
            "x_in": x_slice(d, core),
            "W_uT": d["W_uT"], "W_gT": d["W_gT"], "BmT": d["BmT"], "CmT": d["CmT"],
            "A_T": d["A_T"], "A_nat": d["A_nat"],
            "b_u": d["b_u"], "b_g": d["b_g"], "Dv": d["Dv"],
            "Amy_p": np.ascontiguousarray(d["A_T"][:, core * 128:(core + 1) * 128]),
            "Amy_q": np.ascontiguousarray(d["A_nat"][:, core * 128:(core + 1) * 128]),
            "parsel": host_parsel(core),
        })
    res = run_bass_kernel_spmd(nc, in_maps, core_ids=list(range(NCORES)))
    return np.concatenate([res.results[cc]["x_out"] for cc in range(NCORES)], axis=1)






def fence(c):
    p = c.p
    sems = ("pe", "ve", "se", "wl", "wl2", "cc", "cd", "sp", "pw", "ld", "st",
            "fx0", "fx1", "fx2", "fx3", "fwa", "fwb", "spa", "spb", "foa", "fob")
    for eng in ENGS:
        p.wait(eng, *[(s, p.now(s)) for s in sems])


def build_fused(nstages=6, powers=True):
    nc = bass.Bass("TRN2", target_bir_lowering=False)
    c = Ctx(nc)
    x_in = nc.dram_tensor("x_in", [D, TC], F32, kind="ExternalInput")
    c.t["w_upT_d"] = nc.dram_tensor("w_upT", [D, IFF], F32, kind="ExternalInput")
    c.t["w_gateT_d"] = nc.dram_tensor("w_gateT", [D, IFF], F32, kind="ExternalInput")
    c.t["w_downT_d"] = nc.dram_tensor("w_downT", [IFF, D], F32, kind="ExternalInput")
    c.t["gate_wT_d"] = nc.dram_tensor("gate_wT", [D, 9], F32, kind="ExternalInput")
    c.t["dyn_gT_d"] = nc.dram_tensor("dyn_gT", [NEXP, D, IDYN], F32, kind="ExternalInput")
    c.t["dyn_uT_d"] = nc.dram_tensor("dyn_uT", [NEXP, D, IDYN], F32, kind="ExternalInput")
    c.t["dyn_dT_d"] = nc.dram_tensor("dyn_dT", [NEXP, IDYN, D], F32, kind="ExternalInput")
    c.t["sh_gT_d"] = nc.dram_tensor("sh_gT", [D, ISH], F32, kind="ExternalInput")
    c.t["sh_uT_d"] = nc.dram_tensor("sh_uT", [D, ISH], F32, kind="ExternalInput")
    c.t["sh_dT_d"] = nc.dram_tensor("sh_dT", [ISH, D], F32, kind="ExternalInput")
    c.t["wqvk_d"] = nc.dram_tensor("wqvk", [D, 3072], F32, kind="ExternalInput")
    c.t["wo_d"] = nc.dram_tensor("wo", [D, D], F32, kind="ExternalInput")
    for nm, shp in (("W_uT", [D, D]), ("W_gT", [D, D]), ("BmT", [D, D]), ("CmT", [D, D]),
                    ("A_T", [D, D]), ("A_nat", [D, D])):
        c.t[nm + "_d"] = nc.dram_tensor(nm, shp, F32, kind="ExternalInput")
    bu_in = nc.dram_tensor("b_u", [128, NCH], F32, kind="ExternalInput")
    bg_in = nc.dram_tensor("b_g", [128, NCH], F32, kind="ExternalInput")
    dv_in = nc.dram_tensor("Dv", [128, NCH], F32, kind="ExternalInput")
    amyp = nc.dram_tensor("Amy_p", [D, 128], F32, kind="ExternalInput")
    amyq = nc.dram_tensor("Amy_q", [D, 128], F32, kind="ExternalInput")
    parsel_in = nc.dram_tensor("parsel", [128, 8], F32, kind="ExternalInput")
    prevrow_in = nc.dram_tensor("prevrow", [1, 8], F32, kind="ExternalInput")
    x_out = nc.dram_tensor("x_out", [D, TC], F32, kind="ExternalOutput")

    c.mk_consts()
    c.sbuf("x_sb", [128, NCH * TC], F32)
    c.sbuf("xq", [128, NCH * TC], BF16)
    c.sbuf("stat", [1, 2048], F32)
    c.sbuf("row_bf", [1, 1024], BF16)
    c.sbuf("sq_bf", [128, 512], BF16)
    c.sbuf("xbf_scr", [128, 512], BF16)
    c.sbuf("ln_scr", [128, 512], F32)
    c.sbuf("red_scr", [128, 1], F32)
    c.sbuf("srow", [1, 128], F32)
    c.sbuf("grow", [1, 64], F32)
    c.sbuf("s128", [128, 8], F32)
    c.sbuf("am128", [128, 1], F32)
    c.sbuf("am2", [128, 1], F32)
    c.sbuf("q_scr", [128, 512], F32)
    c.sbuf("ev_scr", [128, 512], F32)
    c.sbuf("b_u_sb", [128, NCH], F32)
    c.sbuf("b_g_sb", [128, NCH], F32)
    c.sbuf("D_sb", [128, NCH], F32)
    c.sbuf("parsel_sb", [128, 8], F32)
    c.sbuf("prevrow_sb", [1, 8], F32)
    idn = c.sbuf("ident_sb", [128, 128], F32)
    oc = c.sbuf("ones_col_sb", [128, 1], BF16)
    orow = c.sbuf("ones_row_sb", [1, 128], BF16)
    a1_psum_alloc(c)

    rem = nc.sbuf_bytes_remaining
    arena_cols = (rem - 512) // 4
    arena = nc.alloc_sbuf_tensor("arena", [128, arena_cols], F32)
    c.arena = (arena[:, :], 0)
    ns = {}
    for name, fn in (("f0", f0_alloc), ("f1", f1_alloc), ("a0", a0_alloc), ("a1", a1_alloc)):
        c.arena_reset()
        c.begin_ns()
        fn(c)
        ns[name] = c.end_ns()
    c.arena = None

    p = c.p
    p.op("sync", lambda e: e.dma_start(out=oc[:, :], in_=c.const["ones_col"].ap()[:, :]), inc=[("ld", 16)])
    p.op("sync", lambda e: e.dma_start(out=orow[:, :], in_=c.const["ones_row"].ap()[:, :]), inc=[("ld", 16)])
    p.op("sync", lambda e: e.dma_start(out=idn[:, :], in_=c.const["ident"].ap()[:, :]), inc=[("ld", 16)])
    xv = x_in.ap().rearrange("(ch p) t -> p ch t", p=128)
    p.op("sync", lambda e: e.dma_start(
        out=c.t["x_sb"].rearrange("p (ch t) -> p ch t", ch=NCH), in_=xv), inc=[("ld", 16)])
    p.op("sync", lambda e: e.dma_start(out=c.t["parsel_sb"][:, :], in_=parsel_in.ap()[:, :]), inc=[("ld", 16)])
    p.op("sync", lambda e: e.dma_start(out=c.t["prevrow_sb"][0:1, :], in_=prevrow_in.ap()[:, :]), inc=[("ld", 16)])
    for nm, t_ in (("b_u_sb", bu_in), ("b_g_sb", bg_in), ("D_sb", dv_in)):
        p.op("sync", (lambda e, nm=nm, t_=t_: e.dma_start(out=c.t[nm][:, :], in_=t_.ap()[:, :])), inc=[("ld", 16)])
    c.t.update(ns["a0"])
    p.op("gpsimd", lambda e: e.dma_start(
        out=c.t["myp_sb"].rearrange("p (kc m) -> p kc m", kc=NCH),
        in_=amyp.ap().rearrange("(kc p) m -> p kc m", p=128)), inc=[("pw", 16)])
    p.op("gpsimd", lambda e: e.dma_start(
        out=c.t["myq_sb"].rearrange("p (kc m) -> p kc m", kc=NCH),
        in_=amyq.ap().rearrange("(kc p) m -> p kc m", p=128)), inc=[("pw", 16)])
    for eng in ENGS:
        p.wait(eng, ("ld", p.now("ld")), ("pw", p.now("pw")))

    if powers:
        emit_a0_powers(c)
        fence(c)
    c.t.update(ns["f0"])
    emit_f0_preconv(c)
    def st_f0a():
        c.t.update(ns["f0"])
        p.op("vector", lambda e: e.memset(c.t["ones8"][0:8, :], 1.0), wait=[("ve", p.now("ve"))], inc=[("ve", 1)])
        emit_f0(c, 0)
    def st_a0():
        c.t.update(ns["f1"])
        emit_f1_preconv(c)
        c.t.update(ns["a0"])
        emit_a0(c, 0)
    def st_f0b():
        c.t.update(ns["f0"])
        p.op("vector", lambda e: e.memset(c.t["ones8"][0:8, :], 1.0), wait=[("ve", p.now("ve"))], inc=[("ve", 1)])
        emit_f0(c, 1)
    def st_f1a():
        c.t.update(ns["f1"])
        emit_f1(c, 0)
    def st_a1():
        c.t.update(ns["a1"])
        emit_a1(c, 0)
    def st_f1b():
        c.t.update(ns["f1"])
        emit_f1(c, 1)
    stages = [st_f0a, st_a0, st_f0b, st_f1a, st_a1, st_f1b]
    for i in range(nstages):
        stages[i]()
        if i < nstages - 1:
            fence(c)
    p.op("sync", lambda e: e.dma_start(
        out=x_out.ap().rearrange("(ch p) t -> p ch t", p=128),
        in_=c.t["x_sb"].rearrange("p (ch t) -> p ch t", ch=NCH)),
        wait=[("ve", p.now("ve")), ("se", p.now("se"))], inc=[("st", 16)])
    p.wait("sync", ("st", p.now("st")))
    p.finalize()
    return nc


def kernel(**inputs):
    d = host_prepare(inputs)
    nc = build_fused()
    in_maps = []
    for core in range(NCORES):
        prev = np.zeros((1, 8), np.float32)
        prev[0, :core] = 1.0
        in_maps.append({
            "x_in": np.ascontiguousarray(d["x_t"][:, core * TC:(core + 1) * TC]),
            "w_upT": d["w_upT"], "w_gateT": d["w_gateT"], "w_downT": d["w_downT"],
            "gate_wT": d["gate_wT"],
            "dyn_gT": d["dyn_gT"], "dyn_uT": d["dyn_uT"], "dyn_dT": d["dyn_dT"],
            "sh_gT": d["sh_gT"], "sh_uT": d["sh_uT"], "sh_dT": d["sh_dT"],
            "wqvk": host_wqvk_perm(d["Wqvk"], core), "wo": d["Wo"],
            "W_uT": d["W_uT"], "W_gT": d["W_gT"], "BmT": d["BmT"], "CmT": d["CmT"],
            "A_T": d["A_T"], "A_nat": d["A_nat"],
            "b_u": d["b_u"], "b_g": d["b_g"], "Dv": d["Dv"],
            "Amy_p": np.ascontiguousarray(d["A_T"][:, core * 128:(core + 1) * 128]),
            "Amy_q": np.ascontiguousarray(d["A_nat"][:, core * 128:(core + 1) * 128]),
            "parsel": host_parsel(core),
            "prevrow": prev,
        })
    res = run_bass_kernel_spmd(nc, in_maps, core_ids=list(range(NCORES)))
    x_t = np.concatenate([res.results[cc]["x_out"] for cc in range(NCORES)], axis=1)
    return np.ascontiguousarray(x_t.T).reshape(4, SEQ, D)


# version 09
# speedup vs baseline: 1.0158x; 1.0158x over previous
import numpy as np
import ml_dtypes
import concourse.bass as bass
import concourse.mybir as mybir
from concourse.bass_isa import ReduceOp
from concourse.bass_utils import run_bass_kernel_spmd

F32 = mybir.dt.float32
BF16 = mybir.dt.bfloat16
AF = mybir.ActivationFunctionType
ALU = mybir.AluOpType
AXX = mybir.AxisListType.X

NCORES = 8
D = 1024
TTOT = 16384
TC = TTOT // NCORES
SEQ = 4096
NCH = D // 128
MAGIC = 12582912.0
QMAX = 32767.0
EPS_LN = 1e-5

ENGS = ("sync", "tensor", "vector", "scalar", "gpsimd")


class Prog:

    def __init__(self, nc):
        self.nc = nc
        self.streams = {e: [] for e in ENGS}
        self.sems = {}

    def _sem(self, name):
        if name not in self.sems:
            self.sems[name] = [self.nc.alloc_semaphore(name), 0]
        return self.sems[name]

    def now(self, name):
        return self._sem(name)[1]

    def op(self, eng, build, wait=(), inc=()):
        w = []
        for n, v in wait:
            s = self._sem(n)
            assert v <= s[1], f"wait {n}>={v} but only {s[1]} emitted (deadlock)"
            if v > 0:
                w.append((s[0], v))
        ii = []
        out_counts = []
        for n, k in inc:
            s = self._sem(n)
            s[1] += k
            ii.append((s[0], k))
            out_counts.append(s[1])

        def fn(e, _w=w, _ii=ii, _b=build):
            for h, v in _w:
                e.wait_ge(h, v)
            ins = _b(e)
            for h, k in _ii:
                assert ins is not None, "build returned None but incs requested"
                ins = ins.then_inc(h, k)

        self.streams[eng].append(fn)
        return out_counts

    def wait(self, eng, *pairs):
        self.op(eng, lambda e: None, wait=pairs)

    def finalize(self):
        nc = self.nc
        with nc.Block() as block:
            for eng in ENGS:
                fns = self.streams[eng]
                if not fns:
                    continue

                def runner(e, _fns=fns):
                    for f in _fns:
                        f(e)

                getattr(block, eng)(runner)



def _t(w):
    return np.ascontiguousarray(np.asarray(w).T)


def host_prepare(inputs):
    d = {}
    x = np.asarray(inputs["L0_attn_x"])
    d["x_t"] = np.ascontiguousarray(x.reshape(TTOT, D).T)
    d["w_upT"] = _t(inputs["L1_ffn_w_up"])
    d["w_gateT"] = _t(inputs["L1_ffn_w_gate"])
    d["w_downT"] = _t(inputs["L1_ffn_w_down"])
    d["Wqvk"] = np.asarray(inputs["L1_attn_Wqvk"])
    d["Wo"] = np.asarray(inputs["L1_attn_Wo"])
    d["W_uT"] = _t(inputs["L0_attn_W_u"])
    d["W_gT"] = _t(inputs["L0_attn_W_g"])
    d["BmT"] = _t(inputs["L0_attn_Bm"])
    d["CmT"] = _t(inputs["L0_attn_Cm"])
    d["A_nat"] = np.ascontiguousarray(np.asarray(inputs["L0_attn_A"], dtype=np.float32))
    d["A_T"] = _t(inputs["L0_attn_A"])
    d["b_u"] = np.ascontiguousarray(np.asarray(inputs["L0_attn_b_u"]).reshape(NCH, 128).T)
    d["b_g"] = np.ascontiguousarray(np.asarray(inputs["L0_attn_b_g"]).reshape(NCH, 128).T)
    d["Dv"] = np.ascontiguousarray(np.asarray(inputs["L0_attn_D"]).reshape(NCH, 128).T)
    d["gate_wT"] = _t(inputs["L0_ffn_gate_w"])
    d["dyn_gT"] = np.ascontiguousarray(np.asarray(inputs["L0_ffn_dyn_gate_w"]).transpose(0, 2, 1))
    d["dyn_uT"] = np.ascontiguousarray(np.asarray(inputs["L0_ffn_dyn_up_w"]).transpose(0, 2, 1))
    d["dyn_dT"] = np.ascontiguousarray(np.asarray(inputs["L0_ffn_dyn_down_w"]).transpose(0, 2, 1))
    d["sh_gT"] = np.ascontiguousarray(np.asarray(inputs["L0_ffn_sh_gate_w"])[0].T)
    d["sh_uT"] = np.ascontiguousarray(np.asarray(inputs["L0_ffn_sh_up_w"])[0].T)
    d["sh_dT"] = np.ascontiguousarray(np.asarray(inputs["L0_ffn_sh_down_w"])[0].T)
    return d


def x_slice(d, core):
    return np.ascontiguousarray(d["x_t"][:, core * TC:(core + 1) * TC])


class Ctx:
    def __init__(self, nc):
        self.nc = nc
        self.p = Prog(nc)
        self.t = {}
        self.const = {}
        self.arena = None
        self._ns_mark = None

    def arena_reset(self):
        self.arena = (self.arena[0], 0)

    def begin_ns(self):
        self._ns_mark = dict(self.t)

    def end_ns(self):
        old = self._ns_mark
        self._ns_mark = None
        return {k: v for k, v in self.t.items() if old.get(k) is not v}

    def sbuf(self, name, shape, dtype):
        if self.arena is not None:
            ap, off = self.arena
            nbytes = shape[1] * (2 if dtype == BF16 else 4)
            off_al = (off + 63) // 64 * 64
            assert off_al + nbytes <= ap.shape[1] * 4, \
                f"arena overflow at {name}: need {nbytes} at {off_al}"
            base = ap[:, off_al // 4:(off_al + nbytes + 3) // 4]
            view = base.bitcast(BF16) if dtype == BF16 else base
            view = view[0:shape[0], 0:shape[1]]
            self.arena = (ap, off_al + nbytes)
            self.t[name] = view
            return view
        h = self.nc.alloc_sbuf_tensor(name, shape, dtype)
        self.t[name] = h
        return h

    def psum(self, name, shape, dtype=F32):
        h = self.nc.alloc_psum_tensor(name, shape, dtype)
        self.t[name] = h
        return h

    def dram(self, name, shape, dtype, **kw):
        h = self.nc.dram_tensor(name, shape, dtype, kind="Internal", **kw)
        self.t[name] = h
        return h

    def inline(self, name, arr):
        h = self.nc.inline_tensor(np.ascontiguousarray(arr), name=name)
        self.const[name] = h
        return h

    def mk_consts(self):
        self.inline("ones_col", np.ones((128, 1), dtype=ml_dtypes.bfloat16))
        self.inline("ones_row", np.ones((1, 128), dtype=ml_dtypes.bfloat16))
        self.inline("ident", np.eye(128, dtype=np.float32))


def load_consts(c, ident=True):
    p = c.p
    oc = c.sbuf("ones_col_sb", [128, 1], BF16)
    orow = c.sbuf("ones_row_sb", [1, 128], BF16)
    p.op("sync", lambda e, t=oc: e.dma_start(out=t[:, :], in_=c.const["ones_col"].ap()[:, :]),
         inc=[("ld", 16)])
    p.op("sync", lambda e, t=orow: e.dma_start(out=t[:, :], in_=c.const["ones_row"].ap()[:, :]),
         inc=[("ld", 16)])
    if ident:
        idn = c.sbuf("ident_sb", [128, 128], F32)
        p.op("sync", lambda e, t=idn: e.dma_start(out=t[:, :], in_=c.const["ident"].ap()[:, :]),
             inc=[("ld", 16)])



def emit_ln(c, xf, out_bf, Tt, absmax_to=None):
    p = c.p
    oc = c.t["ones_col_sb"]
    orow = c.t["ones_row_sb"]
    stat = c.t["stat"]
    row_bf = c.t["row_bf"]
    sq = c.t["sq_bf"]
    psA, psB = c.t["psA"], c.t["psB"]

    xt = c.t["xbf_scr"]
    for ch in range(NCH):
        cast_w = [("pe", p.now("pe")), ("ve", p.now("ve"))] if True else []
        v = p.op("scalar", (lambda e, a=xf(ch), t=xt: e.activation(t[:, 0:Tt], a, AF.Copy)),
                 wait=cast_w, inc=[("se", 1)])[0]
        v2 = p.op("scalar", (lambda e, t=xt, s=sq: e.activation(s[:, 0:Tt], t[:, 0:Tt], AF.Square)),
                  wait=[("se", v)], inc=[("se", 1)])[0]
        p.op("tensor", (lambda e, t=xt, ch=ch: e.matmul(
            psA[0:1, 0:Tt], oc[:, :], t[:, 0:Tt], start=(ch == 0), stop=(ch == NCH - 1))),
            wait=[("se", v)], inc=[])
        p.op("tensor", (lambda e, s=sq, ch=ch: e.matmul(
            psB[0:1, 0:Tt], oc[:, :], s[:, 0:Tt], start=(ch == 0), stop=(ch == NCH - 1))),
            wait=[("se", v2)], inc=[("pe", 1)])
    pe_done = p.now("pe")

    p.op("vector", lambda e: e.tensor_scalar(stat[0:1, 0:Tt], psA[0:1, 0:Tt], 1.0 / D, None, ALU.mult),
         wait=[("pe", pe_done)], inc=[("ve", 1)])
    p.op("vector", lambda e: e.tensor_scalar(stat[0:1, 512:512 + Tt], psB[0:1, 0:Tt], 1.0 / D, None, ALU.mult),
         inc=[("ve", 1)])
    p.op("vector", lambda e: e.tensor_mul(stat[0:1, 1024:1024 + Tt], stat[0:1, 0:Tt], stat[0:1, 0:Tt]),
         wait=[("ve", p.now("ve"))], inc=[("ve", 1)])
    p.op("vector", lambda e: e.tensor_sub(stat[0:1, 512:512 + Tt], stat[0:1, 512:512 + Tt], stat[0:1, 1024:1024 + Tt]),
         wait=[("ve", p.now("ve"))], inc=[("ve", 1)])
    p.op("vector", lambda e: e.tensor_scalar_add(stat[0:1, 512:512 + Tt], stat[0:1, 512:512 + Tt], EPS_LN),
         wait=[("ve", p.now("ve"))], inc=[("ve", 1)])
    sev = p.op("scalar", lambda e: e.activation(stat[0:1, 1024:1024 + Tt], stat[0:1, 512:512 + Tt], AF.Sqrt),
               wait=[("ve", p.now("ve"))], inc=[("se", 1)])[0]
    p.op("vector", lambda e: e.reciprocal(stat[0:1, 1024:1024 + Tt], stat[0:1, 1024:1024 + Tt]),
         wait=[("se", sev)], inc=[("ve", 1)])
    p.op("vector", lambda e: e.tensor_mul(stat[0:1, 1536:1536 + Tt], stat[0:1, 0:Tt], stat[0:1, 1024:1024 + Tt]),
         wait=[("ve", p.now("ve"))], inc=[("ve", 1)])
    p.op("vector", lambda e: e.tensor_copy(row_bf[0:1, 0:Tt], stat[0:1, 1024:1024 + Tt]),
         wait=[("ve", p.now("ve"))], inc=[("ve", 1)])
    p.op("vector", lambda e: e.tensor_copy(row_bf[0:1, 512:512 + Tt], stat[0:1, 1536:1536 + Tt]),
         wait=[("ve", p.now("ve"))], inc=[("ve", 1)])
    ve_rows = p.now("ve")

    p.op("tensor", lambda e: e.matmul(psA[:, 0:Tt], orow[:, :], row_bf[0:1, 0:Tt], start=True, stop=True),
         wait=[("ve", ve_rows)], inc=[("pe", 1)])
    bc = p.op("tensor", lambda e: e.matmul(psB[:, 0:Tt], orow[:, :], row_bf[0:1, 512:512 + Tt], start=True, stop=True),
              inc=[("pe", 1)])[0]

    scr = c.t["ln_scr"]
    red = c.t["red_scr"]
    for ch in range(NCH):
        p.op("vector", lambda e, a=xf(ch): e.tensor_mul(scr[:, 0:Tt], a, psA[:, 0:Tt]),
             wait=([("pe", bc)] if ch == 0 else [("ve", p.now("ve"))]), inc=[("ve", 1)])
        p.op("vector", lambda e, o=out_bf(ch): e.tensor_sub(o, scr[:, 0:Tt], psB[:, 0:Tt]),
             wait=[("ve", p.now("ve"))], inc=[("ve", 1)])
        if absmax_to is not None:
            p.op("vector", lambda e, o=out_bf(ch): e.reduce_max(
                red[:, 0:1], o, axis=AXX, apply_absolute_value=True),
                wait=[("ve", p.now("ve"))], inc=[("ve", 1)])
            p.op("vector", lambda e: e.tensor_max(absmax_to[:, 0:1], absmax_to[:, 0:1], red[:, 0:1]),
                 wait=[("ve", p.now("ve"))], inc=[("ve", 1)])


def emit_absmax_to_scalar(c, am128, slot):
    p = c.p
    psB = c.t["psB"]
    v = p.op("tensor", lambda e: e.transpose(psB[0:1, 0:128], am128[:, 0:1], c.t["ident_sb"][:, :]),
             wait=[("ve", p.now("ve")), ("se", p.now("se"))], inc=[("pe", 1)])[0]
    p.op("vector", lambda e: e.reduce_max(
        c.t["srow"][0:1, slot:slot + 1], psB[0:1, 0:128], axis=AXX),
        wait=[("pe", v)], inc=[("ve", 1)])


def gather_scales(c, nvals):
    p = c.p
    cin = c.t["cc_sc_in"]
    cout = c.t["cc_sc_out"]
    grow = c.t["grow"]
    p.op("gpsimd", lambda e: e.dma_start(
        out=cin.ap()[0:nvals, :].rearrange("n one -> one n"), in_=c.t["srow"][0:1, 0:nvals]),
        wait=[("po", p.now("po")), ("ve", p.now("ve"))], inc=[("cd", 16)])
    p.op("gpsimd", lambda e: e.collective_compute(
        "AllGather", ALU.bypass, replica_groups=[list(range(NCORES))],
        ins=[cin.ap()[0:nvals, :].opt()], outs=[cout.ap()[0:NCORES * nvals, :].opt()]),
        wait=[("cc", p.now("cc")), ("cd", p.now("cd"))], inc=[("cc", 1)])
    p.op("gpsimd", lambda e: e.dma_start(
        out=grow[0:1, 0:NCORES * nvals],
        in_=cout.ap()[0:NCORES * nvals, :].rearrange("n one -> one n")),
        wait=[("cc", p.now("cc")), ("cd", p.now("cd"))], inc=[("cd", 16)])
    ccv = p.now("cc")
    cdv = p.now("cd")
    for s in range(nvals):
        p.op("vector", lambda e, s=s: e.reduce_max(
            c.t["srow"][0:1, s:s + 1],
            grow[0:1, 0:NCORES * nvals].rearrange("one (r n) -> one n r", n=nvals)[:, s],
            axis=AXX),
            wait=([("cc", ccv), ("cd", cdv), ("ve", p.now("ve"))] if s == 0 else [("ve", p.now("ve"))]),
            inc=[("ve", 1)])


def finish_scale(c, slot, dst_col, extra_mult=None):
    p = c.p
    srow = c.t["srow"]
    s128 = c.t["s128"]
    p.op("vector", lambda e: e.tensor_scalar(
        srow[0:1, 32:33], srow[0:1, slot:slot + 1], 1.0 / QMAX, 1e-12, ALU.mult, ALU.max),
        wait=[("ve", p.now("ve")), ("po", p.now("po"))], inc=[("ve", 1)])
    p.op("vector", lambda e: e.reciprocal(srow[0:1, 33:34], srow[0:1, 32:33]),
         wait=[("ve", p.now("ve"))], inc=[("ve", 1)])
    if extra_mult is not None:
        p.op("vector", lambda e: e.tensor_scalar(
            srow[0:1, 32:33], srow[0:1, 32:33], extra_mult, None, ALU.mult),
            wait=[("ve", p.now("ve"))], inc=[("ve", 1)])
    p.op("gpsimd", lambda e: e.dma_start(out=c.t["bc_d"].ap()[:, :], in_=srow[0:1, 32:34]),
         wait=[("ve", p.now("ve"))], inc=[("po", 16)])
    p.op("gpsimd", lambda e: e.dma_start(
        out=s128[:, dst_col:dst_col + 2],
        in_=c.t["bc_d"].ap().partition_broadcast(128)[:, 0, :]),
        wait=[("po", p.now("po"))], inc=[("po", 16)])



IFF = 3072
IGRP = 512
NIG = IFF // IGRP
TT1 = 512
NTT = TC // TT1


def qdq_pair(c, out_ap, in_ap, inv_col, s_col, wait0=()):
    p = c.p
    s128 = c.t["s128"]
    q = c.t["q_scr"]
    Tt = 512
    p.op("vector", lambda e: e.tensor_scalar(
        q[:, 0:Tt], in_ap, s128[:, inv_col:inv_col + 1], MAGIC, ALU.mult, ALU.add),
        wait=list(wait0) + [("ve", p.now("ve"))], inc=[("ve", 1)])
    p.op("vector", lambda e: e.tensor_scalar(
        out_ap, q[:, 0:Tt], MAGIC, s128[:, s_col:s_col + 1], ALU.subtract, ALU.mult),
        wait=[("ve", p.now("ve"))], inc=[("ve", 1)])


def emit_f1_preconv(c):
    p = c.p
    for src, dst in (("w_upT_d", "up_bf"), ("w_gateT_d", "gt_bf"),
                     ("w_downT_d", "wd_bf")):
        sap = c.t[src].ap()
        dap = c.t[dst].ap()
        n = sap.shape[0]
        for h in range(2):
            p.op("gpsimd", (lambda e, s=sap, d=dap, h=h, n=n: e.dma_start(
                out=d[h * (n // 2):(h + 1) * (n // 2), :],
                in_=s[h * (n // 2):(h + 1) * (n // 2), :])), inc=[("cv", 16)])


def emit_f1(c, inv):
    p = c.p
    x_sb = c.t["x_sb"]
    xq = c.t["xq"]
    s128 = c.t["s128"]
    am, am2 = c.t["am128"], c.t["am2"]
    psD = [c.t[f"psO{i + 2}"] for i in range(4)]
    redall = c.t["redall"]
    mgc = c.t["mgc"]

    def xf(ch, j):
        return x_sb[:, ch * TC + j * TT1: ch * TC + j * TT1 + TT1]

    def xqa(ch, j):
        return xq[:, ch * TC + j * TT1: ch * TC + j * TT1 + TT1]

    p.op("vector", lambda e: e.memset(am[:, 0:1], 0.0), wait=[("po", p.now("po"))], inc=[("ve", 1)])
    p.op("vector", lambda e: e.memset(mgc[:, 0:1], MAGIC), wait=[("ve", p.now("ve"))], inc=[("ve", 1)])
    p.op("vector", lambda e: e.memset(mgc[:, 1:2], -MAGIC), wait=[("ve", p.now("ve"))], inc=[("ve", 1)])
    for j in range(NTT):
        emit_ln(c, (lambda ch, j=j: xf(ch, j)),
                (lambda ch, j=j: xqa(ch, j)), TT1, absmax_to=am)
    emit_absmax_to_scalar(c, am, 0)
    gather_scales(c, 1)
    finish_scale(c, 0, 0)
    for j in range(NTT):
        for ch in range(NCH):
            qdq_pair(c, xqa(ch, j), xqa(ch, j), 1, 0,
                     wait0=[("po", p.now("po"))])
    ve_x = p.now("ve")

    wab = [(c.t["wa0"], c.t["wb0"]), (c.t["wa1"], c.t["wb1"])]
    ev4p = [c.t["ev4a"], c.t["ev4b"]]
    upv = c.t["up_bf"].ap().rearrange("(kc p) m -> p kc m", p=128)
    gtv = c.t["gt_bf"].ap().rearrange("(kc p) m -> p kc m", p=128)
    cv_tot = p.now("cv")
    fw = ("fwa", "fwb")
    sp_sem = ("spa", "spb")

    def ld_grp(g, extra=()):
        par = g % 2
        wa_, wb_ = wab[par]
        for half, (wsb, wv) in enumerate(((wa_, upv), (wb_, gtv))):
            p.op("sync", (lambda e, wsb=wsb, wv=wv, g=g: e.dma_start(
                out=wsb.rearrange("p (kc m) -> p kc m", m=IGRP),
                in_=wv[:, :, g * IGRP:(g + 1) * IGRP])),
                wait=(list(extra) + [("cv", cv_tot)]) if half == 0 else [],
                inc=[(fw[par], 16)])

    ld_grp(0)
    ld_grp(1)
    bank_se = [0] * 4
    wave_ve = {}
    widx = 0
    for g in range(NIG):
        par = g % 2
        wa_, wb_ = wab[par]
        fw_tot = p.now(fw[par])
        first_mm_of_grp = True
        for j in range(NTT):
            for half, (w, amcol0) in enumerate(((wa_, 0), (wb_, 96))):
                ev4 = ev4p[widx % 2]
                pe_bank = []
                for mc in range(4):
                    for kc in range(NCH):
                        wl = []
                        if kc == 0:
                            wl.append(("se", bank_se[mc]))
                            if first_mm_of_grp:
                                wl += [(fw[par], fw_tot), ("ve", ve_x)]
                                first_mm_of_grp = False
                        r = p.op("tensor", (lambda e, w=w, kc=kc, mc=mc, j=j: e.matmul(
                            psD[mc][:, 0:TT1],
                            w[:, kc * IGRP + mc * 128: kc * IGRP + mc * 128 + 128],
                            xqa(kc, j), start=(kc == 0), stop=(kc == NCH - 1))),
                            wait=wl, inc=([("pe", 1)] if kc == NCH - 1 else []))
                        if kc == NCH - 1:
                            pe_bank.append(r[0])
                for mc in range(4):
                    wl = [("pe", pe_bank[mc]), ("se", p.now("se"))]
                    if mc == 0 and widx >= 2:
                        ssm = sp_sem[widx % 2]
                        wl += [(ssm, p.now(ssm)), ("ve", wave_ve[widx - 2])]
                    p.op("scalar", (lambda e, mc=mc, ev4=ev4: e.activation(
                        ev4[:, mc * TT1:(mc + 1) * TT1], psD[mc][:, 0:TT1], AF.Copy)),
                        wait=wl, inc=[("se", 1)])
                    bank_se[mc] = p.now("se")
                for mc in range(4):
                    cc_ = amcol0 + (g * NTT + j) * 4 + mc
                    p.op("vector", (lambda e, mc=mc, cc_=cc_, ev4=ev4: e.reduce_max(
                        redall[:, cc_:cc_ + 1], ev4[:, mc * TT1:(mc + 1) * TT1],
                        axis=AXX, apply_absolute_value=True)),
                        wait=[("se", bank_se[mc]), ("ve", p.now("ve"))], inc=[("ve", 1)])
                wave_ve[widx] = p.now("ve")
                spname = "sp_up" if half == 0 else "sp_gate"
                sp_view = c.t[spname].ap().rearrange("(g mc p) t -> g p mc t", g=NIG, mc=4)[g]
                p.op("scalar", (lambda e, sv=sp_view, j=j, ev4=ev4: e.dma_start(
                    out=sv[:, :, j * TT1:(j + 1) * TT1],
                    in_=ev4.rearrange("p (mc t) -> p mc t", mc=4))),
                    wait=[("se", p.now("se"))], inc=[(sp_sem[widx % 2], 16)])
                widx += 1
        if g + 2 < NIG:
            ld_grp(g + 2, extra=[("pe", p.now("pe"))])
    p.op("vector", lambda e: e.reduce_max(am[:, 0:1], redall[:, 0:96], axis=AXX),
         wait=[("ve", p.now("ve"))], inc=[("ve", 1)])
    p.op("vector", lambda e: e.reduce_max(am2[:, 0:1], redall[:, 96:192], axis=AXX),
         wait=[("ve", p.now("ve"))], inc=[("ve", 1)])
    emit_absmax_to_scalar(c, am, 0)
    emit_absmax_to_scalar(c, am2, 1)
    gather_scales(c, 2)
    finish_scale(c, 0, 0)
    finish_scale(c, 1, 2)

    fx = ("fx0", "fx1", "fx2", "fx3")
    sl_ld = [c.t[f"ld{s}"] for s in range(4)]
    sl_ul = [c.t[f"ul{s}"] for s in range(4)]
    sl_qb = [c.t[f"qb{s}"] for s in range(4)]
    sl_sg = [c.t[f"sg{s}"] for s in range(4)]
    sl_tb = [c.t[f"tb{s}"] for s in range(4)]
    NTI = NIG * 4 * NTT
    LA = 3
    gate_rows = c.t["sp_gate"].ap().rearrange("(r p) t -> r p t", p=128)
    up_rows = c.t["sp_up"].ap().rearrange("(r p) t -> r p t", p=128)
    pe_s1 = p.now("pe")
    spa_t, spb_t = p.now("spa"), p.now("spb")
    po_t = p.now("po")
    ld_free = {}

    def s2_load(i, first):
        r, j = divmod(i, NTT)
        s = i % 4
        wl = ([("pe", pe_s1), ("spa", spa_t), ("spb", spb_t)] if first
              else [("ve", ld_free[i - 4])])
        p.op("sync", (lambda e, r=r, j=j, s=s: e.dma_start(
            out=sl_ld[s][:, :], in_=gate_rows[r][:, j * TT1:(j + 1) * TT1])),
            wait=wl, inc=[(fx[s], 16)])

    for i in range(min(LA, NTI)):
        s2_load(i, True)
    for i in range(NTI):
        r, j = divmod(i, NTT)
        s = i % 4
        qb, sg, tb, ldb = sl_qb[s], sl_sg[s], sl_tb[s], sl_ld[s]
        w1 = [("ve", p.now("ve")), (fx[s], p.now(fx[s]))]
        if i == 0:
            w1.append(("po", po_t))
        p.op("vector", (lambda e, qb=qb, ldb=ldb: e.tensor_scalar(
            qb[:, :], ldb[:, :], s128[:, 3:4], MAGIC, ALU.mult, ALU.add)),
            wait=w1, inc=[("ve", 1)])
        ld_free[i] = p.now("ve")
        p.op("vector", (lambda e, qb=qb: e.tensor_scalar(
            qb[:, :], qb[:, :], MAGIC, s128[:, 2:3], ALU.subtract, ALU.mult)),
            wait=[("ve", p.now("ve"))], inc=[("ve", 1)])
        p.op("scalar", (lambda e, sg=sg, qb=qb: e.activation(sg[:, :], qb[:, :], AF.Sigmoid)),
             wait=[("se", p.now("se")), ("ve", p.now("ve"))], inc=[("se", 1)])
        p.op("scalar", (lambda e, sg=sg: e.activation(
            sg[:, :], sg[:, :], AF.Relu, bias=mgc[:, 0:1], scale=65536.0)),
            wait=[("se", p.now("se"))], inc=[("se", 1)])
        p.op("vector", (lambda e, sg=sg: e.tensor_scalar(
            sg[:, :], sg[:, :], MAGIC, 65535.0, ALU.subtract, ALU.min)),
            wait=[("ve", p.now("ve")), ("se", p.now("se"))], inc=[("ve", 1)])
        p.op("vector", (lambda e, qb=qb, sg=sg: e.tensor_mul(qb[:, :], qb[:, :], sg[:, :])),
             wait=[("ve", p.now("ve"))], inc=[("ve", 1)])
        p.op("vector", (lambda e, tb=tb, qb=qb: e.tensor_scalar(
            tb[:, :], qb[:, :], 1.0 / 65536.0, None, ALU.mult)),
            wait=[("ve", p.now("ve"))], inc=[("ve", 1)])
        p.op("vector", (lambda e, tb=tb, i=i: e.reduce_max(
            redall[:, i:i + 1], tb[:, :], axis=AXX, apply_absolute_value=True)),
            wait=[("ve", p.now("ve"))], inc=[("ve", 1)])
        p.op("sync", (lambda e, r=r, j=j, tb=tb: e.dma_start(
            out=gate_rows[r][:, j * TT1:(j + 1) * TT1], in_=tb[:, :])),
            wait=[("ve", p.now("ve"))], inc=[(fx[s], 16)])
        if i + LA < NTI:
            s2_load(i + LA, i + LA < 4)
    ve_s2 = p.now("ve")
    p.op("vector", lambda e: e.reduce_max(am[:, 0:1], redall[:, 0:96], axis=AXX),
         wait=[("ve", p.now("ve"))], inc=[("ve", 1)])
    emit_absmax_to_scalar(c, am, 0)
    gather_scales(c, 1)
    finish_scale(c, 0, 4)

    po_t = p.now("po")
    ldT_free = {}
    sg_free = {}

    def s3_load(i, first):
        r, j = divmod(i, NTT)
        s = i % 4
        wl = ([("ve", ve_s2)] if first
              else [("ve", ldT_free[i - 4]), ("se", sg_free[i - 4][1])])
        p.op("sync", (lambda e, r=r, j=j, s=s: e.dma_start(
            out=sl_ld[s][:, :], in_=gate_rows[r][:, j * TT1:(j + 1) * TT1])),
            wait=wl, inc=[(fx[s], 16)])
        p.op("sync", (lambda e, r=r, j=j, s=s: e.dma_start(
            out=sl_ul[s][:, :], in_=up_rows[r][:, j * TT1:(j + 1) * TT1])),
            inc=[(fx[s], 16)])

    for i in range(min(LA, NTI)):
        s3_load(i, True)
    for i in range(NTI):
        r, j = divmod(i, NTT)
        s = i % 4
        qb, sg, tb, ldb, ulb = sl_qb[s], sl_sg[s], sl_tb[s], sl_ld[s], sl_ul[s]
        fx_tot = p.now(fx[s])
        w1 = [("ve", p.now("ve")), (fx[s], fx_tot)]
        if i == 0:
            w1.append(("po", po_t))
        p.op("vector", (lambda e, qb=qb, ldb=ldb: e.tensor_scalar(
            qb[:, :], ldb[:, :], s128[:, 5:6], MAGIC, ALU.mult, ALU.add)),
            wait=w1, inc=[("ve", 1)])
        ldT_free[i] = p.now("ve")
        p.op("vector", (lambda e, qb=qb: e.tensor_scalar(
            qb[:, :], qb[:, :], MAGIC, s128[:, 4:5], ALU.subtract, ALU.mult)),
            wait=[("ve", p.now("ve"))], inc=[("ve", 1)])
        wsg = [("se", p.now("se")), (fx[s], fx_tot)]
        if i >= 4:
            wsg.append(("ve", sg_free[i - 4][0]))
        else:
            wsg.append(("ve", ve_s2))
        p.op("scalar", (lambda e, sg=sg, ulb=ulb: e.activation(
            sg[:, :], ulb[:, :], AF.Relu, bias=mgc[:, 0:1], scale=s128[:, 1:2])),
            wait=wsg, inc=[("se", 1)])
        se_u = p.now("se")
        p.op("vector", (lambda e, sg=sg: e.tensor_scalar(
            sg[:, :], sg[:, :], MAGIC, s128[:, 0:1], ALU.subtract, ALU.mult)),
            wait=[("ve", p.now("ve")), ("se", se_u)], inc=[("ve", 1)])
        p.op("vector", (lambda e, tb=tb, sg=sg, qb=qb: e.tensor_mul(
            tb[:, :], sg[:, :], qb[:, :])),
            wait=[("ve", p.now("ve"))], inc=[("ve", 1)])
        sg_free[i] = (p.now("ve"), se_u)
        p.op("vector", (lambda e, tb=tb, i=i: e.reduce_max(
            redall[:, i:i + 1], tb[:, :], axis=AXX, apply_absolute_value=True)),
            wait=[("ve", p.now("ve"))], inc=[("ve", 1)])
        p.op("sync", (lambda e, r=r, j=j, tb=tb: e.dma_start(
            out=up_rows[r][:, j * TT1:(j + 1) * TT1], in_=tb[:, :])),
            wait=[("ve", p.now("ve"))], inc=[(fx[s], 16)])
        if i + LA < NTI:
            s3_load(i + LA, i + LA < 4)
    p.op("vector", lambda e: e.reduce_max(am[:, 0:1], redall[:, 0:96], axis=AXX),
         wait=[("ve", p.now("ve"))], inc=[("ve", 1)])
    emit_absmax_to_scalar(c, am, 0)
    gather_scales(c, 1)
    finish_scale(c, 0, 6, extra_mult=0.5)
    fx_s3 = [p.now(fx[s]) for s in range(4)]
    po_t4 = p.now("po")

    wdp = [c.t["wd0"], c.t["wd1"]]
    oqp = [c.t["oq0"], c.t["oq1"]]
    wdv = c.t["wd_bf"].ap().rearrange("(g kc p) m -> g p kc m", p=128, g=NIG)
    ov = c.t["sp_up"].ap().rearrange("(g kc p) t -> g p kc t", g=NIG, kc=4)
    NU = NTT * NIG
    pe_unit = {}

    def wd_load(u, extra=()):
        g = u % NIG
        par = u % 2
        p.op("sync", (lambda e, g=g, par=par: e.dma_start(
            out=wdp[par].rearrange("p (kc m) -> p kc m", kc=4), in_=wdv[g])),
            wait=list(extra) + ([("cv", cv_tot)] if u < 2 else []),
            inc=[(fw[par], 16)])

    fo = ("foa", "fob")

    def o_load(u, extra=()):
        j, g = divmod(u, NIG)
        par = u % 2
        p.op("sync", (lambda e, g=g, j=j, par=par: e.dma_start(
            out=oqp[par].rearrange("p (kc t) -> p kc t", kc=4),
            in_=ov[g][:, :, j * TT1:(j + 1) * TT1])),
            wait=list(extra) + [(fx[j], fx_s3[j])], inc=[(fo[par], 16)])

    wd_load(0)
    wd_load(1)
    o_load(0)
    o_load(1)
    for u in range(NU):
        j, g = divmod(u, NIG)
        par = u % 2
        oq = oqp[par]
        wd = wdp[par]
        for kc in range(4):
            w1 = [(fo[par], p.now(fo[par]))] if kc == 0 else []
            if u == 0 and kc == 0:
                w1.append(("po", po_t4))
            qdq_pair(c, oq[:, kc * TT1:(kc + 1) * TT1], oq[:, kc * TT1:(kc + 1) * TT1],
                     7, 6, wait0=w1)
        ve_q = p.now("ve")
        first = True
        for mc in range(NCH):
            for kc in range(4):
                wl = []
                if first:
                    wl = [("ve", ve_q), (fw[par], p.now(fw[par]))]
                    first = False
                p.op("tensor", (lambda e, mc=mc, kc=kc, g=g, oq=oq, wd=wd: e.matmul(
                    c.t[f"psO{mc}"][:, 0:TT1],
                    wd[:, kc * D + mc * 128: kc * D + mc * 128 + 128],
                    oq[:, kc * TT1:(kc + 1) * TT1],
                    start=(g == 0 and kc == 0), stop=(g == NIG - 1 and kc == 3))),
                    wait=wl, inc=([("pe", 1)] if (mc == NCH - 1 and kc == 3) else []))
        pe_unit[u] = p.now("pe")
        if u + 2 < NU:
            wd_load(u + 2, extra=[("pe", pe_unit[u])])
            o_load(u + 2, extra=[("pe", pe_unit[u])])
        if g == NIG - 1:
            for mc in range(NCH):
                p.op("vector", (lambda e, mc=mc, j=j: e.tensor_add(
                    xf(mc, j), xf(mc, j), c.t[f"psO{mc}"][:, 0:TT1])),
                    wait=[("pe", pe_unit[u]), ("ve", p.now("ve"))], inc=[("ve", 1)])


def f1_alloc(c):
    u = c.sbuf("f1u", [128, 10240], F32)

    def carve(name, off, cols, dtype):
        v = u[:, off:off + cols]
        c.t[name] = v.bitcast(BF16)[0:128, 0:cols * 2] if dtype == BF16 else v

    carve("wa0", 0, 2048, BF16)
    carve("wa1", 2048, 2048, BF16)
    carve("wb0", 4096, 2048, BF16)
    carve("wb1", 6144, 2048, BF16)
    carve("ev4a", 8192, 1024, BF16)
    carve("ev4b", 9216, 1024, BF16)
    for s in range(4):
        b = s * 1792
        carve(f"ld{s}", b, 256, BF16)
        carve(f"ul{s}", b + 256, 256, BF16)
        carve(f"tb{s}", b + 512, 256, BF16)
        carve(f"qb{s}", b + 768, 512, F32)
        carve(f"sg{s}", b + 1280, 512, F32)
    c.sbuf("wd0", [128, 4 * D], BF16)
    c.sbuf("wd1", [128, 4 * D], BF16)
    c.sbuf("oq0", [128, 4 * 512], BF16)
    c.sbuf("oq1", [128, 4 * 512], BF16)
    c.sbuf("redall", [128, 192], F32)
    c.sbuf("mgc", [128, 2], F32)
    c.dram("sp_up", [IFF, TC], BF16)
    c.dram("sp_gate", [IFF, TC], BF16)
    c.dram("up_bf", [D, IFF], BF16)
    c.dram("gt_bf", [D, IFF], BF16)
    c.dram("wd_bf", [IFF, D], BF16)
    c.dram("cc_sc_in", [8, 1], F32)
    c.dram("cc_sc_out", [64, 1], F32)
    c.dram("bc_d", [1, 2], F32)


def build_f1_standalone(debug=False):
    nc = bass.Bass("TRN2", target_bir_lowering=False)
    c = Ctx(nc)
    x_in = nc.dram_tensor("x_in", [D, TC], F32, kind="ExternalInput")
    c.t["w_upT_d"] = nc.dram_tensor("w_upT", [D, IFF], F32, kind="ExternalInput")
    c.t["w_gateT_d"] = nc.dram_tensor("w_gateT", [D, IFF], F32, kind="ExternalInput")
    c.t["w_downT_d"] = nc.dram_tensor("w_downT", [IFF, D], F32, kind="ExternalInput")
    x_out = nc.dram_tensor("x_out", [D, TC], F32, kind="ExternalOutput")
    if debug:
        dbg_xq = nc.dram_tensor("dbg_xq", [D, TC], BF16, kind="ExternalOutput")
        dbg_s = nc.dram_tensor("dbg_s", [128, 8], F32, kind="ExternalOutput")
        dbg_o = nc.dram_tensor("dbg_o", [IFF, TC], BF16, kind="ExternalOutput")
        dbg_g2 = nc.dram_tensor("dbg_g2", [IFF, TC], BF16, kind="ExternalOutput")
    c.mk_consts()
    c.sbuf("x_sb", [128, NCH * TC], F32)
    c.sbuf("xq", [128, NCH * TC], BF16)
    c.sbuf("stat", [1, 2048], F32)
    c.sbuf("row_bf", [1, 1024], BF16)
    c.sbuf("sq_bf", [128, 512], BF16)
    c.sbuf("xbf_scr", [128, 512], BF16)
    c.sbuf("ln_scr", [128, 512], F32)
    c.sbuf("red_scr", [128, 1], F32)
    c.sbuf("am128", [128, 1], F32)
    c.sbuf("am2", [128, 1], F32)
    c.sbuf("srow", [1, 64], F32)
    c.sbuf("grow", [1, 64], F32)
    c.sbuf("s128", [128, 8], F32)
    c.sbuf("q_scr", [128, 512], F32)
    c.sbuf("ev_scr", [128, 512], F32)
    for i in range(NCH):
        c.psum(f"psO{i}", [128, 512], F32)
    c.t["psA"] = c.t["psO0"]
    c.t["psB"] = c.t["psO1"]
    f1_alloc(c)
    p = c.p
    load_consts(c)
    emit_f1_preconv(c)
    xv = x_in.ap().rearrange("(c p) t -> p c t", p=128)
    p.op("sync", lambda e: e.dma_start(
        out=c.t["x_sb"].rearrange("p (c t) -> p c t", c=NCH), in_=xv), inc=[("ld", 16)])
    for eng in ("vector", "tensor", "scalar"):
        p.wait(eng, ("ld", p.now("ld")))
    emit_f1(c, 0)
    p.op("sync", lambda e: e.dma_start(
        out=x_out.ap().rearrange("(c p) t -> p c t", p=128),
        in_=c.t["x_sb"].rearrange("p (c t) -> p c t", c=NCH)),
        wait=[("ve", p.now("ve")), ("se", p.now("se"))], inc=[("st", 16)])
    if debug:
        p.op("sync", lambda e: e.dma_start(
            out=dbg_xq.ap().rearrange("(c p) t -> p c t", p=128),
            in_=c.t["xq"].rearrange("p (c t) -> p c t", c=NCH)),
            wait=[("ve", p.now("ve"))], inc=[("st", 16)])
        p.op("sync", lambda e: e.dma_start(out=dbg_s.ap()[:, :], in_=c.t["s128"][:, :]),
             wait=[("po", p.now("po"))], inc=[("st", 16)])
        p.op("sync", lambda e: e.dma_start(out=dbg_o.ap()[:, :], in_=c.t["sp_up"].ap()[:, :]),
             wait=[("sp", p.now("sp"))], inc=[("st", 16)])
        p.op("sync", lambda e: e.dma_start(out=dbg_g2.ap()[:, :], in_=c.t["sp_gate"].ap()[:, :]),
             inc=[("st", 16)])
    p.wait("sync", ("st", p.now("st")))
    p.finalize()
    return nc


def run_f1(d, debug=False):
    nc = build_f1_standalone(debug)
    in_maps = []
    for core in range(NCORES):
        in_maps.append({
            "x_in": x_slice(d, core),
            "w_upT": d["w_upT"], "w_gateT": d["w_gateT"], "w_downT": d["w_downT"],
        })
    res = run_bass_kernel_spmd(nc, in_maps, core_ids=list(range(NCORES)))
    if debug:
        return res
    return np.concatenate([res.results[cc]["x_out"] for cc in range(NCORES)], axis=1)





NEXP = 8
IDYN = 512
ISH = 2048
CAP = 2560.0
NG = TC // 128
TT0 = 512
NTT0 = TC // TT0


def emit_f0(c, inv):
    p = c.p
    x_sb = c.t["x_sb"]
    xq = c.t["xq"]
    oc = c.t["ones_col_sb"]
    orow = c.t["ones_row_sb"]
    idn = c.t["ident_sb"]
    psA, psB = c.t["psA"], c.t["psB"]
    lgt = c.t["lgt"]
    gsc = c.t["gsc"]
    rw = c.t["rw"]
    glt = c.t["glt"]
    mcap = c.t["mcap_fm"]
    capm = c.t["capm_fm"]
    ones8 = c.t["ones8"]

    def xf(ch, j, Tt=TT0):
        return x_sb[:, ch * TC + j * Tt: ch * TC + j * Tt + Tt]

    def xqa(ch, j, Tt=TT0):
        return xq[:, ch * TC + j * Tt: ch * TC + j * Tt + Tt]

    for j in range(4):
        emit_ln(c, (lambda ch, j=j: x_sb[:, ch * TC + j * 512: ch * TC + j * 512 + 512]),
                (lambda ch, j=j: xq[:, ch * TC + j * 512: ch * TC + j * 512 + 512]), 512)

    w9 = c.t["w9"]
    p.op("gpsimd", lambda e: e.dma_start(
        out=w9.rearrange("p (kc m) -> p kc m", m=9),
        in_=c.t["gate_wT_d"].ap().rearrange("(kc p) m -> p kc m", p=128)),
        wait=[("pe", p.now("pe"))], inc=[("wl", 16)])
    wl = p.now("wl")
    lg_fm = c.t["lg_fm"]
    for j in range(4):
        for kc in range(NCH):
            p.op("tensor", (lambda e, kc=kc, j=j: e.matmul(
                psA[0:9, 0:512], w9[:, kc * 9:(kc + 1) * 9], xqa(kc, j, 512),
                start=(kc == 0), stop=(kc == NCH - 1))),
                wait=([("wl", wl), ("ve", p.now("ve")), ("se", p.now("se"))] if kc == 0 else []),
                inc=([("pe", 1)] if kc == NCH - 1 else []))
        v = p.now("pe")
        p.op("vector", (lambda e, j=j: e.tensor_copy(lg_fm[0:9, j * 512:(j + 1) * 512], psA[0:9, 0:512])),
             wait=[("pe", v)], inc=[("ve", 1)])
    if "dbg_lg_d" in c.t:
        p.op("sync", lambda e: e.dma_start(out=c.t["dbg_lg_d"].ap()[:, :], in_=lg_fm[0:9, :]),
             wait=[("ve", p.now("ve"))], inc=[("st", 16)])
    for g in range(NG):
        v = p.op("tensor", (lambda e, g=g: e.transpose(
            psB[:, 0:9], lg_fm[0:9, g * 128:(g + 1) * 128], idn[0:9, 0:9])),
            wait=[("ve", p.now("ve")), ("se", p.now("se"))], inc=[("pe", 1)])[0]
        p.op("vector", (lambda e, g=g: e.tensor_copy(lgt[:, g * 9:(g + 1) * 9], psB[:, 0:9])),
             wait=[("pe", v)], inc=[("ve", 1)])

    rw_all = c.t["rw_all"]
    for g in range(NG):
        dl = lgt[:, g * 9: g * 9 + 8]
        fl = lgt[:, g * 9: g * 9 + 9]
        srt = gsc[:, 0:8]
        exps = gsc[:, 8:16]
        msk = gsc[:, 16:24]
        msk2 = gsc[:, 24:32]
        onehot = gsc[:, 32:40]
        mr8 = gsc[:, 48:56]
        esum = gsc[:, 56:57]
        csum = gsc[:, 57:58]
        thr075 = gsc[:, 58:59]
        valid = gsc[:, 60:64]
        num = gsc[:, 64:65]
        gsum = gsc[:, 65:66]
        wk = gsc[:, 66:67]
        eff = gsc[:, 72:80]
        V = lambda build, w=(): p.op("vector", build, wait=list(w) + [("ve", p.now("ve"))], inc=[("ve", 1)])
        S = lambda build: p.op("scalar", build, wait=[("ve", p.now("ve")), ("se", p.now("se"))], inc=[("se", 1)])

        V(lambda e, dl=dl: e.max(srt, dl))
        sv = S(lambda e: e.activation(exps, srt, AF.Exp, accum_out=esum))[0]
        V(lambda e: e.memset(valid[:, 0:1], 1.0), w=[("se", sv)])
        V(lambda e: e.tensor_scalar(thr075, esum, 0.75, None, ALU.mult))
        V(lambda e: e.tensor_copy(csum, exps[:, 0:1]))
        for k in range(1, 4):
            V(lambda e, k=k: e.tensor_tensor(valid[:, k:k + 1], csum, thr075, ALU.is_lt))
            if k < 3:
                V(lambda e, k=k: e.tensor_add(csum, csum, exps[:, k:k + 1]))
        if g == 0 and "dbg_gate_d" in c.t:
            V(lambda e: e.tensor_copy(c.t["dbg_gate_sb"][:, 0:8], srt))
            V(lambda e: e.tensor_copy(c.t["dbg_gate_sb"][:, 8:12], valid))
            V(lambda e: e.tensor_copy(c.t["dbg_gate_sb"][:, 12:13], esum))
            V(lambda e: e.tensor_copy(c.t["dbg_gate_sb"][:, 13:14], thr075))
            V(lambda e, dl=dl: e.tensor_copy(c.t["dbg_gate_sb"][:, 14:22], dl))
            V(lambda e: e.tensor_copy(c.t["dbg_gate_sb"][:, 22:30], exps))
        V(lambda e, dl=dl: e.tensor_copy(msk, dl))
        V(lambda e: e.memset(mr8, -1e38))
        V(lambda e: e.memset(rw[:, :], 0.0))
        adl = gsc[:, 80:88]
        asrt = gsc[:, 88:96]
        S(lambda e, dl=dl: e.activation(adl, dl, AF.Abs))
        sv = S(lambda e: e.activation(asrt, srt, AF.Abs))[0]
        p.wait("vector", ("se", sv))
        for k in range(4):
            V(lambda e, k=k: e.tensor_copy(mr8[:, 0:1], srt[:, k:k + 1]))
            V(lambda e: e.match_replace(msk2, mr8, msk, -1e30))
            V(lambda e: e.tensor_tensor(onehot, msk, msk2, ALU.not_equal))
            V(lambda e, k=k, dl=dl: e.tensor_scalar(gsc[:, 40:48], dl, srt[:, k:k + 1], None, ALU.subtract))
            V(lambda e, k=k: e.tensor_scalar(eff, adl, asrt[:, k:k + 1], None, ALU.max))
            V(lambda e: e.scalar_tensor_tensor(eff, eff, 0.2, gsc[:, 40:48], ALU.mult, ALU.add))
            V(lambda e: e.tensor_single_scalar(eff, eff, 0.0, ALU.is_ge))
            V(lambda e: e.tensor_scalar(gsc[:, 40:48], eff, 1e30, -1e30, ALU.mult, ALU.add))
            V(lambda e: e.tensor_mul(eff, eff, msk))
            V(lambda e: e.tensor_add(eff, eff, gsc[:, 40:48]))
            sv = S(lambda e: e.activation(eff, eff, AF.Exp, accum_out=gsum))[0]
            V(lambda e: e.tensor_mul(eff, eff, onehot), w=[("se", sv)])
            V(lambda e: e.reduce_sum(num, eff, axis=AXX))
            V(lambda e: e.reciprocal(wk, gsum))
            V(lambda e: e.tensor_mul(wk, wk, num))
            V(lambda e, k=k: e.tensor_mul(wk, wk, valid[:, k:k + 1]))
            V(lambda e: e.scalar_tensor_tensor(rw[:, :], onehot, wk, rw[:, :], ALU.mult, ALU.add))
            if k < 3:
                V(lambda e: e.tensor_copy(msk, msk2))
        V(lambda e: e.reduce_sum(num, rw[:, :], axis=AXX))
        V(lambda e: e.reciprocal(num, num))
        V(lambda e, g=g: e.tensor_scalar(rw_all[:, g * 8:(g + 1) * 8], rw[:, :], num, None, ALU.mult))
        V(lambda e, g=g: e.tensor_single_scalar(gsc[:, 40:48], rw[:, :], 0.0, ALU.is_gt))
        tv = p.op("tensor", (lambda e: e.transpose(psB[0:8, 128:256], gsc[:, 40:48], idn[:, :])),
                  wait=[("ve", p.now("ve")), ("se", p.now("se"))], inc=[("pe", 1)])[0]
        p.op("vector", (lambda e, g=g: e.tensor_copy(mcap[0:8, g * 128:(g + 1) * 128], psB[0:8, 128:256])),
             wait=[("pe", tv)], inc=[("ve", 1)])

    p.op("gpsimd", lambda e: e.dma_start(out=c.t["cc_cap_in"].ap()[:, :], in_=mcap[0:8, :]),
         wait=[("ve", p.now("ve"))], inc=[("cd", 16)])
    p.op("gpsimd", lambda e: e.collective_compute(
        "AllGather", ALU.bypass, replica_groups=[list(range(NCORES))],
        ins=[c.t["cc_cap_in"].ap().opt()], outs=[c.t["cc_cap_out"].ap().opt()]),
        wait=[("cc", p.now("cc")), ("cd", p.now("cd"))], inc=[("cc", 1)])
    gcap = c.t["gcap"]
    p.op("gpsimd", lambda e: e.dma_start(out=gcap[:, :], in_=c.t["cc_cap_out"].ap()[:, :]),
         wait=[("cc", p.now("cc")), ("cd", p.now("cd"))], inc=[("cd", 16)])
    ccv = p.now("cc")
    cdv = p.now("cd")
    tot = c.t["tot64"]
    p.op("vector", lambda e: e.reduce_sum(tot[:, 0:1], gcap[:, :], axis=AXX),
         wait=[("cc", ccv), ("cd", cdv), ("ve", p.now("ve"))], inc=[("ve", 1)])
    tv = p.op("tensor", lambda e: e.transpose(psB[0:1, 256:320], tot[:, 0:1], idn[0:64, 0:64]),
              wait=[("ve", p.now("ve")), ("se", p.now("se"))], inc=[("pe", 1)])[0]
    p.op("vector", lambda e: e.tensor_copy(c.t["srow"][0:1, 48:112][0:1, 0:64], psB[0:1, 256:320]),
         wait=[("pe", tv)], inc=[("ve", 1)])
    for e_ in range(8):
        p.op("vector", (lambda e, e_=e_: e.tensor_mul(
            c.t["srow"][0:1, 112:120],
            c.t["srow"][0:1, 48:112].rearrange("one (r n) -> one n r", n=8)[:, e_],
            c.t["prevrow_sb"][0:1, 0:8])),
            wait=[("ve", p.now("ve"))], inc=[("ve", 1)])
        p.op("vector", (lambda e, e_=e_: e.reduce_sum(
            c.t["srow"][0:1, 40 + e_:41 + e_], c.t["srow"][0:1, 112:120], axis=AXX)),
            wait=[("ve", p.now("ve"))], inc=[("ve", 1)])
    tv = p.op("tensor", lambda e: e.transpose(psB[0:8, 320:321], c.t["srow"][0:1, 40:48], idn[0:1, 0:1]),
              wait=[("ve", p.now("ve")), ("se", p.now("se"))], inc=[("pe", 1)])[0]
    offs = c.t["offs8"]
    p.op("vector", lambda e: e.tensor_copy(offs[0:8, 0:1], psB[0:8, 320:321]),
         wait=[("pe", tv)], inc=[("ve", 1)])
    for jj in range(4):
        init = 0.0 if jj == 0 else capm[0:8, jj * 512 - 1:jj * 512]
        p.op("vector", (lambda e, jj=jj, init=init: e.tensor_tensor_scan(
            capm[0:8, jj * 512:(jj + 1) * 512], ones8[0:8, 0:512],
            mcap[0:8, jj * 512:(jj + 1) * 512], init, ALU.mult, ALU.add)),
            wait=[("ve", p.now("ve"))], inc=[("ve", 1)])
    p.op("vector", lambda e: e.tensor_scalar(capm[0:8, :], capm[0:8, :], offs[0:8, 0:1], None, ALU.add),
         wait=[("ve", p.now("ve"))], inc=[("ve", 1)])
    p.op("vector", lambda e: e.tensor_single_scalar(capm[0:8, :], capm[0:8, :], CAP, ALU.is_le),
         wait=[("ve", p.now("ve"))], inc=[("ve", 1)])
    p.op("vector", lambda e: e.tensor_mul(capm[0:8, :], capm[0:8, :], mcap[0:8, :]),
         wait=[("ve", p.now("ve"))], inc=[("ve", 1)])

    for g in range(NG):
        fl = lgt[:, g * 9: g * 9 + 9]
        cm_t = gsc[:, 0:8]
        eff9 = gsc[:, 8:17]
        gsum = gsc[:, 56:57]
        sd = gsc[:, 57:58]
        V = lambda build, w=(): p.op("vector", build, wait=list(w) + [("ve", p.now("ve"))], inc=[("ve", 1)])
        tv = p.op("tensor", (lambda e, g=g: e.transpose(
            psB[:, 0:8], capm[0:8, g * 128:(g + 1) * 128], idn[0:8, 0:8])),
            wait=[("ve", p.now("ve")), ("se", p.now("se"))], inc=[("pe", 1)])[0]
        V(lambda e: e.tensor_copy(cm_t, psB[:, 0:8]), w=[("pe", tv)])
        V(lambda e: e.tensor_scalar(gsc[:, 40:48], cm_t, 1e30, -1e30, ALU.mult, ALU.add))
        V(lambda e, fl=fl: e.tensor_mul(eff9[:, 0:8], fl[:, 0:8], cm_t))
        V(lambda e: e.tensor_add(eff9[:, 0:8], eff9[:, 0:8], gsc[:, 40:48]))
        V(lambda e, fl=fl: e.tensor_copy(eff9[:, 8:9], fl[:, 8:9]))
        sv = p.op("scalar", lambda e: e.activation(eff9, eff9, AF.Exp, accum_out=gsum),
                  wait=[("ve", p.now("ve")), ("se", p.now("se"))], inc=[("se", 1)])[0]
        V(lambda e: e.reciprocal(gsum, gsum), w=[("se", sv)])
        V(lambda e: e.tensor_scalar(eff9, eff9, gsum, None, ALU.mult))
        V(lambda e: e.reduce_sum(sd, eff9[:, 0:8], axis=AXX))
        V(lambda e, g=g: e.tensor_scalar(gsc[:, 40:48], rw_all[:, g * 8:(g + 1) * 8], sd, None, ALU.mult))
        V(lambda e: e.tensor_mul(gsc[:, 40:48], gsc[:, 40:48], cm_t))
        V(lambda e, g=g: e.tensor_scalar(glt[:, g * 9:g * 9 + 8], gsc[:, 40:48], 0.5, None, ALU.mult))
        V(lambda e, g=g: e.tensor_scalar(glt[:, g * 9 + 8:g * 9 + 9], eff9[:, 8:9], 0.5, None, ALU.mult))
    gl_fm = c.t["gl_fm"]
    for g in range(NG):
        tv = p.op("tensor", (lambda e, g=g: e.transpose(
            psB[0:9, 0:128], glt[:, g * 9:(g + 1) * 9], idn[:, :])),
            wait=[("ve", p.now("ve")), ("se", p.now("se"))], inc=[("pe", 1)])[0]
        p.op("vector", (lambda e, g=g: e.tensor_copy(gl_fm[0:9, g * 128:(g + 1) * 128], psB[0:9, 0:128])),
             wait=[("pe", tv)], inc=[("ve", 1)])
    p.op("gpsimd", lambda e: e.dma_start(out=c.t["gl_d"].ap()[:, :], in_=gl_fm[0:9, :]),
         wait=[("ve", p.now("ve"))], inc=[("cd", 16)])

    wegp = [c.t["weg0"], c.t["weg1"]]
    weup = [c.t["weu0"], c.t["weu1"]]
    wedp = [c.t["wed0"], c.t["wed1"]]
    gu = c.t["gu_sb"]
    wbc = c.t["wbc"]
    silu_scr = c.t["ev_scr"]
    bank_free = {mc: ("ve", 0) for mc in range(NCH)}
    fw = ("fwa", "fwb")
    cv_tot = p.now("cv")
    NE = NEXP + 4

    def wsrc(e_):
        if e_ < NEXP:
            return (c.t["dyn_g_bf"].ap()[e_].rearrange("(kc p) m -> p kc m", p=128),
                    c.t["dyn_u_bf"].ap()[e_].rearrange("(kc p) m -> p kc m", p=128),
                    c.t["dyn_d_bf"].ap()[e_].rearrange("(kc p) m -> p kc m", p=128))
        q = e_ - NEXP
        return (c.t["sh_g_bf"].ap().rearrange("(kc p) m -> p kc m", p=128)[:, :, q * 512:(q + 1) * 512],
                c.t["sh_u_bf"].ap().rearrange("(kc p) m -> p kc m", p=128)[:, :, q * 512:(q + 1) * 512],
                c.t["sh_d_bf"].ap().rearrange("(q kc p) m -> q p kc m", p=128, q=4)[q])

    fw_mark = {}

    def ld_exp(e_, extra=()):
        par = e_ % 2
        g_, u_, d_ = wsrc(e_)
        p.op("sync", (lambda e2, s=g_, par=par: e2.dma_start(
            out=wegp[par].rearrange("p (kc m) -> p kc m", kc=NCH), in_=s)),
            wait=list(extra) + ([("cv", cv_tot)] if e_ < 2 else []),
            inc=[(fw[par], 16)])
        p.op("sync", (lambda e2, s=u_, par=par: e2.dma_start(
            out=weup[par].rearrange("p (kc m) -> p kc m", kc=NCH), in_=s)),
            inc=[(fw[par], 16)])
        p.op("sync", (lambda e2, s=d_, par=par: e2.dma_start(
            out=wedp[par].rearrange("p (kc m) -> p kc m", kc=4), in_=s)),
            inc=[(fw[par], 16)])
        fw_mark[e_] = p.now(fw[par])

    ld_exp(0)
    ld_exp(1)
    cd_gl = p.now("cd")
    bv_prev = 0
    for e_ in range(NE):
        par = e_ % 2
        weg, weu, wed = wegp[par], weup[par], wedp[par]
        er = min(e_, NEXP)
        for j in range(NTT0):
            glv = p.op("gpsimd", (lambda e, er=er, j=j: e.dma_start(
                out=c.t["glrow_s"][0:1, 0:TT0],
                in_=c.t["gl_d"].ap()[er:er + 1, j * TT0:(j + 1) * TT0])),
                wait=[("cd", cd_gl), ("pe", bv_prev)], inc=[("cd", 16)])[0]
            bv = p.op("tensor", (lambda e: e.matmul(
                psA[:, 0:TT0], orow[:, :], c.t["glrow_s"][0:1, 0:TT0],
                start=True, stop=True)),
                wait=[bank_free[0], ("se", p.now("se")), ("cd", glv)], inc=[("pe", 1)])[0]
            bv_prev = bv
            v = p.op("vector", lambda e: e.tensor_copy(wbc[:, 0:TT0], psA[:, 0:TT0]),
                     wait=[("pe", bv)], inc=[("ve", 1)])[0]
            bank_free[0] = ("ve", v)
            for half2, w in ((0, weg), (1, weu)):
                for mc in range(4):
                    bank_i = half2 * 4 + mc
                    for kc in range(NCH):
                        wl = []
                        if kc == 0:
                            wl = [bank_free[bank_i]]
                            if j == 0 and half2 == 0 and mc == 0:
                                wl.append((fw[par], fw_mark[e_]))
                        p.op("tensor", (lambda e, w=w, mc=mc, kc=kc, j=j, bank_i=bank_i: e.matmul(
                            c.t[f"psO{bank_i}"][:, 0:TT0],
                            w[:, kc * 512 + mc * 128: kc * 512 + mc * 128 + 128],
                            xqa(kc, j), start=(kc == 0), stop=(kc == NCH - 1))),
                            wait=wl, inc=([("pe", 1)] if kc == NCH - 1 else []))
            pe_mm = p.now("pe")
            for mc in range(4):
                sv = p.op("scalar", (lambda e, mc=mc: e.activation(
                    silu_scr[:, 0:TT0], c.t[f"psO{mc}"][:, 0:TT0], AF.Silu)),
                    wait=([("pe", pe_mm), ("ve", p.now("ve"))] if mc == 0
                          else [("se", p.now("se")), ("ve", p.now("ve"))]),
                    inc=[("se", 1)])[0]
                bank_free[mc] = ("se", sv)
                v = p.op("vector", (lambda e, mc=mc: e.tensor_tensor(
                    c.t["q_scr"][:, 0:TT0], silu_scr[:, 0:TT0], c.t[f"psO{4 + mc}"][:, 0:TT0], ALU.mult)),
                    wait=[("se", sv)], inc=[("ve", 1)])[0]
                bank_free[4 + mc] = ("ve", v)
                p.op("vector", (lambda e, mc=mc: e.tensor_tensor(
                    gu[:, mc * TT0:(mc + 1) * TT0],
                    c.t["q_scr"][:, 0:TT0], wbc[:, 0:TT0], ALU.mult)),
                    wait=[("ve", p.now("ve"))], inc=[("ve", 1)])
            ve_gu = p.now("ve")
            for mc in range(NCH):
                for kc in range(4):
                    wl = []
                    if kc == 0:
                        wl = [("ve", ve_gu), bank_free[mc]] if mc == 0 else [bank_free[mc]]
                    p.op("tensor", (lambda e, mc=mc, kc=kc, wed=wed: e.matmul(
                        c.t[f"psO{mc}"][:, 0:TT0],
                        wed[:, kc * D + mc * 128: kc * D + mc * 128 + 128],
                        gu[:, kc * TT0:(kc + 1) * TT0],
                        start=(kc == 0), stop=(kc == 3))),
                        wait=wl, inc=([("pe", 1)] if kc == 3 else []))
            pe_dn = p.now("pe")
            for mc in range(NCH):
                v = p.op("vector", (lambda e, mc=mc, j=j: e.tensor_add(
                    xf(mc, j), xf(mc, j), c.t[f"psO{mc}"][:, 0:TT0])),
                    wait=[("pe", pe_dn), ("ve", p.now("ve"))], inc=[("ve", 1)])[0]
                bank_free[mc] = ("ve", v)
        if e_ + 2 < NE:
            ld_exp(e_ + 2, extra=[("pe", p.now("pe"))])


def emit_f0_preconv(c):
    p = c.p
    for src, dst in (("dyn_gT_d", "dyn_g_bf"), ("dyn_uT_d", "dyn_u_bf"),
                     ("dyn_dT_d", "dyn_d_bf"), ("sh_gT_d", "sh_g_bf"),
                     ("sh_uT_d", "sh_u_bf"), ("sh_dT_d", "sh_d_bf")):
        sap = c.t[src].ap()
        dap = c.t[dst].ap()
        if len(sap.shape) == 3:
            for h in range(NEXP):
                p.op("gpsimd", (lambda e, s=sap, d=dap, h=h: e.dma_start(
                    out=d[h], in_=s[h])), inc=[("cv", 16)])
        else:
            n = sap.shape[0]
            for h in range(2):
                p.op("gpsimd", (lambda e, s=sap, d=dap, h=h, n=n: e.dma_start(
                    out=d[h * (n // 2):(h + 1) * (n // 2), :],
                    in_=s[h * (n // 2):(h + 1) * (n // 2), :])), inc=[("cv", 16)])


def f0_alloc(c):
    c.sbuf("w9", [128, NCH * 9], BF16)
    c.sbuf("fm9", [9, TC], F32)
    c.t["lg_fm"] = c.t["fm9"]
    c.t["gl_fm"] = c.t["fm9"]
    c.sbuf("lgt", [128, NG * 9], F32)
    c.sbuf("gsc", [128, 96], F32)
    c.sbuf("rw", [128, 8], F32)
    c.sbuf("rw_all", [128, NG * 8], F32)
    c.sbuf("glt", [128, NG * 9], F32)
    c.sbuf("mcap_fm", [8, TC], BF16)
    c.sbuf("ones8", [8, 512], F32)
    c.sbuf("gcap", [64, TC], F32)
    c.t["capm_fm"] = c.t["gcap"]
    c.sbuf("tot64", [64, 1], F32)
    c.sbuf("offs8", [8, 1], F32)
    c.sbuf("glrow_s", [1, 512], BF16)
    for par in range(2):
        c.sbuf(f"weg{par}", [128, NCH * 512], BF16)
        c.sbuf(f"weu{par}", [128, NCH * 512], BF16)
        c.sbuf(f"wed{par}", [128, 4 * D], BF16)
    c.sbuf("gu_sb", [128, 4 * 512], BF16)
    c.sbuf("wbc", [128, 512], BF16)
    c.dram("cc_cap_in", [8, TC], F32)
    c.dram("cc_cap_out", [64, TC], F32)
    c.dram("gl_d", [9, TC], F32)
    c.dram("dyn_g_bf", [NEXP, D, IDYN], BF16)
    c.dram("dyn_u_bf", [NEXP, D, IDYN], BF16)
    c.dram("dyn_d_bf", [NEXP, IDYN, D], BF16)
    c.dram("sh_g_bf", [D, ISH], BF16)
    c.dram("sh_u_bf", [D, ISH], BF16)
    c.dram("sh_d_bf", [ISH, D], BF16)


def build_f0_standalone(debug=False):
    nc = bass.Bass("TRN2", target_bir_lowering=False)
    c = Ctx(nc)
    x_in = nc.dram_tensor("x_in", [D, TC], F32, kind="ExternalInput")
    c.t["gate_wT_d"] = nc.dram_tensor("gate_wT", [D, 9], F32, kind="ExternalInput")
    c.t["dyn_gT_d"] = nc.dram_tensor("dyn_gT", [NEXP, D, IDYN], F32, kind="ExternalInput")
    c.t["dyn_uT_d"] = nc.dram_tensor("dyn_uT", [NEXP, D, IDYN], F32, kind="ExternalInput")
    c.t["dyn_dT_d"] = nc.dram_tensor("dyn_dT", [NEXP, IDYN, D], F32, kind="ExternalInput")
    c.t["sh_gT_d"] = nc.dram_tensor("sh_gT", [D, ISH], F32, kind="ExternalInput")
    c.t["sh_uT_d"] = nc.dram_tensor("sh_uT", [D, ISH], F32, kind="ExternalInput")
    c.t["sh_dT_d"] = nc.dram_tensor("sh_dT", [ISH, D], F32, kind="ExternalInput")
    prevrow = nc.dram_tensor("prevrow", [1, 8], F32, kind="ExternalInput")
    x_out = nc.dram_tensor("x_out", [D, TC], F32, kind="ExternalOutput")
    _ = None
    if debug:
        c.t["dbg_lg_d"] = nc.dram_tensor("dbg_lg", [9, TC], F32, kind="ExternalOutput")
        dbg_gl = nc.dram_tensor("dbg_gl", [9, TC], F32, kind="ExternalOutput")
        dbg_capm = nc.dram_tensor("dbg_capm", [8, TC], F32, kind="ExternalOutput")
        dbg_rw = nc.dram_tensor("dbg_rw", [128, NG * 8], F32, kind="ExternalOutput")
        dbg_mcap = nc.dram_tensor("dbg_mcap", [8, TC], F32, kind="ExternalOutput")
        dbg_offs = nc.dram_tensor("dbg_offs", [8, 1], F32, kind="ExternalOutput")
        c.t["dbg_gate_d"] = nc.dram_tensor("dbg_gate", [128, 32], F32, kind="ExternalOutput")
        c.sbuf("dbg_gate_sb", [128, 32], F32)

    c.mk_consts()
    c.sbuf("x_sb", [128, NCH * TC], F32)
    c.sbuf("xq", [128, NCH * TC], BF16)
    c.sbuf("stat", [1, 2048], F32)
    c.sbuf("row_bf", [1, 1024], BF16)
    c.sbuf("sq_bf", [128, 512], BF16)
    c.sbuf("xbf_scr", [128, 512], BF16)
    c.sbuf("ln_scr", [128, 512], F32)
    c.sbuf("red_scr", [128, 1], F32)
    c.sbuf("srow", [1, 128], F32)
    c.sbuf("q_scr", [128, 512], F32)
    c.sbuf("ev_scr", [128, 512], F32)
    c.sbuf("prevrow_sb", [1, 8], F32)
    for i in range(NCH):
        c.psum(f"psO{i}", [128, 512], F32)
    c.t["psA"] = c.t["psO0"]
    c.t["psB"] = c.t["psO1"]
    f0_alloc(c)
    p = c.p
    load_consts(c)
    emit_f0_preconv(c)
    xv = x_in.ap().rearrange("(c p) t -> p c t", p=128)
    p.op("sync", lambda e: e.dma_start(
        out=c.t["x_sb"].rearrange("p (c t) -> p c t", c=NCH), in_=xv), inc=[("ld", 16)])
    p.op("sync", lambda e: e.dma_start(out=c.t["prevrow_sb"][0:1, :], in_=prevrow.ap()[:, :]),
         inc=[("ld", 16)])
    p.op("vector", lambda e: e.memset(c.t["ones8"][0:8, :], 1.0), inc=[("ve", 1)])
    for eng in ("vector", "tensor", "scalar"):
        p.wait(eng, ("ld", p.now("ld")))
    emit_f0(c, 0)
    p.op("sync", lambda e: e.dma_start(
        out=x_out.ap().rearrange("(c p) t -> p c t", p=128),
        in_=c.t["x_sb"].rearrange("p (c t) -> p c t", c=NCH)),
        wait=[("ve", p.now("ve")), ("se", p.now("se"))], inc=[("st", 16)])
    if debug:
        p.op("sync", lambda e: e.dma_start(out=dbg_gl.ap()[:, :], in_=c.t["gl_fm"][0:9, :]),
             inc=[("st", 16)])
        p.op("sync", lambda e: e.dma_start(out=dbg_capm.ap()[:, :], in_=c.t["capm_fm"][0:8, :]),
             inc=[("st", 16)])
        p.op("sync", lambda e: e.dma_start(out=dbg_rw.ap()[:, :], in_=c.t["rw_all"][:, :]),
             inc=[("st", 16)])
        p.op("gpsimd", lambda e: e.dma_start(out=dbg_mcap.ap()[:, :], in_=c.t["mcap_fm"][0:8, :]),
             inc=[("st", 16)])
        p.op("sync", lambda e: e.dma_start(out=dbg_offs.ap()[:, :], in_=c.t["offs8"][0:8, :]),
             inc=[("st", 16)])
        p.op("sync", lambda e: e.dma_start(out=c.t["dbg_gate_d"].ap()[:, :], in_=c.t["dbg_gate_sb"][:, :]),
             inc=[("st", 16)])
    p.wait("sync", ("st", p.now("st")))
    p.finalize()
    return nc


def run_f0(d, debug=False):
    nc = build_f0_standalone(debug)
    in_maps = []
    for core in range(NCORES):
        prev = np.zeros((1, 8), np.float32)
        prev[0, :core] = 1.0
        in_maps.append({
            "x_in": x_slice(d, core), "gate_wT": d["gate_wT"],
            "dyn_gT": d["dyn_gT"], "dyn_uT": d["dyn_uT"], "dyn_dT": d["dyn_dT"],
            "sh_gT": d["sh_gT"], "sh_uT": d["sh_uT"], "sh_dT": d["sh_dT"],
            "prevrow": prev,
        })
    res = run_bass_kernel_spmd(nc, in_maps, core_ids=list(range(NCORES)))
    if debug:
        return res
    return np.concatenate([res.results[cc]["x_out"] for cc in range(NCORES)], axis=1)





NHD = 8
HD = 64


def emit_a1(c, inv):
    p = c.p
    x_sb = c.t["x_sb"]
    xq = c.t["xq"]
    orow = c.t["ones_row_sb"]
    psA, psB = c.t["psA"], c.t["psB"]
    psS = [(c.t["psO2"], c.t["psO3"]), (c.t["psO4"], c.t["psO5"])]
    psAV = (c.t["psO6"], c.t["psO7"])
    qkv_sb = c.t["qkv_sb"]
    th_sb = c.t["th_sb"]
    wq = c.t["wqkv_w"]
    parsel = c.t["parsel_sb"]
    pk = c.t["pk_sb"]
    blk = c.t["blk_sb"]
    vt = c.t["vt_sb"]
    pT = c.t["pT_sb"]
    oh = c.t["oh_sb"]
    lrec = c.t["lrec_sb"]
    lb = c.t["lbc_sb"]
    trimask = c.t["trimask_sb"]

    def xf(ch, j):
        return x_sb[:, ch * TC + j * 512: ch * TC + j * 512 + 512]

    def xqa(ch, j):
        return xq[:, ch * TC + j * 512: ch * TC + j * 512 + 512]

    p.op("sync", lambda e: e.dma_start(out=trimask[:, :], in_=c.const["trimask"].ap()[:, :]),
         inc=[("ld", 16)])
    p.op("sync", lambda e: e.dma_start(out=c.t["ident_bf_sb"][:, :], in_=c.const["ident_bf"].ap()[:, :]),
         inc=[("ld", 16)])
    ldc = p.now("ld")

    for j in range(4):
        emit_ln(c, (lambda ch, j=j: xf(ch, j)), (lambda ch, j=j: xqa(ch, j)), 512)

    for ocg in range(3):
        pe_b = p.now("pe")
        wv = c.t["wqvk_d"].ap().rearrange("(kc p) m -> p kc m", p=128)[:, :, ocg * 1024:(ocg + 1) * 1024]
        p.op("gpsimd", (lambda e, s=wv: e.dma_start(
            out=wq[:, 0:NCH * 1024].rearrange("p (kc m) -> p kc m", kc=NCH), in_=s)),
            wait=[("pe", pe_b)], inc=[("wl", 16)])
        wl = p.now("wl")
        for j in range(4):
            for oc2 in range(8):
                bank = c.t[f"psO{oc2}"]
                for kc in range(NCH):
                    p.op("tensor", (lambda e, oc2=oc2, kc=kc, j=j, bank=bank: e.matmul(
                        bank[:, 0:512],
                        wq[:, kc * 1024 + oc2 * 128: kc * 1024 + oc2 * 128 + 128],
                        xqa(kc, j), start=(kc == 0), stop=(kc == NCH - 1))),
                        wait=([("wl", wl), ("ve", p.now("ve")), ("se", p.now("se"))]
                              if (oc2 == 0 and kc == 0) else []),
                        inc=([("pe", 1)] if kc == NCH - 1 else []))
            pe_mm = p.now("pe")
            for oc2 in range(8):
                ochunk = ocg * 8 + oc2
                bank = c.t[f"psO{oc2}"]
                if ochunk < 12:
                    dst = qkv_sb[:, ochunk * TC + j * 512: ochunk * TC + j * 512 + 512]
                    p.op("vector", (lambda e, dst=dst, bank=bank: e.tensor_copy(dst, bank[:, 0:512])),
                         wait=([("pe", pe_mm)] if oc2 == 0 else [("ve", p.now("ve"))]),
                         inc=[("ve", 1)])
                else:
                    p.op("vector", (lambda e, bank=bank: e.tensor_copy(th_sb[:, 0:512], bank[:, 0:512])),
                         wait=([("pe", pe_mm), ("cd", p.now("cd"))] if oc2 == 0
                               else [("ve", p.now("ve")), ("cd", p.now("cd"))]),
                         inc=[("ve", 1)])
                    tv = c.t["cc_qkv_in"].ap().rearrange("(ch p) t -> ch p t", p=128)[ochunk - 12]
                    p.op("gpsimd", (lambda e, tv=tv, j=j: e.dma_start(
                        out=tv[:, j * 512:(j + 1) * 512], in_=th_sb[:, 0:512])),
                        wait=[("ve", p.now("ve"))], inc=[("cd", 16)])
    p.op("gpsimd", lambda e: e.collective_compute(
        "AllGather", ALU.bypass, replica_groups=[[0, 1], [2, 3], [4, 5], [6, 7]],
        ins=[c.t["cc_qkv_in"].ap().opt()], outs=[c.t["cc_qkv_out"].ap().opt()]),
        wait=[("cc", p.now("cc")), ("cd", p.now("cd"))], inc=[("cc", 1)])
    ccg = p.now("cc")

    psXbf = psB.bitcast(BF16)

    for pk_i in range(NHD // 2):
        for b_i in range(2):
            for s_i in range(3):
                rows0 = b_i * 1536 + s_i * 512 + pk_i * 128
                gv = c.t["cc_qkv_out"].ap().rearrange("(r p) t -> r p t", p=128)[rows0 // 128]
                p.op("gpsimd", (lambda e, gv=gv, b_i=b_i, s_i=s_i: e.dma_start(
                    out=blk[:, (b_i * 3 + s_i) * TC:(b_i * 3 + s_i) * TC + TC], in_=gv)),
                    wait=([("cc", ccg), ("ve", p.now("ve"))] if (b_i == 0 and s_i == 0) else []),
                    inc=[("cd", 16)])
        ccb = p.now("cd")
        for s_i in range(3):
            mine = qkv_sb[:, (s_i * 4 + pk_i) * TC:(s_i * 4 + pk_i) * TC + TC]
            b0 = blk[:, (0 + s_i) * TC:(0 + s_i) * TC + TC]
            b1 = blk[:, (3 + s_i) * TC:(3 + s_i) * TC + TC]
            for half in range(2):
                dst = pk[:, s_i * 4096 + half * 2048: s_i * 4096 + half * 2048 + 2048]
                cm = 2 + half
                cb0 = 4 + 2 * half
                cb1 = 5 + 2 * half
                p.op("vector", (lambda e, dst=dst, mine=mine, cm=cm: e.tensor_scalar(
                    dst, mine, parsel[:, cm:cm + 1], None, ALU.mult)),
                    wait=[("cd", ccb), ("ve", p.now("ve"))], inc=[("ve", 1)])
                p.op("vector", (lambda e, dst=dst, b0=b0, cb0=cb0: e.scalar_tensor_tensor(
                    dst, b0, parsel[:, cb0:cb0 + 1], dst, ALU.mult, ALU.add)),
                    wait=[("ve", p.now("ve"))], inc=[("ve", 1)])
                p.op("vector", (lambda e, dst=dst, b1=b1, cb1=cb1: e.scalar_tensor_tensor(
                    dst, b1, parsel[:, cb1:cb1 + 1], dst, ALU.mult, ALU.add)),
                    wait=[("ve", p.now("ve"))], inc=[("ve", 1)])
        p.op("vector", lambda e: e.memset(vt[:, :], 1.0), wait=[("ve", p.now("ve"))], inc=[("ve", 1)])
        for kt in range(32):
            tv = p.op("tensor", (lambda e, kt=kt: e.transpose(
                psXbf[:, 0:128],
                pk[:, 2 * 4096 + kt * 128: 2 * 4096 + kt * 128 + 128],
                c.t["ident_bf_sb"][:, :])),
                wait=[("ve", p.now("ve")), ("se", p.now("se")), ("ld", ldc)], inc=[("pe", 1)])[0]
            for hh in range(2):
                p.op("vector", (lambda e, kt=kt, hh=hh: e.tensor_copy(
                    vt[:, (hh * 32 + kt) * 65:(hh * 32 + kt) * 65 + 64],
                    psXbf[:, hh * 64:hh * 64 + 64])),
                    wait=([("pe", tv)] if hh == 0 else [("ve", p.now("ve"))]), inc=[("ve", 1)])
        exp_se_mark = {0: p.now("se"), 1: p.now("se")}
        add_ve_mark = {0: p.now("ve"), 1: p.now("ve")}
        for hh in range(2):
            for qg in range(4):
                q0 = qg * 1024
                for kt in range((qg + 1) * 8):
                    di = kt - qg * 8
                    n0 = 0 if di < 0 else di * 128
                    sbanks = psS[kt % 2]
                    first_w = [("se", exp_se_mark[kt % 2]), ("ve", add_ve_mark[kt % 2])]
                    first_half = 0 if n0 < 512 else 1
                    for half in range(2):
                        a0, a1 = half * 512, half * 512 + 512
                        if a1 <= n0:
                            continue
                        s0 = max(a0, n0)
                        p.op("tensor", (lambda e, kt=kt, s0=s0, a1=a1, q0=q0, sb=sbanks[half], a0=a0, hh=hh: e.matmul(
                            sb[:, s0 - a0:a1 - a0],
                            pk[hh * 64:hh * 64 + 64, 4096 + kt * 128: 4096 + kt * 128 + 128],
                            pk[hh * 64:hh * 64 + 64, q0 + s0: q0 + a1],
                            start=True, stop=True)),
                            wait=(first_w if half == first_half else []),
                            inc=[("pe", 1)])
                    pe_s = p.now("pe")
                    if di >= 0:
                        hf = n0 // 512
                        p.op("vector", (lambda e, sb=sbanks[n0 // 512], r0=n0 % 512: e.tensor_add(
                            sb[:, r0:r0 + 128], sb[:, r0:r0 + 128], trimask[:, :])),
                            wait=[("pe", pe_s), ("ve", p.now("ve"))], inc=[("ve", 1)])
                    add_ve_mark[kt % 2] = p.now("ve")
                    sv = None
                    for half in range(2):
                        a0, a1 = half * 512, half * 512 + 512
                        if a1 <= n0:
                            continue
                        s0 = max(a0, n0)
                        sv = p.op("scalar", (lambda e, sb=sbanks[half], s0=s0, a0=a0, a1=a1: e.activation(
                            pT[:, s0:a1], sb[:, s0 - a0:a1 - a0], AF.Exp, scale=1.0 / 32.0)),
                            wait=([("pe", pe_s), ("ve", p.now("ve")), ("se", p.now("se"))]
                                  if half == first_half else []),
                            inc=[("se", 1)])[0]
                    exp_se_mark[kt % 2] = p.now("se")
                    for half in range(2):
                        a0, a1 = half * 512, half * 512 + 512
                        if a1 <= n0:
                            continue
                        s0 = max(a0, n0)
                        p.op("tensor", (lambda e, kt=kt, s0=s0, a0=a0, a1=a1, hh=hh, qg=qg, av=psAV[half]: e.matmul(
                            av[0:65, s0 - a0:a1 - a0],
                            vt[:, (hh * 32 + kt) * 65:(hh * 32 + kt) * 65 + 65],
                            pT[:, s0:a1],
                            start=(kt == 0), stop=(kt == (qg + 1) * 8 - 1),
                            skip_group_check=True)),
                            wait=([("se", sv)] + ([("ve", p.now("ve"))] if kt == 0 else [])
                                  if half == first_half else []),
                            inc=[("pe", 1)])
                pe_av = p.now("pe")
                for half in range(2):
                    h0 = half * 512
                    p.op("vector", (lambda e, av=psAV[half]: e.reciprocal(
                        lrec[0:1, 0:512], av[64:65, 0:512])),
                        wait=[("pe", pe_av), ("ve", p.now("ve")), ("cd", p.now("cd"))], inc=[("ve", 1)])
                    p.op("vector", (lambda e: e.tensor_copy(
                        c.t["lrow_bf"][0:1, 0:512], lrec[0:1, 0:512])),
                        wait=[("ve", p.now("ve"))], inc=[("ve", 1)])
                    bv = p.op("tensor", (lambda e: e.matmul(
                        psB[0:64, 0:512], orow[:, 0:64],
                        c.t["lrow_bf"][0:1, 0:512], start=True, stop=True)),
                        wait=[("ve", p.now("ve")), ("se", p.now("se"))], inc=[("pe", 1)])[0]
                    p.op("vector", (lambda e: e.tensor_copy(
                        lb[0:64, 0:512], psB[0:64, 0:512])),
                        wait=[("pe", bv)], inc=[("ve", 1)])
                    p.op("vector", (lambda e, h0=h0, av=psAV[half]: e.tensor_mul(
                        oh[0:64, h0: h0 + 512],
                        av[0:64, 0:512], lb[0:64, 0:512])),
                        wait=[("ve", p.now("ve"))], inc=[("ve", 1)])
                hrow = (pk_i * 2 + hh) * 64
                ovq = c.t["cc_o_in"].ap().rearrange("(h p) t -> h p t", p=64)[hrow // 64]
                p.op("gpsimd", (lambda e, ovq=ovq, qg=qg: e.dma_start(
                    out=ovq[:, qg * 1024:(qg + 1) * 1024], in_=oh[0:64, 0:1024])),
                    wait=[("ve", p.now("ve"))], inc=[("cd", 16)])

    p.op("gpsimd", lambda e: e.collective_compute(
        "AllGather", ALU.bypass, replica_groups=[[0, 1], [2, 3], [4, 5], [6, 7]],
        ins=[c.t["cc_o_in"].ap().opt()], outs=[c.t["cc_o_out"].ap().opt()]),
        wait=[("cc", p.now("cc")), ("cd", p.now("cd"))], inc=[("cc", 1)])
    cco = p.now("cc")
    o_my = qkv_sb
    for ch in range(8):
        gv0 = c.t["cc_o_out"].ap().rearrange("(ch p) t -> ch p t", p=128)[ch]
        p.op("gpsimd", (lambda e, gv0=gv0: e.dma_start(
            out=blk[:, 0:4096], in_=gv0)),
            wait=[("cc", cco), ("ve", p.now("ve"))], inc=[("cd", 16)])
        ccb = p.now("cd")
        b0 = blk[:, 0:2048]
        b1 = blk[:, 2048:4096]
        dst = o_my[:, ch * TC: ch * TC + TC]
        p.op("vector", (lambda e, dst=dst, b0=b0: e.tensor_scalar(
            dst, b0, parsel[:, 2:3], None, ALU.mult)),
            wait=[("cd", ccb), ("ve", p.now("ve"))], inc=[("ve", 1)])
        p.op("vector", (lambda e, dst=dst, b1=b1: e.scalar_tensor_tensor(
            dst, b1, parsel[:, 3:4], dst, ALU.mult, ALU.add)),
            wait=[("ve", p.now("ve"))], inc=[("ve", 1)])
    wo = blk
    pe_b = p.now("pe")
    p.op("gpsimd", lambda e: e.dma_start(
        out=wo[:, 0:8 * D].rearrange("p (kc m) -> p kc m", m=D),
        in_=c.t["wo_d"].ap().rearrange("(kc p) m -> p kc m", p=128)),
        wait=[("pe", pe_b), ("ve", p.now("ve"))], inc=[("wl", 16)])
    wl = p.now("wl")
    for j in range(4):
        for mc in range(NCH):
            for kc in range(NCH):
                p.op("tensor", (lambda e, mc=mc, kc=kc, j=j: e.matmul(
                    c.t[f"psO{mc}"][:, 0:512],
                    wo[:, kc * D + mc * 128: kc * D + mc * 128 + 128],
                    o_my[:, kc * TC + j * 512: kc * TC + j * 512 + 512],
                    start=(kc == 0), stop=(kc == NCH - 1))),
                    wait=([("wl", wl), ("ve", p.now("ve")), ("se", p.now("se"))]
                          if (mc == 0 and kc == 0) else []),
                    inc=([("pe", 1)] if kc == NCH - 1 else []))
        pe_dn = p.now("pe")
        for mc in range(NCH):
            p.op("vector", (lambda e, mc=mc, j=j: e.tensor_add(
                xf(mc, j), xf(mc, j), c.t[f"psO{mc}"][:, 0:512])),
                wait=([("pe", pe_dn)] if mc == 0 else [("ve", p.now("ve"))]), inc=[("ve", 1)])


def a1_alloc(c):
    c.sbuf("qkv_sb", [128, 12 * TC], BF16)
    c.sbuf("th_sb", [128, 512], BF16)
    c.sbuf("pk_sb", [128, 3 * 4096], BF16)
    c.t["wqkv_w"] = c.t["pk_sb"]
    c.t["blk_sb"] = c.t["xq"]
    c.sbuf("vt_sb", [128, 2 * 32 * 65], BF16)
    c.sbuf("pT_sb", [128, 1024], BF16)
    c.sbuf("oh_sb", [64, 1024], BF16)
    c.sbuf("lrec_sb", [1, 512], F32)
    c.sbuf("lrow_bf", [1, 512], BF16)
    c.sbuf("lbc_sb", [64, 512], F32)
    c.sbuf("trimask_sb", [128, 128], F32)
    c.sbuf("ident_bf_sb", [128, 128], BF16)
    if "trimask" not in c.const:
        c.inline("trimask", np.where(np.arange(128)[:, None] > np.arange(128)[None, :],
                                     np.float32(-1e30), np.float32(0.0)).astype(np.float32))
        c.inline("ident_bf", np.eye(128, dtype=ml_dtypes.bfloat16))
    c.dram("cc_qkv_in", [12 * 128, TC], BF16)
    c.dram("cc_qkv_out", [2 * 12 * 128, TC], BF16)
    c.dram("cc_o_in", [512, 4096], BF16)
    c.dram("cc_o_out", [1024, 4096], BF16)


def a1_psum_alloc(c):
    for i in range(NCH):
        c.psum(f"psO{i}", [128, 512], F32)
    c.t["psA"] = c.t["psO0"]
    c.t["psB"] = c.t["psO1"]


def host_parsel(core):
    par = core & 1
    c0 = 1.0 if par == 1 else 0.0
    c1 = 1.0 - c0
    c2 = 1.0 if par == 0 else 0.0
    c3 = 1.0 - c2
    row = np.array([c0, c1, c2, c3, c0 * c3, c1 * c3, c0 * c2, c1 * c2], np.float32)
    return np.broadcast_to(row, (128, 8)).copy()


def host_wqvk_perm(Wqvk, core):
    par = core & 1
    mine = list(range(8 * par, 8 * par + 8))
    oth = list(range(8 * (1 - par), 8 * (1 - par) + 8))
    cols = []
    for heads in (mine, oth):
        for slot in (0, 2, 1):
            for h in heads:
                base = slot * 1024 + h * 64
                cols.extend(range(base, base + 64))
    return np.ascontiguousarray(Wqvk[:, cols])


def build_a1_standalone(debug=False):
    nc = bass.Bass("TRN2", target_bir_lowering=False)
    c = Ctx(nc)
    x_in = nc.dram_tensor("x_in", [D, TC], F32, kind="ExternalInput")
    c.t["wqvk_d"] = nc.dram_tensor("wqvk", [D, 3072], F32, kind="ExternalInput")
    c.t["wo_d"] = nc.dram_tensor("wo", [D, D], F32, kind="ExternalInput")
    parsel_in = nc.dram_tensor("parsel", [128, 8], F32, kind="ExternalInput")
    x_out = nc.dram_tensor("x_out", [D, TC], F32, kind="ExternalOutput")
    c.mk_consts()
    c.sbuf("x_sb", [128, NCH * TC], F32)
    c.sbuf("xq", [128, NCH * TC], BF16)
    c.sbuf("stat", [1, 2048], F32)
    c.sbuf("row_bf", [1, 1024], BF16)
    c.sbuf("sq_bf", [128, 512], BF16)
    c.sbuf("xbf_scr", [128, 512], BF16)
    c.sbuf("ln_scr", [128, 512], F32)
    c.sbuf("red_scr", [128, 1], F32)
    c.sbuf("srow", [1, 128], F32)
    a1_psum_alloc(c)
    a1_alloc(c)
    p = c.p
    load_consts(c, ident=False)
    xv = x_in.ap().rearrange("(c p) t -> p c t", p=128)
    p.op("sync", lambda e: e.dma_start(
        out=c.t["x_sb"].rearrange("p (c t) -> p c t", c=NCH), in_=xv), inc=[("ld", 16)])
    p.op("sync", lambda e: e.dma_start(out=c.t["parsel_sb"][:, :], in_=parsel_in.ap()[:, :]),
         inc=[("ld", 16)])
    for eng in ("vector", "tensor", "scalar"):
        p.wait(eng, ("ld", p.now("ld")))
    emit_a1(c, 0)
    p.op("sync", lambda e: e.dma_start(
        out=x_out.ap().rearrange("(c p) t -> p c t", p=128),
        in_=c.t["x_sb"].rearrange("p (c t) -> p c t", c=NCH)),
        wait=[("ve", p.now("ve")), ("se", p.now("se"))], inc=[("st", 16)])
    p.wait("sync", ("st", p.now("st")))
    p.finalize()
    return nc


def run_a1(d, debug=False):
    nc = build_a1_standalone(debug)
    in_maps = []
    for core in range(NCORES):
        in_maps.append({
            "x_in": x_slice(d, core),
            "wqvk": host_wqvk_perm(d["Wqvk"], core),
            "wo": d["Wo"],
            "parsel": host_parsel(core),
        })
    res = run_bass_kernel_spmd(nc, in_maps, core_ids=list(range(NCORES)))
    return np.concatenate([res.results[cc]["x_out"] for cc in range(NCORES)], axis=1)





LCH = 16
NCHK = TC // LCH


def emit_a0_powers(c):
    p = c.p
    pwa = c.t["pw_a"]
    pwb = c.t["pw_b"]
    myp, myq = c.t["myp_sb"], c.t["myq_sb"]
    p.op("gpsimd", lambda e: e.dma_start(
        out=pwa.rearrange("p (kc m) -> p kc m", kc=NCH),
        in_=c.t["A_T_d"].ap().rearrange("(kc p) m -> p kc m", p=128)), inc=[("pw", 16)])
    p.op("gpsimd", lambda e: e.dma_start(
        out=pwb.rearrange("p (kc m) -> p kc m", kc=NCH),
        in_=c.t["A_nat_d"].ap().rearrange("(kc p) m -> p kc m", p=128)), inc=[("pw", 16)])
    d = 1
    for rnd in range(8):
        pwv = p.now("pw")
        for which in range(2):
            lw = pwb if which == 0 else pwa
            rhs = myp if which == 0 else myq
            dstmy = myp if which == 0 else myq
            for ic in range(NCH):
                bank = c.t[f"psO{ic}"]
                for kc in range(NCH):
                    p.op("tensor", (lambda e, lw=lw, ic=ic, kc=kc, rhs=rhs, bank=bank: e.matmul(
                        bank[:, 0:128],
                        lw[:, kc * 1024 + ic * 128: kc * 1024 + ic * 128 + 128],
                        rhs[:, kc * 128:(kc + 1) * 128],
                        start=(kc == 0), stop=(kc == NCH - 1))),
                        wait=([("pw", pwv), ("ve", p.now("ve")), ("se", p.now("se"))]
                              if (ic == 0 and kc == 0) else []),
                        inc=([("pe", 1)] if kc == NCH - 1 else []))
            pe_mm = p.now("pe")
            for ic in range(NCH):
                bank = c.t[f"psO{ic}"]
                p.op("vector", (lambda e, ic=ic, bank=bank, dstmy=dstmy: e.tensor_copy(
                    dstmy[:, ic * 128:(ic + 1) * 128], bank[:, 0:128])),
                    wait=([("pe", pe_mm)] if ic == 0 else [("ve", p.now("ve"))]), inc=[("ve", 1)])
        for which in range(2):
            dstmy = myp if which == 0 else myq
            dst = pwa if which == 0 else pwb
            cin = c.t["cc_pw_in"]
            cout = c.t["cc_pw_out"] if which == 0 else c.t["cc_pw_out2"]
            p.op("gpsimd", (lambda e, cin=cin, dstmy=dstmy: e.dma_start(
                out=cin.ap().rearrange("(kc p) m -> p kc m", p=128),
                in_=dstmy.rearrange("p (kc m) -> p kc m", kc=NCH))),
                wait=[("ve", p.now("ve")), ("pw", p.now("pw")), ("cc", p.now("cc")), ("cd", p.now("cd"))], inc=[("pw", 16)])
            p.op("gpsimd", (lambda e, cin=cin, cout=cout: e.collective_compute(
                "AllGather", ALU.bypass, replica_groups=[list(range(NCORES))],
                ins=[cin.ap().opt()], outs=[cout.ap().opt()])),
                wait=[("pw", p.now("pw")), ("cc", p.now("cc")), ("cd", p.now("cd"))], inc=[("cc", 1)])
            pwg = p.now("pw")
            for r_ in range(NCORES):
                cv = cout.ap().rearrange("(r kc p) m -> r p kc m", r=NCORES, p=128)[r_]
                dv_ = dst.rearrange("p (kc rm) -> p kc rm", kc=NCH)[:, :, r_ * 128:(r_ + 1) * 128]
                p.op("gpsimd", (lambda e, cv=cv, dv_=dv_: e.dma_start(out=dv_, in_=cv)),
                     wait=([("pw", pwg), ("cc", p.now("cc")), ("cd", p.now("cd"))] if r_ == 0 else []), inc=[("pw", 16)])
        d *= 2
        if d in (16, 256):
            tgt = c.t[f"pw_p{d}"]
            p.op("gpsimd", (lambda e, tgt=tgt: e.dma_start(
                out=tgt.ap().rearrange("(kc p) m -> p kc m", p=128),
                in_=pwa.rearrange("p (kc m) -> p kc m", kc=NCH))),
                wait=[("pw", p.now("pw")), ("cc", p.now("cc")), ("cd", p.now("cd"))], inc=[("pw", 16)])
    p.wait("vector", ("pw", p.now("pw")), ("cc", p.now("cc")), ("cd", p.now("cd")))


def emit_a0(c, inv):
    p = c.p
    x_sb = c.t["x_sb"]
    xq = c.t["xq"]
    psA, psB = c.t["psA"], c.t["psB"]
    wbuf = c.t["wu_sb"]
    wbuf2 = c.t["wg_sb"]
    uB = c.t["uB_all"]
    s_all = c.t["xq"]
    sp_ = [c.t["s_ping"], c.t["s_pong"]]
    parsel = c.t["parsel_sb"]

    def xf(ch, j):
        return x_sb[:, ch * TC + j * 512: ch * TC + j * 512 + 512]

    def xqa(ch, j):
        return xq[:, ch * TC + j * 512: ch * TC + j * 512 + 512]

    for j in range(4):
        emit_ln(c, (lambda ch, j=j: xf(ch, j)), (lambda ch, j=j: xqa(ch, j)), 512)

    for nm, w in (("W_uT_d", wbuf), ("W_gT_d", wbuf2)):
        p.op("gpsimd", (lambda e, nm=nm, w=w: e.dma_start(
            out=w.rearrange("p (kc m) -> p kc m", kc=NCH),
            in_=c.t[nm].ap().rearrange("(kc p) m -> p kc m", p=128))),
            wait=[("pe", p.now("pe"))] if nm == "W_uT_d" else [], inc=[("wl", 16)])
    wl = p.now("wl")
    gel = c.t["gel_scr"]
    u_t = c.t["u_tile"]
    for j in range(4):
        for half, w in ((0, wbuf), (1, wbuf2)):
            for mc in range(NCH):
                bank = c.t[f"psO{mc}"]
                for kc in range(NCH):
                    p.op("tensor", (lambda e, w=w, mc=mc, kc=kc, j=j, bank=bank: e.matmul(
                        bank[:, 0:512],
                        w[:, kc * 1024 + mc * 128: kc * 1024 + mc * 128 + 128],
                        xqa(kc, j), start=(kc == 0), stop=(kc == NCH - 1))),
                        wait=([("wl", wl), ("ve", p.now("ve")), ("se", p.now("se"))]
                              if (mc == 0 and kc == 0) else []),
                        inc=([("pe", 1)] if kc == NCH - 1 else []))
            pe_mm = p.now("pe")
            for mc in range(NCH):
                bank = c.t[f"psO{mc}"]
                if half == 0:
                    p.op("scalar", (lambda e, mc=mc, bank=bank: e.activation(
                        gel[:, mc * 512:(mc + 1) * 512], bank[:, 0:512], AF.Gelu,
                        bias=c.t["b_u_sb"][:, mc:mc + 1])),
                        wait=([("pe", pe_mm), ("ve", p.now("ve"))] if mc == 0 else [("se", p.now("se"))]),
                        inc=[("se", 1)])
                else:
                    p.op("scalar", (lambda e, mc=mc, bank=bank: e.activation(
                        c.t["ev_scr"][:, 0:512], bank[:, 0:512], AF.Sigmoid,
                        bias=c.t["b_g_sb"][:, mc:mc + 1])),
                        wait=([("pe", pe_mm), ("ve", p.now("ve"))] if mc == 0 else [("se", p.now("se")), ("ve", p.now("ve"))]),
                        inc=[("se", 1)])
                    p.op("vector", (lambda e, mc=mc: e.tensor_mul(
                        u_t[:, mc * 512:(mc + 1) * 512],
                        gel[:, mc * 512:(mc + 1) * 512], c.t["ev_scr"][:, 0:512])),
                        wait=[("se", p.now("se"))], inc=[("ve", 1)])
        ve_u = p.now("ve")
        for mh in range(2):
            pe_b = p.now("pe")
            bmv = c.t["BmT_d"].ap().rearrange("(kc p) m -> p kc m", p=128)[:, :, mh * 512:(mh + 1) * 512]
            p.op("gpsimd", (lambda e, s=bmv: e.dma_start(
                out=c.t["wu2_sb"].rearrange("p (kc m) -> p kc m", kc=NCH), in_=s)),
                wait=[("pe", pe_b)], inc=[("wl", 16)])
            wlb = p.now("wl")
            for mc2 in range(4):
                mc = mh * 4 + mc2
                bank = c.t[f"psO{mc}"]
                for kc in range(NCH):
                    p.op("tensor", (lambda e, mc2=mc2, kc=kc, bank=bank: e.matmul(
                        bank[:, 0:512],
                        c.t["wu2_sb"][:, kc * 512 + mc2 * 128: kc * 512 + mc2 * 128 + 128],
                        u_t[:, kc * 512:(kc + 1) * 512],
                        start=(kc == 0), stop=(kc == NCH - 1))),
                        wait=([("ve", ve_u), ("wl", wlb), ("se", p.now("se"))] if (mc2 == 0 and kc == 0) else []),
                        inc=([("pe", 1)] if kc == NCH - 1 else []))
            pe_mm = p.now("pe")
            for mc2 in range(4):
                mc = mh * 4 + mc2
                bank = c.t[f"psO{mc}"]
                p.op("vector", (lambda e, mc=mc, j=j, bank=bank: e.tensor_copy(
                    uB[:, mc * TC + j * 512: mc * TC + j * 512 + 512], bank[:, 0:512])),
                    wait=([("pe", pe_mm)] if mc2 == 0 else [("ve", p.now("ve"))]), inc=[("ve", 1)])

    p.op("gpsimd", lambda e: e.dma_start(
        out=wbuf.rearrange("p (kc m) -> p kc m", kc=NCH),
        in_=c.t["A_T_d"].ap().rearrange("(kc p) m -> p kc m", p=128)),
        wait=[("pe", p.now("pe"))], inc=[("wl", 16)])
    wl = p.now("wl")

    def scan_steps(nsteps, rhs_init, uB_at, out_compact, out_strided=None, lhs=None, NW=128):
        lhs = lhs if lhs is not None else wbuf
        cur = rhs_init
        for i in range(nsteps):
            for mc in range(NCH):
                bank = c.t[f"psO{mc}"]
                for kc in range(NCH):
                    p.op("tensor", (lambda e, mc=mc, kc=kc, cur=cur, lhs=lhs: e.matmul(
                        bank[:, 0:NW] if False else c.t[f"psO{mc}"][:, 0:NW],
                        lhs[:, kc * 1024 + mc * 128: kc * 1024 + mc * 128 + 128],
                        cur[:, kc * NW:(kc + 1) * NW],
                        start=(kc == 0), stop=(kc == NCH - 1))),
                        wait=([("wl", p.now("wl")), ("ve", p.now("ve"))]
                              if (mc == 0 and kc == 0) else []),
                        inc=([("pe", 1)] if kc == NCH - 1 else []))
            pe_mm = p.now("pe")
            nxt = sp_[i % 2]
            for mc in range(NCH):
                p.op("vector", (lambda e, mc=mc, i=i, nxt=nxt: e.tensor_add(
                    nxt[:, mc * NW:(mc + 1) * NW], c.t[f"psO{mc}"][:, 0:NW], uB_at(i, mc))),
                    wait=([("pe", pe_mm)] if mc == 0 else [("ve", p.now("ve"))]), inc=[("ve", 1)])
                if out_strided is not None:
                    p.op("vector", (lambda e, mc=mc, i=i, nxt=nxt: e.tensor_copy(
                        out_strided(i, mc), nxt[:, mc * NW:(mc + 1) * NW])),
                        wait=[("ve", p.now("ve"))], inc=[("ve", 1)])
            cur = nxt
        if out_compact is not None:
            for mc in range(NCH):
                p.op("vector", (lambda e, mc=mc, cur=cur: e.tensor_copy(
                    out_compact[:, mc * NW:(mc + 1) * NW], cur[:, mc * NW:(mc + 1) * NW])),
                    wait=[("ve", p.now("ve"))], inc=[("ve", 1)])
        return cur

    uBv = uB.rearrange("p (ch c i) -> p ch c i", ch=NCH, i=LCH)
    sallv = s_all.rearrange("p (ch c i) -> p ch c i", ch=NCH, i=LCH)

    p.op("vector", lambda e: e.memset(sp_[1][:, :], 0.0),
         wait=[("ve", p.now("ve")), ("pw", p.now("pw")), ("cc", p.now("cc")), ("cd", p.now("cd")), ("pe", p.now("pe"))], inc=[("ve", 1)])
    fin = scan_steps(LCH, sp_[1], (lambda i, ch: uBv[:, ch, :, i]), None)

    p.op("gpsimd", (lambda e, fin=fin: e.dma_start(
        out=c.t["cc_sum_in"].ap().rearrange("(ch p) r -> p ch r", p=128),
        in_=fin.rearrange("p (ch r) -> p ch r", ch=NCH))),
        wait=[("ve", p.now("ve"))], inc=[("cd", 16)])
    p.op("gpsimd", lambda e: e.collective_compute(
        "AllGather", ALU.bypass, replica_groups=[[0, 1], [2, 3], [4, 5], [6, 7]],
        ins=[c.t["cc_sum_in"].ap().opt()], outs=[c.t["cc_sum_out"].ap().opt()]),
        wait=[("cc", p.now("cc")), ("cd", p.now("cd"))], inc=[("cc", 1)])
    for b_i in range(2):
        p.op("gpsimd", (lambda e, b_i=b_i: e.dma_start(
            out=c.t["Rbuf"].rearrange("p (ch b r) -> p ch b r", ch=NCH, b=2)[:, :, b_i],
            in_=c.t["cc_sum_out"].ap().rearrange("(b ch p) r -> p ch r", b=2, p=128)[:, b_i * NCH:(b_i + 1) * NCH]
            if False else c.t["cc_sum_out"].ap().rearrange("(b chp) r -> b chp r", b=2)[b_i].rearrange("(ch p) r -> p ch r", p=128))),
            wait=[("cc", p.now("cc")), ("cd", p.now("cd"))], inc=[("cd", 16)])
    p.wait("vector", ("cc", p.now("cc")), ("cd", p.now("cd")))

    p.op("gpsimd", lambda e: e.dma_start(
        out=wbuf2.rearrange("p (kc m) -> p kc m", kc=NCH),
        in_=c.t["pw_p16"].ap().rearrange("(kc p) m -> p kc m", p=128)),
        wait=[("pe", p.now("pe"))], inc=[("wl", 16)])
    Rv = c.t["Rbuf"].rearrange("p (ch g i) -> p ch g i", ch=NCH, i=LCH)
    p.op("vector", lambda e: e.memset(sp_[1][:, 0:NCH * 16], 0.0),
         wait=[("ve", p.now("ve")), ("cc", p.now("cc")), ("cd", p.now("cd"))], inc=[("ve", 1)])
    scan_steps(16, sp_[1], (lambda i, ch: Rv[:, ch, :, i]), c.t["G_sb"], lhs=wbuf2, NW=16)
    p.op("gpsimd", lambda e: e.dma_start(
        out=wbuf.rearrange("p (kc m) -> p kc m", kc=NCH),
        in_=c.t["pw_p256"].ap().rearrange("(kc p) m -> p kc m", p=128)),
        wait=[("pe", p.now("pe"))], inc=[("wl", 16)])
    wl3 = p.now("wl")
    Sg = c.t["Sg_sb"]
    p.op("vector", lambda e: e.memset(Sg[:, :], 0.0), wait=[("ve", p.now("ve"))], inc=[("ve", 1)])
    for g in range(1, 16):
        for mc in range(NCH):
            for kc in range(NCH):
                p.op("tensor", (lambda e, mc=mc, kc=kc, g=g: e.matmul(
                    c.t[f"psO{mc}"][:, 0:1],
                    wbuf[:, kc * 1024 + mc * 128: kc * 1024 + mc * 128 + 128],
                    Sg[:, kc * 16 + g - 1: kc * 16 + g],
                    start=(kc == 0), stop=(kc == NCH - 1))),
                    wait=([("wl", wl3), ("ve", p.now("ve"))] if (mc == 0 and kc == 0) else []),
                    inc=([("pe", 1)] if kc == NCH - 1 else []))
        pe_mm = p.now("pe")
        for mc in range(NCH):
            p.op("vector", (lambda e, mc=mc, g=g: e.tensor_add(
                Sg[:, mc * 16 + g: mc * 16 + g + 1], c.t[f"psO{mc}"][:, 0:1],
                c.t["G_sb"][:, mc * 16 + g - 1: mc * 16 + g])),
                wait=([("pe", pe_mm)] if mc == 0 else [("ve", p.now("ve"))]), inc=[("ve", 1)])
    Sallv = c.t["Sall_sb"].rearrange("p (ch g i) -> p ch g i", ch=NCH, i=LCH)
    for mc in range(NCH):
        p.op("vector", (lambda e, mc=mc: e.tensor_copy(
            Sallv[:, mc, :, 0], Sg[:, mc * 16:(mc + 1) * 16])),
            wait=[("ve", p.now("ve"))], inc=[("ve", 1)])
    cur = Sg
    for i in range(1, 16):
        for mc in range(NCH):
            for kc in range(NCH):
                p.op("tensor", (lambda e, mc=mc, kc=kc, cur=cur: e.matmul(
                    c.t[f"psO{mc}"][:, 0:16],
                    wbuf2[:, kc * 1024 + mc * 128: kc * 1024 + mc * 128 + 128],
                    cur[:, kc * 16:(kc + 1) * 16],
                    start=(kc == 0), stop=(kc == NCH - 1))),
                    wait=([("ve", p.now("ve"))] if (mc == 0 and kc == 0) else []),
                    inc=([("pe", 1)] if kc == NCH - 1 else []))
        pe_mm = p.now("pe")
        nxt = sp_[i % 2]
        for mc in range(NCH):
            p.op("vector", (lambda e, mc=mc, i=i, nxt=nxt: e.tensor_add(
                nxt[:, mc * 16:(mc + 1) * 16], c.t[f"psO{mc}"][:, 0:16],
                Rv[:, mc, :, i - 1])),
                wait=([("pe", pe_mm)] if mc == 0 else [("ve", p.now("ve"))]), inc=[("ve", 1)])
            p.op("vector", (lambda e, mc=mc, i=i, nxt=nxt: e.tensor_copy(
                Sallv[:, mc, :, i], nxt[:, mc * 16:(mc + 1) * 16])),
                wait=[("ve", p.now("ve"))], inc=[("ve", 1)])
        cur = nxt

    p.op("gpsimd", lambda e: e.dma_start(
        out=wbuf.rearrange("p (kc m) -> p kc m", kc=NCH),
        in_=c.t["A_T_d"].ap().rearrange("(kc p) m -> p kc m", p=128)),
        wait=[("pe", p.now("pe"))], inc=[("wl", 16)])
    Sall_f = c.t["Sall_sb"]
    for mc in range(NCH):
        p.op("vector", (lambda e, mc=mc: e.tensor_scalar(
            sp_[1][:, mc * 128:(mc + 1) * 128],
            Sall_f[:, mc * 256: mc * 256 + 128], parsel[:, 2:3], None, ALU.mult)),
            wait=[("ve", p.now("ve"))], inc=[("ve", 1)])
        p.op("vector", (lambda e, mc=mc: e.scalar_tensor_tensor(
            sp_[1][:, mc * 128:(mc + 1) * 128],
            Sall_f[:, mc * 256 + 128: mc * 256 + 256], parsel[:, 3:4],
            sp_[1][:, mc * 128:(mc + 1) * 128], ALU.mult, ALU.add)),
            wait=[("ve", p.now("ve"))], inc=[("ve", 1)])
    scan_steps(LCH, sp_[1], (lambda i, ch: uBv[:, ch, :, i]), None,
               out_strided=(lambda i, ch: sallv[:, ch, :, i]))

    p.op("gpsimd", lambda e: e.dma_start(
        out=wbuf.rearrange("p (kc m) -> p kc m", kc=NCH),
        in_=c.t["CmT_d"].ap().rearrange("(kc p) m -> p kc m", p=128)),
        wait=[("pe", p.now("pe"))], inc=[("wl", 16)])
    wl = p.now("wl")
    for j in range(4):
        for mc in range(NCH):
            bank = c.t[f"psO{mc}"]
            for kc in range(NCH):
                p.op("tensor", (lambda e, mc=mc, kc=kc, j=j, bank=bank: e.matmul(
                    bank[:, 0:512],
                    wbuf[:, kc * 1024 + mc * 128: kc * 1024 + mc * 128 + 128],
                    s_all[:, kc * TC + j * 512: kc * TC + j * 512 + 512],
                    start=(kc == 0), stop=(kc == NCH - 1))),
                    wait=([("wl", wl), ("ve", p.now("ve")), ("se", p.now("se"))]
                          if (mc == 0 and kc == 0) else []),
                    inc=([("pe", 1)] if kc == NCH - 1 else []))
        pe_mm = p.now("pe")
        for mc in range(NCH):
            bank = c.t[f"psO{mc}"]
            p.op("vector", (lambda e, mc=mc, bank=bank: e.tensor_scalar(
                c.t["ev_scr"][:, 0:512], bank[:, 0:512], c.t["D_sb"][:, mc:mc + 1], None, ALU.add)),
                wait=([("pe", pe_mm), ("ve", p.now("ve"))] if mc == 0 else [("ve", p.now("ve"))]),
                inc=[("ve", 1)])
            p.op("vector", (lambda e, mc=mc, j=j: e.tensor_add(
                xf(mc, j), xf(mc, j), c.t["ev_scr"][:, 0:512])),
                wait=[("ve", p.now("ve"))], inc=[("ve", 1)])


def a0_alloc(c):
    c.sbuf("wu_sb", [128, NCH * 1024], BF16)
    c.sbuf("wg_sb", [128, NCH * 1024], BF16)
    c.sbuf("wu2_sb", [128, NCH * 512], BF16)
    c.sbuf("u_tile", [128, NCH * 512], BF16)
    c.sbuf("uB_all", [128, NCH * TC], BF16)
    c.sbuf("gel_scr", [128, NCH * 512], BF16)
    c.t["s_ping"] = None
    c.t["s_pong"] = None
    c.t["Rbuf"] = None
    c.sbuf("G_sb", [128, NCH * 16], BF16)
    c.sbuf("Sg_sb", [128, NCH * 16], BF16)
    c.t["pw_a"] = c.t["wu_sb"]
    c.t["pw_b"] = c.t["wg_sb"]
    c.t["Rbuf"] = c.t["gel_scr"][:, 0:NCH * 256]
    c.t["Sall_sb"] = c.t["gel_scr"][:, NCH * 256:NCH * 512]
    c.t["s_ping"] = c.t["u_tile"][:, 0:NCH * 128]
    c.t["s_pong"] = c.t["u_tile"][:, NCH * 128:NCH * 256]
    c.t["myp_sb"] = c.t["s_ping"]
    c.t["myq_sb"] = c.t["s_pong"]
    c.dram("cc_pw_in", [D, 128], BF16)
    c.dram("cc_pw_out", [NCORES * D, 128], BF16)
    c.dram("cc_pw_out2", [NCORES * D, 128], BF16)
    c.dram("pw_p16", [D, D], BF16)
    c.dram("pw_p256", [D, D], BF16)
    c.dram("cc_sum_in", [D, 128], BF16)
    c.dram("cc_sum_out", [2 * D, 128], BF16)


def build_a0_standalone(debug=False):
    nc = bass.Bass("TRN2", target_bir_lowering=False)
    c = Ctx(nc)
    x_in = nc.dram_tensor("x_in", [D, TC], F32, kind="ExternalInput")
    for nm, shp in (("W_uT", [D, D]), ("W_gT", [D, D]), ("BmT", [D, D]), ("CmT", [D, D]),
                    ("A_T", [D, D]), ("A_nat", [D, D])):
        c.t[nm + "_d"] = nc.dram_tensor(nm, shp, F32, kind="ExternalInput")
    bu_in = nc.dram_tensor("b_u", [128, NCH], F32, kind="ExternalInput")
    bg_in = nc.dram_tensor("b_g", [128, NCH], F32, kind="ExternalInput")
    dv_in = nc.dram_tensor("Dv", [128, NCH], F32, kind="ExternalInput")
    amyp = nc.dram_tensor("Amy_p", [D, 128], F32, kind="ExternalInput")
    amyq = nc.dram_tensor("Amy_q", [D, 128], F32, kind="ExternalInput")
    parsel_in = nc.dram_tensor("parsel", [128, 8], F32, kind="ExternalInput")
    x_out = nc.dram_tensor("x_out", [D, TC], F32, kind="ExternalOutput")
    if debug:
        dbg_uB = nc.dram_tensor("dbg_uB", [D, TC], BF16, kind="ExternalOutput")
        dbg_sall = nc.dram_tensor("dbg_sall", [D, TC], BF16, kind="ExternalOutput")
        dbg_p16 = nc.dram_tensor("dbg_p16", [D, D], BF16, kind="ExternalOutput")
        dbg_sinit = nc.dram_tensor("dbg_sinit", [D, 256], BF16, kind="ExternalOutput")
    c.mk_consts()
    c.sbuf("x_sb", [128, NCH * TC], F32)
    c.sbuf("xq", [128, NCH * TC], BF16)
    c.sbuf("stat", [1, 2048], F32)
    c.sbuf("row_bf", [1, 1024], BF16)
    c.sbuf("sq_bf", [128, 512], BF16)
    c.sbuf("xbf_scr", [128, 512], BF16)
    c.sbuf("ln_scr", [128, 512], F32)
    c.sbuf("red_scr", [128, 1], F32)
    c.sbuf("srow", [1, 128], F32)
    for i in range(NCH):
        c.psum(f"psO{i}", [128, 512], F32)
    c.t["psA"] = c.t["psO0"]
    c.t["psB"] = c.t["psO1"]
    a0_alloc(c)
    p = c.p
    load_consts(c, ident=False)
    xv = x_in.ap().rearrange("(c p) t -> p c t", p=128)
    p.op("sync", lambda e: e.dma_start(
        out=c.t["x_sb"].rearrange("p (c t) -> p c t", c=NCH), in_=xv), inc=[("ld", 16)])
    p.op("sync", lambda e: e.dma_start(out=c.t["parsel_sb"][:, :], in_=parsel_in.ap()[:, :]),
         inc=[("ld", 16)])
    for nm, t_ in (("b_u_sb", bu_in), ("b_g_sb", bg_in), ("D_sb", dv_in)):
        p.op("sync", (lambda e, nm=nm, t_=t_: e.dma_start(
            out=c.t[nm][:, :], in_=t_.ap()[:, :])),
            inc=[("ld", 16)])
    p.op("gpsimd", lambda e: e.dma_start(
        out=c.t["myp_sb"].rearrange("p (kc m) -> p kc m", kc=NCH),
        in_=amyp.ap().rearrange("(kc p) m -> p kc m", p=128)), inc=[("pw", 16)])
    p.op("gpsimd", lambda e: e.dma_start(
        out=c.t["myq_sb"].rearrange("p (kc m) -> p kc m", kc=NCH),
        in_=amyq.ap().rearrange("(kc p) m -> p kc m", p=128)), inc=[("pw", 16)])
    for eng in ("vector", "tensor", "scalar", "gpsimd"):
        p.wait(eng, ("ld", p.now("ld")), ("pw", p.now("pw")), ("cc", p.now("cc")), ("cd", p.now("cd")))
    emit_a0_powers(c)
    emit_a0(c, 0)
    p.op("sync", lambda e: e.dma_start(
        out=x_out.ap().rearrange("(c p) t -> p c t", p=128),
        in_=c.t["x_sb"].rearrange("p (c t) -> p c t", c=NCH)),
        wait=[("ve", p.now("ve")), ("se", p.now("se"))], inc=[("st", 16)])
    if debug:
        p.op("sync", lambda e: e.dma_start(
            out=dbg_uB.ap().rearrange("(c p) t -> p c t", p=128),
            in_=c.t["uB_all"].rearrange("p (c t) -> p c t", c=NCH)), inc=[("st", 16)])
        p.op("sync", lambda e: e.dma_start(
            out=dbg_sall.ap().rearrange("(c p) t -> p c t", p=128),
            in_=c.t["xq"].rearrange("p (c t) -> p c t", c=NCH)), inc=[("st", 16)])
        p.op("gpsimd", lambda e: e.dma_start(
            out=dbg_p16.ap()[:, :], in_=c.t["pw_p16"].ap()[:, :]), inc=[("cd", 16)])
        p.op("sync", lambda e: e.dma_start(
            out=dbg_sinit.ap().rearrange("(c p) t -> p c t", p=128),
            in_=c.t["Sall_sb"].rearrange("p (c t) -> p c t", c=NCH)), inc=[("st", 16)])
        p.wait("sync", ("cd", p.now("cd")))
    p.wait("sync", ("st", p.now("st")))
    p.finalize()
    return nc


def run_a0(d, debug=False):
    nc = build_a0_standalone(debug)
    in_maps = []
    for core in range(NCORES):
        in_maps.append({
            "x_in": x_slice(d, core),
            "W_uT": d["W_uT"], "W_gT": d["W_gT"], "BmT": d["BmT"], "CmT": d["CmT"],
            "A_T": d["A_T"], "A_nat": d["A_nat"],
            "b_u": d["b_u"], "b_g": d["b_g"], "Dv": d["Dv"],
            "Amy_p": np.ascontiguousarray(d["A_T"][:, core * 128:(core + 1) * 128]),
            "Amy_q": np.ascontiguousarray(d["A_nat"][:, core * 128:(core + 1) * 128]),
            "parsel": host_parsel(core),
        })
    res = run_bass_kernel_spmd(nc, in_maps, core_ids=list(range(NCORES)))
    return np.concatenate([res.results[cc]["x_out"] for cc in range(NCORES)], axis=1)






def fence(c):
    p = c.p
    sems = ("pe", "ve", "se", "wl", "wl2", "cc", "cd", "sp", "pw", "ld", "st",
            "fx0", "fx1", "fx2", "fx3", "fwa", "fwb", "spa", "spb", "foa", "fob")
    for eng in ENGS:
        p.wait(eng, *[(s, p.now(s)) for s in sems])


def build_fused(nstages=6, powers=True):
    nc = bass.Bass("TRN2", target_bir_lowering=False)
    c = Ctx(nc)
    x_in = nc.dram_tensor("x_in", [D, TC], F32, kind="ExternalInput")
    c.t["w_upT_d"] = nc.dram_tensor("w_upT", [D, IFF], F32, kind="ExternalInput")
    c.t["w_gateT_d"] = nc.dram_tensor("w_gateT", [D, IFF], F32, kind="ExternalInput")
    c.t["w_downT_d"] = nc.dram_tensor("w_downT", [IFF, D], F32, kind="ExternalInput")
    c.t["gate_wT_d"] = nc.dram_tensor("gate_wT", [D, 9], F32, kind="ExternalInput")
    c.t["dyn_gT_d"] = nc.dram_tensor("dyn_gT", [NEXP, D, IDYN], F32, kind="ExternalInput")
    c.t["dyn_uT_d"] = nc.dram_tensor("dyn_uT", [NEXP, D, IDYN], F32, kind="ExternalInput")
    c.t["dyn_dT_d"] = nc.dram_tensor("dyn_dT", [NEXP, IDYN, D], F32, kind="ExternalInput")
    c.t["sh_gT_d"] = nc.dram_tensor("sh_gT", [D, ISH], F32, kind="ExternalInput")
    c.t["sh_uT_d"] = nc.dram_tensor("sh_uT", [D, ISH], F32, kind="ExternalInput")
    c.t["sh_dT_d"] = nc.dram_tensor("sh_dT", [ISH, D], F32, kind="ExternalInput")
    c.t["wqvk_d"] = nc.dram_tensor("wqvk", [D, 3072], F32, kind="ExternalInput")
    c.t["wo_d"] = nc.dram_tensor("wo", [D, D], F32, kind="ExternalInput")
    for nm, shp in (("W_uT", [D, D]), ("W_gT", [D, D]), ("BmT", [D, D]), ("CmT", [D, D]),
                    ("A_T", [D, D]), ("A_nat", [D, D])):
        c.t[nm + "_d"] = nc.dram_tensor(nm, shp, F32, kind="ExternalInput")
    bu_in = nc.dram_tensor("b_u", [128, NCH], F32, kind="ExternalInput")
    bg_in = nc.dram_tensor("b_g", [128, NCH], F32, kind="ExternalInput")
    dv_in = nc.dram_tensor("Dv", [128, NCH], F32, kind="ExternalInput")
    amyp = nc.dram_tensor("Amy_p", [D, 128], F32, kind="ExternalInput")
    amyq = nc.dram_tensor("Amy_q", [D, 128], F32, kind="ExternalInput")
    parsel_in = nc.dram_tensor("parsel", [128, 8], F32, kind="ExternalInput")
    prevrow_in = nc.dram_tensor("prevrow", [1, 8], F32, kind="ExternalInput")
    x_out = nc.dram_tensor("x_out", [D, TC], F32, kind="ExternalOutput")

    c.mk_consts()
    c.sbuf("x_sb", [128, NCH * TC], F32)
    c.sbuf("xq", [128, NCH * TC], BF16)
    c.sbuf("stat", [1, 2048], F32)
    c.sbuf("row_bf", [1, 1024], BF16)
    c.sbuf("sq_bf", [128, 512], BF16)
    c.sbuf("xbf_scr", [128, 512], BF16)
    c.sbuf("ln_scr", [128, 512], F32)
    c.sbuf("red_scr", [128, 1], F32)
    c.sbuf("srow", [1, 128], F32)
    c.sbuf("grow", [1, 64], F32)
    c.sbuf("s128", [128, 8], F32)
    c.sbuf("am128", [128, 1], F32)
    c.sbuf("am2", [128, 1], F32)
    c.sbuf("q_scr", [128, 512], F32)
    c.sbuf("ev_scr", [128, 512], F32)
    c.sbuf("b_u_sb", [128, NCH], F32)
    c.sbuf("b_g_sb", [128, NCH], F32)
    c.sbuf("D_sb", [128, NCH], F32)
    c.sbuf("parsel_sb", [128, 8], F32)
    c.sbuf("prevrow_sb", [1, 8], F32)
    idn = c.sbuf("ident_sb", [128, 128], F32)
    oc = c.sbuf("ones_col_sb", [128, 1], BF16)
    orow = c.sbuf("ones_row_sb", [1, 128], BF16)
    a1_psum_alloc(c)

    rem = nc.sbuf_bytes_remaining
    arena_cols = (rem - 512) // 4
    arena = nc.alloc_sbuf_tensor("arena", [128, arena_cols], F32)
    c.arena = (arena[:, :], 0)
    ns = {}
    for name, fn in (("f0", f0_alloc), ("f1", f1_alloc), ("a0", a0_alloc), ("a1", a1_alloc)):
        c.arena_reset()
        c.begin_ns()
        fn(c)
        ns[name] = c.end_ns()
    c.arena = None

    p = c.p
    p.op("sync", lambda e: e.dma_start(out=oc[:, :], in_=c.const["ones_col"].ap()[:, :]), inc=[("ld", 16)])
    p.op("sync", lambda e: e.dma_start(out=orow[:, :], in_=c.const["ones_row"].ap()[:, :]), inc=[("ld", 16)])
    p.op("sync", lambda e: e.dma_start(out=idn[:, :], in_=c.const["ident"].ap()[:, :]), inc=[("ld", 16)])
    xv = x_in.ap().rearrange("(ch p) t -> p ch t", p=128)
    p.op("sync", lambda e: e.dma_start(
        out=c.t["x_sb"].rearrange("p (ch t) -> p ch t", ch=NCH), in_=xv), inc=[("ld", 16)])
    p.op("sync", lambda e: e.dma_start(out=c.t["parsel_sb"][:, :], in_=parsel_in.ap()[:, :]), inc=[("ld", 16)])
    p.op("sync", lambda e: e.dma_start(out=c.t["prevrow_sb"][0:1, :], in_=prevrow_in.ap()[:, :]), inc=[("ld", 16)])
    for nm, t_ in (("b_u_sb", bu_in), ("b_g_sb", bg_in), ("D_sb", dv_in)):
        p.op("sync", (lambda e, nm=nm, t_=t_: e.dma_start(out=c.t[nm][:, :], in_=t_.ap()[:, :])), inc=[("ld", 16)])
    c.t.update(ns["a0"])
    p.op("gpsimd", lambda e: e.dma_start(
        out=c.t["myp_sb"].rearrange("p (kc m) -> p kc m", kc=NCH),
        in_=amyp.ap().rearrange("(kc p) m -> p kc m", p=128)), inc=[("pw", 16)])
    p.op("gpsimd", lambda e: e.dma_start(
        out=c.t["myq_sb"].rearrange("p (kc m) -> p kc m", kc=NCH),
        in_=amyq.ap().rearrange("(kc p) m -> p kc m", p=128)), inc=[("pw", 16)])
    for eng in ENGS:
        p.wait(eng, ("ld", p.now("ld")), ("pw", p.now("pw")))

    if powers:
        emit_a0_powers(c)
        fence(c)
    c.t.update(ns["f0"])
    emit_f0_preconv(c)
    def st_f0a():
        c.t.update(ns["f0"])
        p.op("vector", lambda e: e.memset(c.t["ones8"][0:8, :], 1.0), wait=[("ve", p.now("ve"))], inc=[("ve", 1)])
        emit_f0(c, 0)
    def st_a0():
        c.t.update(ns["f1"])
        emit_f1_preconv(c)
        c.t.update(ns["a0"])
        emit_a0(c, 0)
    def st_f0b():
        c.t.update(ns["f0"])
        p.op("vector", lambda e: e.memset(c.t["ones8"][0:8, :], 1.0), wait=[("ve", p.now("ve"))], inc=[("ve", 1)])
        emit_f0(c, 1)
    def st_f1a():
        c.t.update(ns["f1"])
        emit_f1(c, 0)
    def st_a1():
        c.t.update(ns["a1"])
        emit_a1(c, 0)
    def st_f1b():
        c.t.update(ns["f1"])
        emit_f1(c, 1)
    stages = [st_f0a, st_a0, st_f0b, st_f1a, st_a1, st_f1b]
    for i in range(nstages):
        stages[i]()
        if i < nstages - 1:
            fence(c)
    p.op("sync", lambda e: e.dma_start(
        out=x_out.ap().rearrange("(ch p) t -> p ch t", p=128),
        in_=c.t["x_sb"].rearrange("p (ch t) -> p ch t", ch=NCH)),
        wait=[("ve", p.now("ve")), ("se", p.now("se"))], inc=[("st", 16)])
    p.wait("sync", ("st", p.now("st")))
    p.finalize()
    return nc


def kernel(**inputs):
    d = host_prepare(inputs)
    nc = build_fused()
    in_maps = []
    for core in range(NCORES):
        prev = np.zeros((1, 8), np.float32)
        prev[0, :core] = 1.0
        in_maps.append({
            "x_in": np.ascontiguousarray(d["x_t"][:, core * TC:(core + 1) * TC]),
            "w_upT": d["w_upT"], "w_gateT": d["w_gateT"], "w_downT": d["w_downT"],
            "gate_wT": d["gate_wT"],
            "dyn_gT": d["dyn_gT"], "dyn_uT": d["dyn_uT"], "dyn_dT": d["dyn_dT"],
            "sh_gT": d["sh_gT"], "sh_uT": d["sh_uT"], "sh_dT": d["sh_dT"],
            "wqvk": host_wqvk_perm(d["Wqvk"], core), "wo": d["Wo"],
            "W_uT": d["W_uT"], "W_gT": d["W_gT"], "BmT": d["BmT"], "CmT": d["CmT"],
            "A_T": d["A_T"], "A_nat": d["A_nat"],
            "b_u": d["b_u"], "b_g": d["b_g"], "Dv": d["Dv"],
            "Amy_p": np.ascontiguousarray(d["A_T"][:, core * 128:(core + 1) * 128]),
            "Amy_q": np.ascontiguousarray(d["A_nat"][:, core * 128:(core + 1) * 128]),
            "parsel": host_parsel(core),
            "prevrow": prev,
        })
    res = run_bass_kernel_spmd(nc, in_maps, core_ids=list(range(NCORES)))
    x_t = np.concatenate([res.results[cc]["x_out"] for cc in range(NCORES)], axis=1)
    return np.ascontiguousarray(x_t.T).reshape(4, SEQ, D)
